# version 17
# baseline (speedup 1.0000x reference)
import math
from contextlib import ExitStack

import numpy as np
import ml_dtypes

import concourse.bass as bass
import concourse.mybir as mybir
from concourse.bass_utils import run_bass_kernel_spmd

F32 = mybir.dt.float32
BF16 = mybir.dt.bfloat16
AF = mybir.ActivationFunctionType
ALU = mybir.AluOpType
AX = mybir.AxisListType.X
AP = bass.AP

S = 4096
DM = 1024
NH = 16
HD = 64
NT = S // 128
NC = DM // 128

ENGS = ("sync", "tensor", "vector", "scalar", "gpsimd")


def row_ap(dram_ap, n):
    return AP(dram_ap.tensor, dram_ap.offset, [[n, 1], [1, n]])


class Sym:
    __slots__ = ("val",)
    def __init__(self):
        self.val = None


def _resolve(v):
    if v is None:
        return 0
    if isinstance(v, Sym):
        return v.val
    if isinstance(v, (list, tuple)):
        return max((_resolve(x) for x in v), default=0)
    return v


class Pr:
    def __init__(self, nc, es):
        self.nc = nc
        self.es = es
        self.q = {e: [] for e in ENGS}
        self.sems = {}
        self.last = {}
        self._rec = None

    def sem(self, name):
        if name not in self.sems:
            self.sems[name] = self.es.enter_context(self.nc.semaphore(name))
        return self.sems[name]

    def _push(self, eng, entry):
        (self._rec if self._rec is not None else self.q)[eng].append(entry)

    def wait(self, eng, name, val):
        if val is None or val == 0:
            return
        self.sem(name)
        self._push(eng, ("w", name, val))

    def step(self, eng, fn):
        name = "e_" + eng
        self.sem(name)
        s = Sym()
        self.last[name] = s
        self._push(eng, ("s", fn, name, 1, s))
        return s

    def do(self, eng, fn):
        self._push(eng, ("p", fn))

    def dma(self, eng, fn, ctr=None):
        name = ctr or ("d_" + eng)
        self.sem(name)
        s = Sym()
        self.last[name] = s
        self._push(eng, ("s", fn, name, 16, s))
        return s

    def marks(self):
        return dict(self.last)

    def barrier(self):
        vals = dict(self.last)
        for e in ENGS:
            for n, s in vals.items():
                if n != "e_" + e:
                    self.wait(e, n, s)

    def record(self):
        assert self._rec is None
        self._rec = {e: [] for e in ENGS}

    def stop(self):
        r = self._rec
        self._rec = None
        return r

    def emit_zip(self, *recs):
        for e in ENGS:
            ls = [r[e] for r in recs]
            idx = [0] * len(ls)
            while any(idx[i] < len(ls[i]) for i in range(len(ls))):
                for i in range(len(ls)):
                    if idx[i] < len(ls[i]):
                        self.q[e].append(ls[i][idx[i]])
                        idx[i] += 1

    def bake(self):
        cnt = {}
        for e in ENGS:
            for ent in self.q[e]:
                if ent[0] == "s":
                    _, fn, name, inc, s = ent
                    cnt[name] = cnt.get(name, 0) + inc
                    s.val = cnt[name]
        self.cnt = cnt

    def run_body(self, eng_name):
        sems = self.sems
        def body(E):
            for ent in self.q[eng_name]:
                k = ent[0]
                if k == "w":
                    E.wait_ge(sems[ent[1]], _resolve(ent[2]))
                elif k == "s":
                    _, fn, name, inc, s = ent
                    fn(E).then_inc(sems[name], inc)
                else:
                    ent[1](E)
        return body

    def finish(self, nc):
        self.bake()
        with nc.Block() as block:
            block.sync(self.run_body("sync"))
            block.tensor(self.run_body("tensor"))
            block.vector(self.run_body("vector"))
            block.scalar(self.run_body("scalar"))
            block.gpsimd(self.run_body("gpsimd"))


def ln_stats(P, src, ncols, stat, eps_col, sq_tmp):
    nch = (ncols + 511) // 512
    v0 = None
    for i in range(nch):
        w = min(512, ncols - i * 512)
        v0 = P.step("vector", lambda E, s=src, q=sq_tmp, i=i, w=w: E.bn_stats(
            q[:, i * 6:(i + 1) * 6], s[:, i * 512:i * 512 + w]))
    P.wait("vector", "e_vector", v0)
    v1 = P.step("vector", lambda E, q=sq_tmp, d=stat, nch=nch: E.bn_aggr(
        d[:, 2:4], q[:, 0:6 * nch]))
    P.wait("scalar", "e_vector", v1)
    s5 = P.step("scalar", lambda E, d=stat, eps=eps_col: E.activation(
        d[:, 0:1], d[:, 3:4], AF.Ln, bias=eps))
    P.wait("scalar", "e_scalar", s5)
    s6 = P.step("scalar", lambda E, d=stat: E.activation(
        d[:, 3:4], d[:, 0:1], AF.Exp, scale=-0.5))
    return s6


def ln_apply(P, src, dst, stat, ready_sc):
    P.wait("vector", "e_scalar", ready_sc)
    return P.step("vector", lambda E, s=src, d=dst, t=stat: E.tensor_scalar(
        d, s, t[:, 2:3], t[:, 3:4], ALU.subtract, ALU.mult))


def bcast_row(P, C, src_row, width, dst_sb, psum, after_v=None):
    last_v = after_v
    for i in range((width + 511) // 512):
        w = min(512, width - i * 512)
        P.wait("tensor", "e_vector", last_v)
        t = P.step("tensor", lambda E, C=C, s=src_row, p=psum, i=i, w=w:
                   E.matmul(p[:, 0:w], C.ones_row[:], s[:, i * 512:i * 512 + w],
                            start=True, stop=True, skip_group_check=True))
        P.wait("vector", "e_tensor", t)
        last_v = P.step("vector", lambda E, p=psum, d=dst_sb, i=i, w=w:
                        E.tensor_copy(d[:, i * 512:i * 512 + w], p[:, 0:w]))
    return last_v


def host_consts():
    c = {}
    c["ident"] = np.eye(128, dtype=np.float32).astype(ml_dtypes.bfloat16)
    c["antid"] = np.eye(128, dtype=np.float32)[::-1].copy().astype(ml_dtypes.bfloat16)
    c["ones_row_f32"] = np.ones((1, 128), np.float32)
    inv_freq = (1.0 / 10000.0 ** (np.arange(0, HD, 2, dtype=np.float32) / HD)).astype(np.float32)
    pos = np.arange(S, dtype=np.float32)
    fr = pos[:, None] * inv_freq[None, :]
    c["cosrep"] = np.tile(np.cos(fr).astype(np.float32), (1, NH))
    c["sinrep"] = np.tile(np.sin(fr).astype(np.float32), (1, NH))
    return c


class Ctx:
    pass


def setup_common(P, need_rope, need_antid=False):
    nc, es = P.nc, P.es
    C = Ctx()
    C.ident_h = nc.dram_tensor("c_ident", [128, 128], BF16, kind="ExternalInput").ap()
    if need_antid:
        C.antid_h = nc.dram_tensor("c_antid", [128, 128], BF16, kind="ExternalInput").ap()
    C.ones_h = nc.dram_tensor("c_ones", [1, 128], F32, kind="ExternalInput").ap()
    C.ident = es.enter_context(nc.sbuf_tensor("ident", [128, 128], BF16))
    C.ones_row = es.enter_context(nc.sbuf_tensor("ones_row", [1, 128], F32))
    C.ones_col = es.enter_context(nc.sbuf_tensor("ones_col", [128, 1], BF16))
    P.dma("sync", lambda E, C=C: E.dma_start(out=C.ident[:], in_=C.ident_h), ctr="d_pre")
    P.dma("sync", lambda E, C=C: E.dma_start(out=C.ones_row[:], in_=C.ones_h), ctr="d_pre")
    C.v_setup = P.step("vector", lambda E, C=C: E.memset(C.ones_col[:], 1.0))
    C.consts = {}
    def const_col(value, parts=128):
        key = (value, parts)
        if key not in C.consts:
            t = es.enter_context(nc.sbuf_tensor(f"cst{len(C.consts)}", [parts, 1], F32))
            C.consts[key] = (t, P.step("vector", lambda E, t=t, v=value: E.memset(t[:], v)))
        return C.consts[key][0].ap()
    C.const_col = const_col
    C.eps5 = const_col(1e-5)
    C.eps3 = const_col(1e-3)
    import math as _m
    C.lnsc8 = const_col(float(_m.log(1.0 / 8.0)), 1)
    C.lnsc32 = const_col(float(_m.log(1.0 / 32.0)), 1)
    if need_rope:
        C.cos_h = nc.dram_tensor("c_cosrep", [S, 512], F32, kind="ExternalInput").ap()
        C.sin_h = nc.dram_tensor("c_sinrep", [S, 512], F32, kind="ExternalInput").ap()
    return C


PHASE_LIMIT = 99
P3_PAIRS = 8
P3_HH = 2
P3_QQ = 4
P3_KC = NT
P4_NT = NT
P4_WO = True
P4_BC = True

def emit_l0_attn(P, C, xin, xout, W):
    nc = P.nc
    scale = 1.0 / math.sqrt(HD)

    QT_h = nc.dram_tensor("l0_qt", [DM, S], BF16).ap()
    KT_h = nc.dram_tensor("l0_kt", [DM, S], BF16).ap()
    V_h = nc.dram_tensor("l0_v", [8, S, 130], BF16).ap()

    blk = ExitStack()
    attno = blk.enter_context(nc.sbuf_tensor("attno", [128, NT * DM], BF16))
    nmQ = blk.enter_context(nc.sbuf_tensor("nmq", [128, NH], F32))
    nmK = blk.enter_context(nc.sbuf_tensor("nmk", [128, NH], F32))
    negc = blk.enter_context(nc.sbuf_tensor("negc", [128, 1], F32))

    vz = P.step("vector", lambda E, a=nmQ: E.memset(a[:], 0.0))
    vz2 = P.step("vector", lambda E, b=nmK: E.memset(b[:], 0.0))

    ph = ExitStack()
    Wqkv = ph.enter_context(nc.sbuf_tensor("wqkv", [128, NC * 3072], BF16))
    bqkv = ph.enter_context(nc.sbuf_tensor("bqkv", [1, 3072], F32))
    xt = [ph.enter_context(nc.sbuf_tensor(f"xt{i}", [128, DM], F32)) for i in range(2)]
    lnb = ph.enter_context(nc.sbuf_tensor("lnb", [128, DM], BF16))
    lnTt = ph.enter_context(nc.sbuf_tensor("lnTt", [128, DM], BF16))
    sq_tmp = ph.enter_context(nc.sbuf_tensor("sq_tmp", [128, 24], F32))
    stat = ph.enter_context(nc.sbuf_tensor("stat", [128, 4], F32))
    cost = [ph.enter_context(nc.sbuf_tensor(f"cost{i}", [128, 512], F32)) for i in range(2)]
    sint = [ph.enter_context(nc.sbuf_tensor(f"sint{i}", [128, 512], F32)) for i in range(2)]
    rtmp = [ph.enter_context(nc.sbuf_tensor(f"rtmp{i}", [128, 512], F32)) for i in range(2)]
    Qro = ph.enter_context(nc.sbuf_tensor("qro", [128, DM], BF16))
    Kro = ph.enter_context(nc.sbuf_tensor("kro", [128, DM], BF16))
    Vt = ph.enter_context(nc.sbuf_tensor("vt", [128, 8 * 130], BF16))
    QTt = ph.enter_context(nc.sbuf_tensor("qtt", [128, DM], BF16))
    KTt = ph.enter_context(nc.sbuf_tensor("ktt", [128, DM], BF16))
    sqf = ph.enter_context(nc.sbuf_tensor("sqf", [128, DM], F32))
    nupd = ph.enter_context(nc.sbuf_tensor("nupd", [128, NH], F32))
    psT = ph.enter_context(nc.psum_tensor("psT", [128, 1024], BF16))
    psA = ph.enter_context(nc.psum_tensor("psA", [128, 1024], F32))
    psB = ph.enter_context(nc.psum_tensor("psB", [128, 2048], F32))

    wq_src = AP(W["Wqkv"].tensor, 0, [[3072, 128], [128 * 3072, NC], [1, 3072]])
    wq_dst = AP(Wqkv.ap().tensor, 0, [[NC * 3072, 128], [3072, NC], [1, 3072]])
    gw = P.dma("gpsimd", lambda E, s=wq_src, d=wq_dst: E.dma_start(out=d, in_=s), ctr="d_wqkv")
    lb = P.dma("sync", lambda E, d=bqkv, s=row_ap(W["bqkv"], 3072):
               E.dma_start(out=d[:], in_=s), ctr="d_pre")

    vone = P.step("vector", lambda E, d=AP(Vt.ap().tensor, 64, [[8 * 130, 128], [130, 8], [65, 2]]):
                  E.memset(d, 1.0))
    P.wait("tensor", "d_wqkv", gw)
    P.wait("tensor", "d_pre", lb)
    endm = [P.marks(), P.marks()]
    prevm = P.marks()

    for t in range(NT):
        pm = endm[t % 2]
        P.wait("sync", "e_vector", pm.get("e_vector", 0))
        P.wait("sync", "e_scalar", pm.get("e_scalar", 0))
        xb, cb, sb = xt[t % 2], cost[t % 2], sint[t % 2]
        dxc = f"d_x{t % 2}"
        vx = P.dma("sync", lambda E, d=xb, s=xin, t=t:
                   E.dma_start(out=d[:], in_=s[t * 128:(t + 1) * 128, :]), ctr=dxc)
        P.dma("sync", lambda E, d=cb, s=C.cos_h, t=t:
              E.dma_start(out=d[:], in_=s[t * 128:(t + 1) * 128, :]), ctr=dxc)
        vs = P.dma("sync", lambda E, d=sb, s=C.sin_h, t=t:
                   E.dma_start(out=d[:], in_=s[t * 128:(t + 1) * 128, :]), ctr=dxc)
        P.wait("vector", dxc, vx)
        P.wait("scalar", dxc, vx)
        P.wait("vector", "e_tensor", prevm.get("e_tensor", 0))
        P.wait("scalar", "e_vector", prevm.get("e_vector", 0))
        rs = ln_stats(P, xb.ap(), DM, stat.ap(), C.eps5, sq_tmp.ap())
        vln = ln_apply(P, xb.ap(), lnb.ap(), stat.ap(), rs)
        P.wait("tensor", "e_vector", vln)
        for cch in range(NC):
            tt = P.step("tensor", lambda E, C=C, c=cch, s=lnb, p=psT:
                        E.transpose(p[:, c * 128:(c + 1) * 128],
                                    s[:, c * 128:(c + 1) * 128], C.ident[:]))
        P.wait("vector", "e_tensor", tt)
        vT = P.step("vector", lambda E, p=psT, d=lnTt: E.tensor_copy(d[:], p[:]))
        P.wait("tensor", "e_vector", vT)
        for cch in range(NC):
            for j in range(2):
                P.do("tensor", lambda E, c=cch, j=j, l=lnTt, w=Wqkv, p=psA:
                     E.matmul(p[:, j * 512:(j + 1) * 512], l[:, c * 128:(c + 1) * 128],
                              w[:, c * 3072 + j * 512:c * 3072 + (j + 1) * 512],
                              start=(c == 0), stop=False, skip_group_check=True))
            for j in range(4):
                P.do("tensor", lambda E, c=cch, j=j, l=lnTt, w=Wqkv, p=psB:
                     E.matmul(p[:, j * 512:(j + 1) * 512], l[:, c * 128:(c + 1) * 128],
                              w[:, c * 3072 + 1024 + j * 512:c * 3072 + 1024 + (j + 1) * 512],
                              start=(c == 0), stop=False, skip_group_check=True))
        for j in range(2):
            P.do("tensor", lambda E, C=C, j=j, b=bqkv, p=psA:
                 E.matmul(p[:, j * 512:(j + 1) * 512], C.ones_row[:],
                          b[:, j * 512:(j + 1) * 512], start=False, stop=True,
                          skip_group_check=True))
        for j in range(4):
            tq = P.step("tensor", lambda E, C=C, j=j, b=bqkv, p=psB:
                        E.matmul(p[:, j * 512:(j + 1) * 512], C.ones_row[:],
                                 b[:, 1024 + j * 512:1024 + (j + 1) * 512],
                                 start=False, stop=True, skip_group_check=True))
        P.wait("vector", dxc, vs)
        P.wait("vector", f"d_sp{t % 2}", prevm.get(f"d_sp{t % 2}", 0))
        P.wait("vector", "e_tensor", tq)
        vr = None
        for (src, dst) in ((psA, Qro), (psB, Kro)):
            w = src.ap().shape[1]
            e0 = AP(src.ap().tensor, 0, [[w, 128], [2, 512]])
            e1 = AP(src.ap().tensor, 1, [[w, 128], [2, 512]])
            d0 = AP(dst.ap().tensor, 0, [[DM, 128], [2, 512]])
            d1 = AP(dst.ap().tensor, 1, [[DM, 128], [2, 512]])
            P.do("vector", lambda E, e=e0, c=cb, r=rtmp[0]: E.tensor_mul(r[:], e, c[:]))
            va = P.step("vector", lambda E, e=e1, s2=sb, r=rtmp[1]: E.tensor_mul(r[:], e, s2[:]))
            P.wait("vector", "e_vector", va)
            P.do("vector", lambda E, a=rtmp[0], b=rtmp[1], d=d0: E.tensor_sub(d, a[:], b[:]))
            P.do("vector", lambda E, e=e0, s2=sb, r=rtmp[0]: E.tensor_mul(r[:], e, s2[:]))
            vb = P.step("vector", lambda E, e=e1, c=cb, r=rtmp[1]: E.tensor_mul(r[:], e, c[:]))
            P.wait("vector", "e_vector", vb)
            vr = P.step("vector", lambda E, a=rtmp[0], b=rtmp[1], d=d1: E.tensor_add(d, a[:], b[:]))
        vsrcp = AP(psB.ap().tensor, 1024, [[2048, 128], [128, 8], [64, 2], [1, 64]])
        vdstp = AP(Vt.ap().tensor, 0, [[8 * 130, 128], [130, 8], [65, 2], [1, 64]])
        vv = P.step("vector", lambda E, sp=vsrcp, dp=vdstp: E.tensor_copy(dp, sp))
        for (ro, nm) in ((Qro, nmQ), (Kro, nmK)):
            P.wait("vector", "e_vector", vr)
            vq = P.step("vector", lambda E, r=ro, q=sqf: E.tensor_mul(q[:], r[:], r[:]))
            P.wait("vector", "e_vector", vq)
            red_src = AP(sqf.ap().tensor, 0, [[DM, 128], [HD, NH], [1, HD]])
            red_dst = AP(nupd.ap().tensor, 0, [[NH, 128], [1, NH], [1, 1]])
            vq2 = P.step("vector", lambda E, q=red_src, d=red_dst: E.reduce_sum(d, q, axis=AX))
            P.wait("vector", "e_vector", vq2)
            vq3 = P.step("vector", lambda E, a=nm, b=nupd: E.tensor_max(a[:], a[:], b[:]))
        prev_ve = vT
        for (ro, dd, hbm) in ((Qro, QTt, QT_h), (Kro, KTt, KT_h)):
            P.wait("tensor", "e_vector", vr if ro is Qro else vr)
            P.wait("tensor", "e_vector", prev_ve)
            for cch in range(NC):
                tt2 = P.step("tensor", lambda E, C=C, c=cch, s=ro, p=psT:
                             E.transpose(p[:, c * 128:(c + 1) * 128],
                                         s[:, c * 128:(c + 1) * 128], C.ident[:]))
            P.wait("vector", "e_tensor", tt2)
            P.wait("vector", f"d_sp{t % 2}", pm.get(f"d_sp{t % 2}", 0))
            prev_ve = P.step("vector", lambda E, p=psT, d=dd: E.tensor_copy(d[:], p[:]))
            P.wait("sync", "e_vector", prev_ve)
            src = AP(dd.ap().tensor, 0, [[DM, 128], [128, NC], [1, 128]])
            dst = AP(hbm.tensor, t * 128, [[S, 128], [128 * S, NC], [1, 128]])
            P.dma("sync", lambda E, d=dst, s=src: E.dma_start(out=d, in_=s), ctr=f"d_sp{t % 2}")
        P.wait("sync", "e_vector", vv)
        vsrc = AP(Vt.ap().tensor, 0, [[8 * 130, 128], [130, 8], [1, 130]])
        vdst = AP(V_h.tensor, t * 128 * 130, [[130, 128], [S * 130, 8], [1, 130]])
        P.dma("sync", lambda E, d=vdst, s=vsrc: E.dma_start(out=d, in_=s), ctr=f"d_sp{t % 2}")
        endm[t % 2] = P.marks()
        prevm = P.marks()
    ph.close()
    P.barrier()
    if PHASE_LIMIT <= 1:
        blk.close()
        return

    fz = ExitStack()
    nrow = fz.enter_context(nc.sbuf_tensor("nrow", [128, 2], F32))
    nbf = fz.enter_context(nc.sbuf_tensor("nbf", [128, 2], BF16))
    ones2 = fz.enter_context(nc.sbuf_tensor("ones2", [128, 1], F32))
    ntp = fz.enter_context(nc.psum_tensor("ntp", [128, 512], F32))
    ntpb = fz.enter_context(nc.psum_tensor("ntpb", [2, 128], BF16))
    v = P.step("vector", lambda E, o=ones2: E.memset(o[:], 1.0))
    v = P.step("vector", lambda E, a=nmQ, d=nrow: E.reduce_max(d[:, 0:1], a[:], axis=AX))
    P.wait("vector", "e_vector", v)
    v = P.step("vector", lambda E, a=nmK, d=nrow: E.reduce_max(d[:, 1:2], a[:], axis=AX))
    P.wait("vector", "e_vector", v)
    v = P.step("vector", lambda E, s=nrow, d=nbf: E.tensor_copy(d[:], s[:]))
    P.wait("tensor", "e_vector", v)
    t0 = P.step("tensor", lambda E, C=C, s=nbf, p=ntpb: E.transpose(p[:], s[:], C.ident[:]))
    P.wait("vector", "e_tensor", t0)
    v = P.step("vector", lambda E, p=ntpb, d=nrow: E.reduce_max(d[0:2, 0:1], p[:], axis=AX))
    P.wait("scalar", "e_vector", v)
    s1 = P.step("scalar", lambda E, d=nrow: E.activation(d[0:2, 1:2], d[0:2, 0:1], AF.Ln))
    P.wait("tensor", "e_scalar", s1)
    t1 = P.step("tensor", lambda E, s=nrow, o=ones2, p=ntp: E.matmul(
        p[0:1, 0:1], s[0:2, 1:2], o[0:2, 0:1], start=True, stop=True, skip_group_check=True))
    P.wait("scalar", "e_tensor", t1)
    s2 = P.step("scalar", lambda E, p=ntp, d=nrow, b=C.lnsc8: E.activation(
        d[0:1, 1:2], p[0:1, 0:1], AF.Exp, scale=0.5, bias=b))
    P.wait("vector", "e_scalar", s2)
    v = P.step("vector", lambda E, d=nrow: E.tensor_scalar_mul(d[0:1, 0:1], d[0:1, 1:2], -1.0))
    P.wait("tensor", "e_vector", v)
    t2 = P.step("tensor", lambda E, C=C, s=nrow, p=ntp: E.matmul(
        p[:, 0:1], C.ones_row[:], s[0:1, 0:1], start=True, stop=True, skip_group_check=True))
    P.wait("vector", "e_tensor", t2)
    v = P.step("vector", lambda E, p=ntp, d=negc: E.tensor_copy(d[:], p[:, 0:1]))
    fz.close()
    P.barrier()
    if PHASE_LIMIT <= 2:
        blk.close()
        return

    ph = ExitStack()
    KTp = ph.enter_context(nc.sbuf_tensor("ktp", [128, S], BF16))
    QTp = ph.enter_context(nc.sbuf_tensor("qtp", [128, S], BF16))
    Vp = ph.enter_context(nc.sbuf_tensor("vp", [128, NT * 130], BF16))
    Pt = [[ph.enter_context(nc.sbuf_tensor(f"pt{hh}_{i}", [128, 1024], BF16))
           for i in range(2)] for hh in range(2)]
    aT = [ph.enter_context(nc.sbuf_tensor(f"aT{hh}", [65, S], BF16)) for hh in range(2)]
    rcol8 = ph.enter_context(nc.sbuf_tensor("rcol8", [128, 8], F32))

    for pr in range(P3_PAIRS):
        m = P.marks()
        for e in ("e_tensor", "e_vector", "e_scalar"):
            P.wait("sync", e, m.get(e, 0))
        dprc = f"d_pr{pr % 2}"
        P.dma("sync", lambda E, d=KTp, s=KT_h, pr=pr:
              E.dma_start(out=d[:], in_=s[pr * 128:(pr + 1) * 128, :]), ctr=dprc)
        P.dma("sync", lambda E, d=QTp, s=QT_h, pr=pr:
              E.dma_start(out=d[:], in_=s[pr * 128:(pr + 1) * 128, :]), ctr=dprc)
        vpd = AP(Vp.ap().tensor, 0, [[NT * 130, 128], [130, NT], [1, 130]])
        vps = AP(V_h.tensor, pr * S * 130, [[130, 128], [130 * 128, NT], [1, 130]])
        wv = P.dma("sync", lambda E, d=vpd, s=vps: E.dma_start(out=d, in_=s), ctr=dprc)
        P.wait("tensor", dprc, wv)
        P.wait("scalar", dprc, wv)
        qs = ExitStack()
        psS = [qs.enter_context(nc.psum_tensor(f"psS{pr}_{hh}", [128, 1024], F32)) for hh in range(2)]
        psPV = [qs.enter_context(nc.psum_tensor(f"psPV{pr}_{hh}", [128, 1024], F32)) for hh in range(2)]
        for qq in range(P3_QQ):
            sE = {}
            tS = {}
            for kc in range(P3_KC):
                for hh in range(2):
                    hb = hh * 64
                    if kc >= 1:
                        P.wait("tensor", "e_scalar", sE[(kc - 1, hh)])
                    for j in range(2):
                        tS[(kc, hh)] = P.step("tensor",
                                    lambda E, k=KTp, q=QTp, p=psS[hh], kc=kc, qq=qq, j=j, hb=hb:
                                    E.matmul(p[:, j * 512:(j + 1) * 512],
                                             k[hb:hb + 64, kc * 128:(kc + 1) * 128],
                                             q[hb:hb + 64, qq * 1024 + j * 512:qq * 1024 + (j + 1) * 512],
                                             start=True, stop=True, skip_group_check=True))
                for hh in range(2):
                    P.wait("scalar", "e_tensor", tS[(kc, hh)])
                    sE[(kc, hh)] = P.step("scalar",
                                lambda E, p=psS[hh], d=Pt[hh][kc % 2], nb=negc, sc=scale:
                                E.activation(d[:], p[:], AF.Exp, bias=nb[:, 0:1], scale=sc))
                if kc >= 1:
                    for hh in range(2):
                        P.wait("tensor", "e_scalar", sE[(kc - 1, hh)])
                        for j in range(2):
                            tP = P.step("tensor",
                                        lambda E, v2=Vp, d=Pt[hh][(kc - 1) % 2], p=psPV[hh],
                                        kc=kc - 1, j=j, hh=hh:
                                        E.matmul(p[0:65, j * 512:(j + 1) * 512],
                                                 v2[:, kc * 130 + hh * 65:kc * 130 + hh * 65 + 65],
                                                 d[:, j * 512:(j + 1) * 512],
                                                 start=(kc == 0), stop=False,
                                                 skip_group_check=True))
            for hh in range(2):
                P.wait("tensor", "e_scalar", sE[(P3_KC - 1, hh)])
                for j in range(2):
                    tP = P.step("tensor",
                                lambda E, v2=Vp, d=Pt[hh][(P3_KC - 1) % 2], p=psPV[hh],
                                kc=P3_KC - 1, j=j, hh=hh:
                                E.matmul(p[0:65, j * 512:(j + 1) * 512],
                                         v2[:, kc * 130 + hh * 65:kc * 130 + hh * 65 + 65],
                                         d[:, j * 512:(j + 1) * 512],
                                         start=False, stop=True, skip_group_check=True))
            P.wait("vector", "e_tensor", tP)
            for hh in range(2):
                vA = P.step("vector", lambda E, p=psPV[hh], d=aT[hh], qq=qq:
                            E.tensor_copy(d[:, qq * 1024:(qq + 1) * 1024], p[0:65, :]))
            P.wait("tensor", "e_vector", vA)
        qs.close()
        es2 = ExitStack()
        psM = [es2.enter_context(nc.psum_tensor(f"psM{pr}_{i}", [128, 1024], BF16)) for i in range(2)]
        for hh in range(P3_HH):
            h = pr * 2 + hh
            vg = {}
            P.wait("tensor", "e_vector", P.last.get("e_vector", 0))
            for g in range(4):
                pq = psM[g % 2]
                if g >= 2:
                    P.wait("tensor", "e_vector", vg[g - 2])
                for j8 in range(8):
                    last_tT = P.step("tensor", lambda E, C=C, s=aT[hh], p=pq, g=g, j8=j8:
                                     E.transpose(p[:, j8 * 66:j8 * 66 + 65],
                                                 s[0:65, (g * 8 + j8) * 128:(g * 8 + j8 + 1) * 128],
                                                 C.ident[0:65, 0:65]))
                P.wait("vector", "e_tensor", last_tT)
                den_src = AP(pq.ap().tensor, 64, [[1024, 128], [66, 8]])
                v1 = P.step("vector", lambda E, s=den_src, d=rcol8:
                            E.tensor_single_scalar(d[:], s, 1e-38, ALU.add))
                P.wait("vector", "e_vector", v1)
                v2 = P.step("vector", lambda E, d=rcol8: E.reciprocal(d[:], d[:]))
                P.wait("vector", "e_vector", v2)
                for j8 in range(8):
                    qt = g * 8 + j8
                    vg[g] = P.step("vector", lambda E, p=pq, r=rcol8, d=attno, qt=qt, h=h, j8=j8:
                                E.tensor_scalar_mul(d[:, qt * DM + h * 64:qt * DM + (h + 1) * 64],
                                                    p[:, j8 * 66:j8 * 66 + 64], r[:, j8:j8 + 1]))
        es2.close()
    ph.close()
    P.barrier()

    ph = ExitStack()
    Wo_sb = ph.enter_context(nc.sbuf_tensor("wo", [128, NC * DM], BF16))
    bo_sb = ph.enter_context(nc.sbuf_tensor("bo", [1, DM], F32))
    grep = ph.enter_context(nc.sbuf_tensor("grep", [128, DM], F32))
    brep = ph.enter_context(nc.sbuf_tensor("brep", [128, DM], F32))
    grow = ph.enter_context(nc.sbuf_tensor("grow", [1, DM], F32))
    brow = ph.enter_context(nc.sbuf_tensor("brow", [1, DM], F32))
    xb2 = [ph.enter_context(nc.sbuf_tensor(f"xb2{i}", [128, DM], F32)) for i in range(4)]
    aTt = [ph.enter_context(nc.sbuf_tensor(f"aTt{i}", [128, DM], BF16)) for i in range(4)]
    ht = [ph.enter_context(nc.sbuf_tensor(f"ht{i}", [128, DM], F32)) for i in range(4)]
    yt = [ph.enter_context(nc.sbuf_tensor(f"yt{i}", [128, DM], F32)) for i in range(4)]
    sq_tmp = [ph.enter_context(nc.sbuf_tensor(f"sq_tmp2{i}", [128, 24], F32)) for i in range(4)]
    stat = [ph.enter_context(nc.sbuf_tensor(f"stat2{i}", [128, 4], F32)) for i in range(4)]
    psT2 = [ph.enter_context(nc.psum_tensor(f"psT2{i}", [128, 1024], BF16)) for i in range(3)]
    psO = [ph.enter_context(nc.psum_tensor(f"psO{i}", [128, 512], F32)) for i in range(3)]
    psBC = ph.enter_context(nc.psum_tensor("psBC", [128, 512], F32))

    wo_src = AP(W["Wo"].tensor, 0, [[DM, 128], [128 * DM, NC], [1, DM]])
    wo_dst = AP(Wo_sb.ap().tensor, 0, [[NC * DM, 128], [DM, NC], [1, DM]])
    gw2 = P.dma("gpsimd", lambda E, s=wo_src, d=wo_dst: E.dma_start(out=d, in_=s), ctr="d_wo")
    P.dma("sync", lambda E, d=bo_sb, s=row_ap(W["bo"], DM): E.dma_start(out=d[:], in_=s), ctr="d_pre")
    P.dma("sync", lambda E, d=grow, s=row_ap(W["gamma"], DM): E.dma_start(out=d[:], in_=s), ctr="d_pre")
    l3 = P.dma("sync", lambda E, d=brow, s=row_ap(W["beta"], DM): E.dma_start(out=d[:], in_=s), ctr="d_pre")
    P.wait("tensor", "d_pre", l3)
    v1b = bcast_row(P, C, grow.ap(), DM, grep, psBC)
    vbc = bcast_row(P, C, brow.ap(), DM, brep, psBC, after_v=v1b)
    P.wait("tensor", "d_wo", gw2)
    P.wait("tensor", "e_vector", vbc)

    endm4 = [P.marks() for _ in range(4)]
    endp3 = [P.marks() for _ in range(3)]
    def _zbody(t):
            par = t % 4
            pm = endm4[par]
            pm3 = endp3[t % 3]
            xb, aT2, htp, ytp, sqp, stp = xb2[par], aTt[par], ht[par], yt[par], sq_tmp[par], stat[par]
            pT, pO = psT2[t % 3], psO[t % 3]
            P.wait("sync", "e_vector", pm.get("e_vector", 0))
            P.wait("sync", "e_scalar", pm.get("e_scalar", 0))
            vx = P.dma("sync", lambda E, d=xb, s=xin, t=t:
                       E.dma_start(out=d[:], in_=s[t * 128:(t + 1) * 128, :]), ctr=f"d_x{par}")
            P.wait("tensor", "e_vector", pm3.get("e_vector", 0))
            for cch in range(NC):
                tt = P.step("tensor", lambda E, C=C, s=attno, p=pT, t=t, c=cch:
                            E.transpose(p[:, c * 128:(c + 1) * 128],
                                        s[:, t * DM + c * 128:t * DM + (c + 1) * 128], C.ident[:]))
            P.wait("vector", "e_tensor", tt)
            P.wait("vector", f"d_x{par}", vx)
            vT = P.step("vector", lambda E, p=pT, d=aT2: E.tensor_copy(d[:], p[:]))
            P.wait("tensor", "e_vector", vT)
            for jj in range(2):
                for cch in range(NC):
                    P.do("tensor", lambda E, c=cch, j=jj, a=aT2, w=Wo_sb, p=pO:
                         E.matmul(p[:], a[:, c * 128:(c + 1) * 128],
                                  w[:, c * DM + j * 512:c * DM + (j + 1) * 512],
                                  start=(c == 0), stop=False, skip_group_check=True))
                tb = P.step("tensor", lambda E, C=C, j=jj, b=bo_sb, p=pO:
                            E.matmul(p[:], C.ones_row[:],
                                     b[:, j * 512:(j + 1) * 512], start=False, stop=True,
                                     skip_group_check=True))
                P.wait("vector", "e_tensor", tb)
                vev = P.step("vector", lambda E, p=pO, d=ytp, j=jj:
                             E.tensor_copy(d[:, j * 512:(j + 1) * 512], p[:]))
                P.wait("tensor", "e_vector", vev)
            P.wait("vector", f"d_o{par}", pm.get(f"d_o{par}", 0))
            P.wait("scalar", f"d_x{par}", vx)
            P.wait("scalar", "e_vector", pm.get("e_vector", 0))
            rs = ln_stats(P, xb.ap(), DM, stp.ap(), C.eps5, sqp.ap())
            vln = ln_apply(P, xb.ap(), htp.ap(), stp.ap(), rs)
            P.wait("vector", "e_vector", [vln, vev])
            vh = P.step("vector", lambda E, a=ytp, b=htp, d=htp: E.tensor_add(d[:], a[:], b[:]))
            P.wait("vector", "e_vector", vh)
            P.wait("scalar", "e_vector", vh)
            rs2 = ln_stats(P, htp.ap(), DM, stp.ap(), C.eps5, sqp.ap())
            P.wait("vector", "e_scalar", rs2)
            vy = P.step("vector", lambda E, hsrc=htp, g=grep, st=stp, d=ytp:
                        E.scalar_tensor_tensor(d[:], hsrc[:], st[:, 2:3], g[:], ALU.subtract, ALU.mult))
            P.wait("vector", "e_vector", vy)
            vy2 = P.step("vector", lambda E, d=ytp, st=stp: E.tensor_scalar_mul(d[:], d[:], st[:, 3:4]))
            P.wait("vector", "e_vector", vy2)
            vy3 = P.step("vector", lambda E, d=ytp, b=brep: E.tensor_add(d[:], d[:], b[:]))
            P.wait("vector", "e_vector", vy3)
            P.wait("scalar", "e_vector", vy3)
            rs3 = ln_stats(P, ytp.ap(), DM, stp.ap(), C.eps5, sqp.ap())
            vo1 = ln_apply(P, ytp.ap(), htp.ap(), stp.ap(), rs3)
            P.wait("vector", "e_vector", vo1)
            vo2 = P.step("vector", lambda E, x=xb, hsrc=htp, d=htp: E.tensor_add(d[:], x[:], hsrc[:]))
            P.wait("sync", "e_vector", vo2)
            P.dma("sync", lambda E, d=xout, s=htp, t=t:
                  E.dma_start(out=d[t * 128:(t + 1) * 128, :], in_=s[:]), ctr=f"d_o{par}")
            endm4[par] = endp3[t % 3] = P.marks()
    for _z0 in range(0, (P4_NT), 2):
        _rs = []
        for _zi in range(min(2, P4_NT - _z0)):
            P.record(); _zbody(_z0 + _zi); _rs.append(P.stop())
        P.emit_zip(*_rs)
    ph.close()
    blk.close()


def build_l0_attn():
    nc = bass.Bass("TRN2", target_bir_lowering=False)
    es = ExitStack()
    P = Pr(nc, es)
    xin = nc.dram_tensor("xin", [S, DM], F32, kind="ExternalInput").ap()
    xout = nc.dram_tensor("xout", [S, DM], F32, kind="ExternalOutput").ap()
    W = {
        "Wqkv": nc.dram_tensor("w_qkv", [DM, 3 * DM], F32, kind="ExternalInput").ap(),
        "bqkv": nc.dram_tensor("w_bqkv", [3 * DM], F32, kind="ExternalInput").ap(),
        "Wo": nc.dram_tensor("w_wo", [DM, DM], F32, kind="ExternalInput").ap(),
        "bo": nc.dram_tensor("w_bo", [DM], F32, kind="ExternalInput").ap(),
        "gamma": nc.dram_tensor("w_gamma", [DM], F32, kind="ExternalInput").ap(),
        "beta": nc.dram_tensor("w_beta", [DM], F32, kind="ExternalInput").ap(),
    }
    C = setup_common(P, need_rope=True)
    emit_l0_attn(P, C, xin, xout, W)
    for n in [k for k in P.last if k.startswith("d_o")]:
        P.wait("sync", n, P.last[n])
    P.finish(nc)
    es.close()
    return nc


def l0_attn_inmap(inputs, b, hc):
    return {
        "xin": np.ascontiguousarray(inputs["L0_attn_x"][b]),
        "w_qkv": inputs["L0_attn_Wqkv"], "w_bqkv": inputs["L0_attn_bqkv"],
        "w_wo": inputs["L0_attn_Wo"], "w_bo": inputs["L0_attn_bo"],
        "w_gamma": inputs["L0_attn_gamma"], "w_beta": inputs["L0_attn_beta"],
        "c_ident": hc["ident"], "c_ones": hc["ones_row_f32"],
        "c_cosrep": hc["cosrep"], "c_sinrep": hc["sinrep"],
    }


def kernel(**inputs):
    hc = host_consts()
    cores = list(range(8))
    nc = build_fused()
    res = run_bass_kernel_spmd(nc, [fused_inmap(inputs, b, hc) for b in cores], core_ids=cores)
    return np.stack([r["xout"] for r in res.results]).astype(np.float32)

def emit_l0_ffn(P, C, xin, xout, W):
    nc = P.nc
    blk = ExitStack()
    WTr = blk.enter_context(nc.sbuf_tensor("wtr", [128, 4 * 512], BF16))
    WTi = blk.enter_context(nc.sbuf_tensor("wti", [128, 4 * 512], BF16))
    WTin = blk.enter_context(nc.sbuf_tensor("wtin", [128, 4 * 512], BF16))
    brr = blk.enter_context(nc.sbuf_tensor("brr", [1, 512], F32))
    bri = blk.enter_context(nc.sbuf_tensor("bri", [1, 512], F32))
    g1t = blk.enter_context(nc.sbuf_tensor("g1t", [128, 512], F32))
    g2t = blk.enter_context(nc.sbuf_tensor("g2t", [128, 512], F32))
    accw = blk.enter_context(nc.sbuf_tensor("accw", [128, 512], F32))
    prodw = blk.enter_context(nc.sbuf_tensor("prodw", [128, 512], F32))

    P.dma("sync", lambda E, d=brr, s=row_ap(W["bias_real"], 512): E.dma_start(out=d[:], in_=s), ctr="d_pre")
    P.dma("sync", lambda E, d=bri, s=row_ap(W["bias_imag"], 512): E.dma_start(out=d[:], in_=s), ctr="d_pre")
    for (g1h, g2h, dst, extra_neg) in ((W["g1xr"], W["g2xr"], WTr, None),
                                       (W["g1xi"], W["g2xi"], WTi, WTin)):
        for k in range(4):
            last_v = None
            for c in range(4):
                ld = P.dma("sync", lambda E, d=g1t, s=g1h, c=c, k=k:
                           E.dma_start(out=d[:], in_=s[c, k * 128:(k + 1) * 128, :]), ctr="d_pre")
                ld = P.dma("sync", lambda E, d=g2t, s=g2h, c=c, k=k:
                           E.dma_start(out=d[:], in_=s[c, k * 128:(k + 1) * 128, :]), ctr="d_pre")
                P.wait("vector", "d_pre", ld)
                if c == 0:
                    last_v = P.step("vector", lambda E, a=g1t, b=g2t, d=accw:
                                    E.tensor_mul(d[:], a[:], b[:]))
                else:
                    vp_ = P.step("vector", lambda E, a=g1t, b=g2t, d=prodw:
                                 E.tensor_mul(d[:], a[:], b[:]))
                    P.wait("vector", "e_vector", vp_)
                    last_v = P.step("vector", lambda E, d=accw, p=prodw:
                                    E.tensor_add(d[:], d[:], p[:]))
                P.wait("vector", "e_vector", last_v)
                P.wait("sync", "e_vector", last_v)
            vc = P.step("vector", lambda E, d=dst, a=accw, k=k:
                        E.tensor_copy(d[:, k * 512:(k + 1) * 512], a[:]))
            if extra_neg is not None:
                P.wait("vector", "e_vector", vc)
                vc = P.step("vector", lambda E, d=extra_neg, a=accw, k=k:
                            E.tensor_scalar_mul(d[:, k * 512:(k + 1) * 512], a[:], -1.0))
            P.wait("vector", "e_vector", vc)

    ph = ExitStack()
    xt = [ph.enter_context(nc.sbuf_tensor(f"fxt{i}", [128, DM], F32)) for i in range(4)]
    lnb = [ph.enter_context(nc.sbuf_tensor(f"flnb{i}", [128, DM], BF16)) for i in range(4)]
    lnTt = [ph.enter_context(nc.sbuf_tensor(f"flnTt{i}", [128, DM], BF16)) for i in range(4)]
    sq_tmp = [ph.enter_context(nc.sbuf_tensor(f"fsq{i}", [128, 24], F32)) for i in range(4)]
    stat = [ph.enter_context(nc.sbuf_tensor(f"fstat{i}", [128, 4], F32)) for i in range(4)]
    yt = [ph.enter_context(nc.sbuf_tensor(f"fyt{i}", [128, DM], F32)) for i in range(4)]
    ht = [ph.enter_context(nc.sbuf_tensor(f"fht{i}", [128, DM], F32)) for i in range(4)]
    psT = [ph.enter_context(nc.psum_tensor(f"fpsT{i}", [128, 1024], BF16)) for i in range(2)]
    psR = [ph.enter_context(nc.psum_tensor(f"fpsR{i}", [128, 512], F32)) for i in range(2)]
    psI = [ph.enter_context(nc.psum_tensor(f"fpsI{i}", [128, 512], F32)) for i in range(2)]

    endm = [P.marks(), P.marks()]
    def _zbody(t):
            par = t % 2
            pm = endm[par]
            xb, lb2, lT, sqp, stp, ytp, htp = (xt[par], lnb[par], lnTt[par], sq_tmp[par],
                                               stat[par], yt[par], ht[par])
            pT, pR, pI = psT[par], psR[par], psI[par]
            P.wait("sync", "e_vector", pm.get("e_vector", 0))
            P.wait("sync", "e_scalar", pm.get("e_scalar", 0))
            vx = P.dma("sync", lambda E, d=xb, s=xin, t=t:
                       E.dma_start(out=d[:], in_=s[t * 128:(t + 1) * 128, :]), ctr=f"d_x{par}")
            P.wait("vector", f"d_x{par}", vx)
            P.wait("scalar", f"d_x{par}", vx)
            P.wait("vector", "e_tensor", pm.get("e_tensor", 0))
            P.wait("vector", f"d_o{par}", pm.get(f"d_o{par}", 0))
            P.wait("scalar", "e_vector", pm.get("e_vector", 0))
            rs = ln_stats(P, xb.ap(), DM, stp.ap(), C.eps5, sqp.ap())
            vln = ln_apply(P, xb.ap(), lb2.ap(), stp.ap(), rs)
            P.wait("tensor", "e_vector", vln)
            for cch in range(NC):
                tt = P.step("tensor", lambda E, C=C, c=cch, s=lb2, p=pT:
                            E.transpose(p[:, c * 128:(c + 1) * 128],
                                        s[:, c * 128:(c + 1) * 128], C.ident[:]))
            P.wait("vector", "e_tensor", tt)
            vT = P.step("vector", lambda E, p=pT, d=lT: E.tensor_copy(d[:], p[:]))
            P.wait("tensor", "e_vector", vT)
            for cch in range(NC):
                wsel_r = WTr if cch < 4 else WTin
                wsel_i = WTi if cch < 4 else WTr
                kk = cch % 4
                P.do("tensor", lambda E, c=cch, k=kk, l=lT, w=wsel_r, p=pR:
                     E.matmul(p[:], l[:, c * 128:(c + 1) * 128],
                              w[:, k * 512:(k + 1) * 512],
                              start=(c == 0), stop=False, skip_group_check=True))
                P.do("tensor", lambda E, c=cch, k=kk, l=lT, w=wsel_i, p=pI:
                     E.matmul(p[:], l[:, c * 128:(c + 1) * 128],
                              w[:, k * 512:(k + 1) * 512],
                              start=(c == 0), stop=False, skip_group_check=True))
            P.do("tensor", lambda E, C=C, b=brr, p=pR:
                 E.matmul(p[:], C.ones_row[:], b[:], start=False, stop=True,
                          skip_group_check=True))
            tb = P.step("tensor", lambda E, C=C, b=bri, p=pI:
                        E.matmul(p[:], C.ones_row[:], b[:], start=False, stop=True,
                                 skip_group_check=True))
            P.wait("vector", "e_tensor", tb)
            vy1 = P.step("vector", lambda E, p=pR, d=ytp: E.tensor_copy(d[:, 0:512], p[:]))
            vy2 = P.step("vector", lambda E, p=pI, d=ytp: E.tensor_copy(d[:, 512:1024], p[:]))
            P.wait("vector", "e_vector", vy2)
            P.wait("scalar", "e_vector", vy2)
            rs2 = ln_stats(P, ytp.ap(), DM, stp.ap(), C.eps5, sqp.ap())
            vo1 = ln_apply(P, ytp.ap(), htp.ap(), stp.ap(), rs2)
            P.wait("vector", "e_vector", vo1)
            vo2 = P.step("vector", lambda E, x=xb, h=htp, d=htp: E.tensor_add(d[:], x[:], h[:]))
            P.wait("sync", "e_vector", vo2)
            P.dma("sync", lambda E, d=xout, s=htp, t=t:
                  E.dma_start(out=d[t * 128:(t + 1) * 128, :], in_=s[:]), ctr=f"d_o{par}")
            endm[par] = P.marks()
    for _z0 in range(0, (NT), 2):
        _rs = []
        for _zi in range(2):
            P.record(); _zbody(_z0 + _zi); _rs.append(P.stop())
        P.emit_zip(*_rs)
    ph.close()
    blk.close()


def build_l0_ffn():
    nc = bass.Bass("TRN2", target_bir_lowering=False)
    es = ExitStack()
    P = Pr(nc, es)
    xin = nc.dram_tensor("xin", [S, DM], F32, kind="ExternalInput").ap()
    xout = nc.dram_tensor("xout", [S, DM], F32, kind="ExternalOutput").ap()
    W = {
        "g1xr": nc.dram_tensor("w_g1xr", [4, 512, 512], F32, kind="ExternalInput").ap(),
        "g2xr": nc.dram_tensor("w_g2xr", [4, 512, 512], F32, kind="ExternalInput").ap(),
        "g1xi": nc.dram_tensor("w_g1xi", [4, 512, 512], F32, kind="ExternalInput").ap(),
        "g2xi": nc.dram_tensor("w_g2xi", [4, 512, 512], F32, kind="ExternalInput").ap(),
        "bias_real": nc.dram_tensor("w_br", [512], F32, kind="ExternalInput").ap(),
        "bias_imag": nc.dram_tensor("w_bi", [512], F32, kind="ExternalInput").ap(),
    }
    C = setup_common(P, need_rope=False)
    emit_l0_ffn(P, C, xin, xout, W)
    for n in [k for k in P.last if k.startswith("d_o")]:
        P.wait("sync", n, P.last[n])
    P.finish(nc)
    es.close()
    return nc


def l0_ffn_views(inputs):
    out = {}
    for (nm, g1, g2) in (("r", inputs["L0_ffn_G1_R"], inputs["L0_ffn_G2_R"]),
                         ("i", inputs["L0_ffn_G1_I"], inputs["L0_ffn_G2_I"])):
        a = np.empty((4, 512, 512), np.float32)
        b = np.empty((4, 512, 512), np.float32)
        for c in range(4):
            A = g1[0, :, c, :]
            B = g2[c, :, 0, :]
            a[c] = np.broadcast_to(A.T[:, None, :, None], (32, 16, 32, 16)).reshape(512, 512)
            b[c] = np.broadcast_to(B.T[None, :, None, :], (32, 16, 32, 16)).reshape(512, 512)
        out["g1x" + nm] = a
        out["g2x" + nm] = b
    return out


def l0_ffn_inmap(inputs, xin, hc):
    v = l0_ffn_views(inputs)
    return {
        "xin": np.ascontiguousarray(xin),
        "w_g1xr": v["g1xr"], "w_g2xr": v["g2xr"],
        "w_g1xi": v["g1xi"], "w_g2xi": v["g2xi"],
        "w_br": inputs["L0_ffn_bias_real"], "w_bi": inputs["L0_ffn_bias_imag"],
        "c_ident": hc["ident"], "c_ones": hc["ones_row_f32"],
    }


GOLDEN_C = float(1.0 / math.e)
GOLDEN_LO = float(0.5 - math.log(4.0 / 3.0))
GOLDEN_UP = 0.5

def emit_l1_ffn(P, C, xin, xout, W):
    nc = P.nc
    ln3T_h = nc.dram_tensor("l1f_lnt", [DM, S], BF16).ap()

    blk = ExitStack()
    wr_sb = blk.enter_context(nc.sbuf_tensor("wr", [128, NC * 8], BF16))
    w_sb = blk.enter_context(nc.sbuf_tensor("wrt", [128, NT * 8], F32))
    t11 = blk.enter_context(nc.sbuf_tensor("t11", [1, 2], F32))
    ninvt = blk.enter_context(nc.sbuf_tensor("ninvt", [128, 1], F32))

    wr_src = AP(W["Wr"].tensor, 0, [[8, 128], [128 * 8, NC], [1, 8]])
    wr_dst = AP(wr_sb.ap().tensor, 0, [[NC * 8, 128], [8, NC], [1, 8]])
    gw = P.dma("gpsimd", lambda E, s=wr_src, d=wr_dst: E.dma_start(out=d, in_=s), ctr="d_wr")
    lt = P.dma("sync", lambda E, d=t11, s=row_ap(W["temp"], 1): E.dma_start(out=d[0:1, 0:1], in_=s), ctr="d_pre")
    P.wait("vector", "d_pre", lt)
    v = P.step("vector", lambda E, d=t11: E.reciprocal(d[0:1, 1:2], d[0:1, 0:1]))
    P.wait("vector", "e_vector", v)
    v = P.step("vector", lambda E, d=t11: E.tensor_scalar_mul(d[0:1, 0:1], d[0:1, 1:2], -1.0))
    P.wait("tensor", "e_vector", v)

    ph = ExitStack()
    xt = [ph.enter_context(nc.sbuf_tensor(f"gxt{i}", [128, DM], F32)) for i in range(4)]
    lnb = [ph.enter_context(nc.sbuf_tensor(f"glnb{i}", [128, DM], BF16)) for i in range(4)]
    lnTt = [ph.enter_context(nc.sbuf_tensor(f"glnTt{i}", [128, DM], BF16)) for i in range(4)]
    sq_tmp = [ph.enter_context(nc.sbuf_tensor(f"gsq{i}", [128, 24], F32)) for i in range(4)]
    stat = [ph.enter_context(nc.sbuf_tensor(f"gstat{i}", [128, 4], F32)) for i in range(4)]
    rtb = [ph.enter_context(nc.sbuf_tensor(f"rtb{i}", [128, 64], F32)) for i in range(4)]
    rc = [ph.enter_context(nc.sbuf_tensor(f"rc{i}", [128, 8], F32)) for i in range(4)]
    psT = [ph.enter_context(nc.psum_tensor(f"gpsT{i}", [128, 1024], BF16)) for i in range(3)]
    psZ = [ph.enter_context(nc.psum_tensor(f"gpsZ{i}", [128, 8], F32)) for i in range(3)]
    psBC2 = ph.enter_context(nc.psum_tensor("gpsBC", [128, 512], F32))

    t0 = P.step("tensor", lambda E, C=C, s=t11, p=psBC2: E.matmul(
        p[:, 0:1], C.ones_row[:], s[0:1, 0:1], start=True, stop=True, skip_group_check=True))
    P.wait("vector", "e_tensor", t0)
    v = P.step("vector", lambda E, p=psBC2, d=ninvt: E.tensor_copy(d[:], p[:, 0:1]))
    P.wait("tensor", "d_wr", gw)

    prevm = P.marks()
    endg = [P.marks() for _ in range(4)]
    def _zbody(t):
            par = t % 4
            pm2 = endg[par]
            rsl = (lambda par: (lambda i: rtb[par].ap()[:, i * 8:(i + 1) * 8]))(par)
            rcp = rc[par]
            lb2, lT, sqp, stp, pT, pZ = (lnb[par], lnTt[par], sq_tmp[par], stat[par],
                                         psT[t % 3], psZ[t % 3])
            P.wait("sync", "e_vector", pm2.get("e_vector", 0))
            P.wait("sync", "e_scalar", pm2.get("e_scalar", 0))
            xb = xt[par]
            vx = P.dma("sync", lambda E, d=xb, s=xin, t=t:
                       E.dma_start(out=d[:], in_=s[t * 128:(t + 1) * 128, :]), ctr=f"d_x{par}")
            P.wait("vector", f"d_x{par}", vx)
            P.wait("scalar", f"d_x{par}", vx)
            P.wait("vector", "e_tensor", pm2.get("e_tensor", 0))
            P.wait("scalar", "e_vector", pm2.get("e_vector", 0))
            rs = ln_stats(P, xb.ap(), DM, stp.ap(), C.eps5, sqp.ap())
            vln = ln_apply(P, xb.ap(), lb2.ap(), stp.ap(), rs)
            P.wait("tensor", "e_vector", vln)
            for cch in range(NC):
                tt = P.step("tensor", lambda E, C=C, c=cch, s=lb2, p=pT:
                            E.transpose(p[:, c * 128:(c + 1) * 128],
                                        s[:, c * 128:(c + 1) * 128], C.ident[:]))
            P.wait("vector", "e_tensor", tt)
            P.wait("vector", f"d_sp{par}", pm2.get(f"d_sp{par}", 0))
            vT = P.step("vector", lambda E, p=pT, d=lT: E.tensor_copy(d[:], p[:]))
            P.wait("sync", "e_vector", vT)
            src = AP(lT.ap().tensor, 0, [[DM, 128], [128, NC], [1, 128]])
            dst = AP(ln3T_h.tensor, t * 128, [[S, 128], [128 * S, NC], [1, 128]])
            P.dma("sync", lambda E, d=dst, s=src: E.dma_start(out=d, in_=s), ctr=f"d_sp{par}")
            P.wait("tensor", "e_vector", vT)
            for cch in range(NC):
                tz = P.step("tensor", lambda E, c=cch, l=lT, w=wr_sb, p=pZ:
                            E.matmul(p[:], l[:, c * 128:(c + 1) * 128],
                                     w[:, c * 8:(c + 1) * 8],
                                     start=(c == 0), stop=(c == NC - 1), skip_group_check=True))
            P.wait("scalar", "e_tensor", tz)
            s1 = P.step("scalar", lambda E, p=pZ, d=rsl(0), sc=ninvt:
                        E.activation(d, p[:], AF.Exp, scale=sc[:, 0:1]))
            P.wait("vector", "e_scalar", s1)
            v = P.step("vector", lambda E, a=rsl(0), d=rsl(1): E.tensor_scalar_add(d, a, 1.0))
            P.wait("vector", "e_vector", v)
            v = P.step("vector", lambda E, d=rsl(1): E.reciprocal(d, d))
            P.wait("vector", "e_vector", v)
            v = P.step("vector", lambda E, a=rsl(1), d=rsl(2): E.tensor_single_scalar(d, a, GOLDEN_LO, ALU.is_ge))
            P.wait("vector", "e_vector", v)
            v = P.step("vector", lambda E, a=rsl(1), d=rsl(6): E.tensor_single_scalar(d, a, GOLDEN_UP, ALU.is_le))
            P.wait("vector", "e_vector", v)
            v = P.step("vector", lambda E, a=rsl(2), b=rsl(6), d=rsl(2): E.tensor_mul(d, a, b))
            P.wait("vector", "e_vector", v)
            v = P.step("vector", lambda E, a=rsl(1), d=rsl(3): E.tensor_single_scalar(d, a, GOLDEN_C, ALU.subtract))
            P.wait("vector", "e_vector", v)
            v = P.step("vector", lambda E, a=rsl(3), d=rsl(6): E.tensor_scalar_mul(d, a, -1.0))
            P.wait("vector", "e_vector", v)
            v = P.step("vector", lambda E, a=rsl(3), b=rsl(6), d=rsl(3): E.tensor_max(d, a, b))
            P.wait("scalar", "e_vector", v)
            s2 = P.step("scalar", lambda E, a=rsl(3), d=rsl(4): E.activation(d, a, AF.Exp, scale=-10.0))
            s3 = P.step("scalar", lambda E, a=rsl(3), d=rsl(5): E.activation(d, a, AF.Exp, scale=float(-1.0 / 0.3)))
            P.wait("vector", "e_scalar", s2)
            v = P.step("vector", lambda E, a=rsl(4), b=rsl(2), d=rsl(4): E.tensor_mul(d, a, b))
            P.wait("vector", "e_vector", v)
            v = P.step("vector", lambda E, a=rsl(4), d=rcp: E.reduce_sum(d[:, 0:1], a, axis=AX))
            P.wait("vector", "e_vector", v)
            v = P.step("vector", lambda E, d=rcp: E.tensor_single_scalar(d[:, 1:2], d[:, 0:1], 1e-8, ALU.is_lt))
            P.wait("vector", "e_scalar", s3)
            P.wait("vector", "e_vector", v)
            v = P.step("vector", lambda E, a=rsl(5), d=rcp: E.reduce_max(d[:, 2:3], a, axis=AX))
            P.wait("vector", "e_vector", v)
            v = P.step("vector", lambda E, a=rsl(5), d=rsl(6), m=rcp: E.tensor_single_scalar(d, a, m[:, 2:3], ALU.is_equal))
            P.wait("vector", "e_vector", v)
            v = P.step("vector", lambda E, a=rsl(6), b=rsl(5), d=rsl(7): E.scalar_tensor_tensor(d, a, -1e30, b, ALU.mult, ALU.add))
            P.wait("vector", "e_vector", v)
            v = P.step("vector", lambda E, a=rsl(7), d=rcp: E.reduce_max(d[:, 3:4], a, axis=AX))
            P.wait("vector", "e_vector", v)
            v = P.step("vector", lambda E, a=rsl(5), d=rsl(6), m=rcp: E.tensor_single_scalar(d, a, m[:, 3:4], ALU.is_ge))
            P.wait("vector", "e_vector", v)
            v = P.step("vector", lambda E, a=rsl(5), b=rsl(6), d=rsl(7): E.tensor_mul(d, a, b))
            P.wait("vector", "e_vector", v)
            v = P.step("vector", lambda E, a=rsl(7), d=rcp: E.reduce_sum(d[:, 4:5], a, axis=AX))
            P.wait("vector", "e_vector", v)
            v = P.step("vector", lambda E, d=rcp: E.tensor_scalar_max(d[:, 4:5], d[:, 4:5], 1e-8))
            P.wait("vector", "e_vector", v)
            v = P.step("vector", lambda E, d=rcp: E.reciprocal(d[:, 4:5], d[:, 4:5]))
            P.wait("vector", "e_vector", v)
            v = P.step("vector", lambda E, a=rsl(7), d=rsl(7), m=rcp: E.tensor_scalar_mul(d, a, m[:, 4:5]))
            P.wait("vector", "e_vector", v)
            v = P.step("vector", lambda E, a=rsl(7), b=rsl(4), d=rsl(6): E.tensor_sub(d, a, b))
            P.wait("vector", "e_vector", v)
            v = P.step("vector", lambda E, a=rsl(6), b=rsl(4), d=rsl(4), m=rcp: E.scalar_tensor_tensor(d, a, m[:, 1:2], b, ALU.mult, ALU.add))
            P.wait("vector", "e_vector", v)
            v = P.step("vector", lambda E, a=rsl(4), d=rcp: E.reduce_sum(d[:, 5:6], a, axis=AX))
            P.wait("vector", "e_vector", v)
            v = P.step("vector", lambda E, d=rcp: E.tensor_scalar_max(d[:, 5:6], d[:, 5:6], 1e-8))
            P.wait("vector", "e_vector", v)
            v = P.step("vector", lambda E, d=rcp: E.reciprocal(d[:, 5:6], d[:, 5:6]))
            P.wait("vector", "e_vector", v)
            v = P.step("vector", lambda E, a=rsl(4), d=w_sb, t=t, m=rcp:
                       E.tensor_scalar_mul(d[:, t * 8:(t + 1) * 8], a, m[:, 5:6]))
            endg[par] = P.marks()
    for _z0 in range(0, (NT), 2):
        _rs = []
        for _zi in range(min(2, NT - _z0)):
            P.record(); _zbody(_z0 + _zi); _rs.append(P.stop())
        P.emit_zip(*_rs)
    ph.close()
    P.barrier()

    ph = ExitStack()
    lnq = ph.enter_context(nc.sbuf_tensor("lnq", [128, NC * 1024], BF16))
    hq = ph.enter_context(nc.sbuf_tensor("hq", [128, 8 * 1024], BF16))
    gsb = ph.enter_context(nc.sbuf_tensor("gsb", [128, 1024], BF16))
    acc = ph.enter_context(nc.sbuf_tensor("acc", [128, 8 * 1024], F32))
    wg_sb = [ph.enter_context(nc.sbuf_tensor(f"wg{i}", [128, NC * 1024], BF16)) for i in range(2)]
    wu_sb = [ph.enter_context(nc.sbuf_tensor(f"wu{i}", [128, NC * 1024], BF16)) for i in range(2)]
    wd_sb = [ph.enter_context(nc.sbuf_tensor(f"wd{i}", [128, NC * 1024], BF16)) for i in range(2)]
    xt2 = [ph.enter_context(nc.sbuf_tensor(f"hxt{i}", [128, DM], F32)) for i in range(4)]
    sq2 = [ph.enter_context(nc.sbuf_tensor(f"hsq{i}", [128, 24], F32)) for i in range(4)]
    st2 = [ph.enter_context(nc.sbuf_tensor(f"hstat{i}", [128, 4], F32)) for i in range(4)]
    ht2 = [ph.enter_context(nc.sbuf_tensor(f"hht{i}", [128, DM], F32)) for i in range(4)]
    psG = ph.enter_context(nc.psum_tensor("psG", [128, 1024], F32))
    psU = ph.enter_context(nc.psum_tensor("psU", [128, 1024], F32))
    psD = [ph.enter_context(nc.psum_tensor(f"psD{i}", [128, 1024], F32)) for i in range(2)]

    items = [(q, e) for q in range(4) for e in range(8)]

    def prefetch(idx):
        q, e = items[idx]
        bi = idx % 2
        gw = None
        for (wh, wsbs) in ((W["Wg"], wg_sb), (W["Wu"], wu_sb), (W["Wd"], wd_sb)):
            wsrc = AP(wh.tensor, e * DM * DM, [[DM, 128], [128 * DM, NC], [1, DM]])
            wdst = AP(wsbs[bi].ap().tensor, 0, [[NC * 1024, 128], [1024, NC], [1, DM]])
            gw = P.dma("gpsimd", lambda E, s=wsrc, d=wdst: E.dma_start(out=d, in_=s),
                       ctr=f"d_w{idx % 2}")
        return gw

    dval = {}
    dval[0] = prefetch(0)
    end_mark = {}
    for idx, (q, e) in enumerate(items):
        bi = idx % 2
        wg, wu, wd = wg_sb[bi], wu_sb[bi], wd_sb[bi]
        if e == 0:
            mq = P.marks()
            P.wait("sync", "e_tensor", mq.get("e_tensor", 0))
            lq_src = AP(ln3T_h.tensor, q * 1024, [[S, 128], [128 * S, NC], [1, 1024]])
            lq_dst = AP(lnq.ap().tensor, 0, [[NC * 1024, 128], [1024, NC], [1, 1024]])
            lq = P.dma("sync", lambda E, d=lq_dst, s=lq_src: E.dma_start(out=d, in_=s), ctr="d_lnq")
            P.wait("tensor", "d_lnq", lq)
        if idx + 1 < len(items):
            P.wait("gpsimd", "e_tensor", end_mark.get(idx - 1, 0))
            dval[idx + 1] = prefetch(idx + 1)
        P.wait("tensor", f"d_w{idx % 2}", dval[idx])
        for ic in range(NC):
            for (ps, wsb2) in ((psG, wg), (psU, wu)):
                for cch in range(NC):
                    for j in range(2):
                        tm = P.step("tensor", lambda E, c=cch, j=j, i=ic, l=lnq, w=wsb2, p=ps:
                                    E.matmul(p[:, j * 512:(j + 1) * 512],
                                             w[:, c * 1024 + i * 128:c * 1024 + (i + 1) * 128],
                                             l[:, c * 1024 + j * 512:c * 1024 + (j + 1) * 512],
                                             start=(c == 0), stop=(c == NC - 1),
                                             skip_group_check=True))
                if ps is psG:
                    tG = tm
            P.wait("scalar", "e_tensor", tG)
            sg = P.step("scalar", lambda E, p=psG, d=gsb: E.activation(d[:], p[:], AF.Silu))
            P.wait("vector", "e_scalar", sg)
            P.wait("vector", "e_tensor", tm)
            vh = P.step("vector", lambda E, g=gsb, p=psU, d=hq, i=ic:
                        E.tensor_mul(d[:, i * 1024:(i + 1) * 1024], g[:], p[:]))
            P.wait("tensor", "e_scalar", sg)
            P.wait("tensor", "e_vector", vh)
        vaccs = {}
        for tt2 in range(8):
            pD = psD[tt2 % 2]
            if tt2 >= 2:
                P.wait("tensor", "e_vector", vaccs[tt2 - 2])
            for ic in range(NC):
                for j in range(2):
                    td = P.step("tensor", lambda E, i=ic, j=j, tt=tt2, h=hq, w=wd, p=pD:
                                E.matmul(p[:, j * 512:(j + 1) * 512],
                                         h[:, i * 1024 + tt * 128:i * 1024 + (tt + 1) * 128],
                                         w[:, i * 1024 + j * 512:i * 1024 + (j + 1) * 512],
                                         start=(i == 0), stop=(i == NC - 1),
                                         skip_group_check=True))
            P.wait("vector", "e_tensor", td)
            wcol = w_sb.ap()[:, (q * 8 + tt2) * 8 + e:(q * 8 + tt2) * 8 + e + 1]
            if e == 0:
                vaccs[tt2] = P.step("vector", lambda E, p=pD, d=acc, tt=tt2, wc=wcol:
                            E.tensor_scalar_mul(d[:, tt * 1024:(tt + 1) * 1024], p[:], wc))
            else:
                vaccs[tt2] = P.step("vector", lambda E, p=pD, d=acc, tt=tt2, wc=wcol:
                            E.scalar_tensor_tensor(d[:, tt * 1024:(tt + 1) * 1024],
                                                   p[:], wc, d[:, tt * 1024:(tt + 1) * 1024],
                                                   ALU.mult, ALU.add))
        P.wait("tensor", "e_vector", vaccs[7])
        end_mark[idx] = P.last.get("e_tensor", 0)
        if e == 7:
            epar = [P.marks(), P.marks()]
            def _zbody(tt2):
                    t = q * 8 + tt2
                    par2 = tt2 % 2
                    mp = epar[par2]
                    P.wait("sync", "e_vector", mp.get("e_vector", 0))
                    P.wait("sync", "e_scalar", mp.get("e_scalar", 0))
                    xb = xt2[par2]
                    sqp2, stp2, htp2 = sq2[par2], st2[par2], ht2[par2]
                    vx = P.dma("sync", lambda E, d=xb, s=xin, t=t:
                               E.dma_start(out=d[:], in_=s[t * 128:(t + 1) * 128, :]), ctr=f"d_x{par2}")
                    P.wait("vector", f"d_x{par2}", vx)
                    P.wait("vector", f"d_o{par2}", mp.get(f"d_o{par2}", 0))
                    P.wait("scalar", f"d_x{par2}", vx)
                    P.wait("scalar", "e_vector", mp.get("e_vector", 0))
                    atile = acc.ap()[:, tt2 * 1024:(tt2 + 1) * 1024]
                    rs = ln_stats(P, atile, DM, stp2.ap(), C.eps5, sqp2.ap())
                    vo1 = ln_apply(P, atile, htp2.ap(), stp2.ap(), rs)
                    P.wait("vector", "e_vector", vo1)
                    vo2 = P.step("vector", lambda E, x=xb, h=htp2, d=htp2: E.tensor_add(d[:], x[:], h[:]))
                    P.wait("sync", "e_vector", vo2)
                    P.dma("sync", lambda E, d=xout, s=htp2, t=t:
                          E.dma_start(out=d[t * 128:(t + 1) * 128, :], in_=s[:]), ctr=f"d_o{par2}")
                    epar[par2] = P.marks()
            for _z0 in range(0, (8), 2):
                _rs = []
                for _zi in range(2):
                    P.record(); _zbody(_z0 + _zi); _rs.append(P.stop())
                P.emit_zip(*_rs)
    ph.close()
    blk.close()


def build_l1_ffn():
    nc = bass.Bass("TRN2", target_bir_lowering=False)
    es = ExitStack()
    P = Pr(nc, es)
    xin = nc.dram_tensor("xin", [S, DM], F32, kind="ExternalInput").ap()
    xout = nc.dram_tensor("xout", [S, DM], F32, kind="ExternalOutput").ap()
    W = {
        "Wg": nc.dram_tensor("w_wg", [8, DM, DM], F32, kind="ExternalInput").ap(),
        "Wu": nc.dram_tensor("w_wu", [8, DM, DM], F32, kind="ExternalInput").ap(),
        "Wd": nc.dram_tensor("w_wd", [8, DM, DM], F32, kind="ExternalInput").ap(),
        "Wr": nc.dram_tensor("w_wr", [DM, 8], F32, kind="ExternalInput").ap(),
        "temp": nc.dram_tensor("w_temp", [1], F32, kind="ExternalInput").ap(),
    }
    C = setup_common(P, need_rope=False)
    emit_l1_ffn(P, C, xin, xout, W)
    for n in [k for k in P.last if k.startswith("d_o")]:
        P.wait("sync", n, P.last[n])
    P.finish(nc)
    es.close()
    return nc


def l1_ffn_inmap(inputs, xin, hc):
    return {
        "xin": np.ascontiguousarray(xin),
        "w_wg": inputs["L1_ffn_Wg"], "w_wu": inputs["L1_ffn_Wu"],
        "w_wd": inputs["L1_ffn_Wd"], "w_wr": inputs["L1_ffn_Wr"],
        "w_temp": inputs["L1_ffn_temperature"],
        "c_ident": hc["ident"], "c_ones": hc["ones_row_f32"],
    }


def emit_l1_attn(P, C, xin, xout, W):
    nc = P.nc
    scale = 1.0 / 32.0

    QT_h = nc.dram_tensor("l1_qt", [DM, S], BF16).ap()
    KT_h = nc.dram_tensor("l1_kt", [DM, S], BF16).ap()
    V_h = nc.dram_tensor("l1_v", [8, S, 130], BF16).ap()
    D_h = nc.dram_tensor("l1_dflat", [2304 * 16], BF16).ap()
    stage_h = nc.dram_tensor("l1_stage", [S, 64], F32).ap()

    blk = ExitStack()
    attno = blk.enter_context(nc.sbuf_tensor("a1_attno", [128, NT * DM], BF16))
    tv_col = blk.enter_context(nc.sbuf_tensor("a1_tv", [128, 1], BF16))
    antid_p = blk.enter_context(nc.sbuf_tensor("a1_antidp", [128, 128], BF16))
    u128 = blk.enter_context(nc.sbuf_tensor("a1_u128", [128, 64], F32))

    dz = ExitStack()
    zb = dz.enter_context(nc.sbuf_tensor("a1_zb", [128, 288], BF16))
    tload = dz.enter_context(nc.sbuf_tensor("a1_tload", [128, 16], F32))
    t2r = dz.enter_context(nc.sbuf_tensor("a1_t2r", [1, 128], F32))
    t2rep = dz.enter_context(nc.sbuf_tensor("a1_t2rep", [128, 16], F32))
    dt_bf = dz.enter_context(nc.sbuf_tensor("a1_dtbf", [128, 16], BF16))
    psX = dz.enter_context(nc.psum_tensor("a1_psX", [128, 512], F32))

    la = P.dma("sync", lambda E, d=antid_p, s=C.antid_h: E.dma_start(out=d[:], in_=s), ctr="d_pre")
    vz = P.step("vector", lambda E, d=zb: E.memset(d[:], 0.0))
    P.wait("sync", "e_vector", vz)
    zdst = AP(D_h.tensor, 0, [[288, 128], [1, 288]])
    P.dma("sync", lambda E, d=zdst, s=zb: E.dma_start(out=d, in_=s.ap()), ctr="d_db")
    l2 = P.dma("sync", lambda E, d=t2r, s=AP(W["rel"].tensor, 2046 * 16, [[16, 1], [1, 16]]):
               E.dma_start(out=d.ap()[0:1, 0:16], in_=s), ctr="d_pre")
    P.wait("vector", "d_pre", l2)
    lastv = None
    for i in range(1, 8):
        lastv = P.step("vector", lambda E, d=t2r, i=i:
                       E.tensor_copy(d[0:1, i * 16:(i + 1) * 16], d[0:1, 0:16]))
        P.wait("vector", "e_vector", lastv)
    P.wait("tensor", "e_vector", lastv)
    P.wait("tensor", "d_pre", la)
    tbc = P.step("tensor", lambda E, C=C, s=t2r, p=psX: E.matmul(
        p[:, 0:16], C.ones_row[:], s[0:1, 0:16], start=True, stop=True, skip_group_check=True))
    P.wait("vector", "e_tensor", tbc)
    v = P.step("vector", lambda E, p=psX, d=t2rep: E.tensor_copy(d[:], p[:, 0:16]))
    P.wait("vector", "e_vector", v)
    for tile in range(16):
        nrows = 128 if tile < 15 else 127
        P.wait("sync", "e_vector", P.last.get("e_vector", 0))
        tsrc = AP(W["rel"].tensor, tile * 128 * 16, [[16, nrows], [1, 16]])
        lt2 = P.dma("sync", lambda E, d=tload, s=tsrc, n=nrows:
                    E.dma_start(out=d.ap()[0:n, :], in_=s), ctr="d_pre")
        P.wait("vector", "d_pre", lt2)
        vsub = P.step("vector", lambda E, a=tload, b=t2rep, d=dt_bf: E.tensor_sub(d[:], a[:], b[:]))
        P.wait("sync", "e_vector", vsub)
        ddst2 = AP(D_h.tensor, tile * 128 * 16, [[16, nrows], [1, 16]])
        P.dma("sync", lambda E, d=ddst2, s=dt_bf, n=nrows:
              E.dma_start(out=d, in_=s.ap()[0:n, :]), ctr="d_db")
    P.wait("tensor", "e_vector", P.last.get("e_vector", 0))
    tvm = P.step("tensor", lambda E, s=t2r, o=C.ones_row, p=psX: E.matmul(
        p[:, 300:301], s[0:1, 0:128], o[0:1, 0:1], start=True, stop=True, skip_group_check=True))
    P.wait("vector", "e_tensor", tvm)
    v = P.step("vector", lambda E, p=psX, d=tv_col: E.tensor_copy(d[:], p[:, 300:301]))
    P.wait("vector", "e_vector", v)
    dz.close()

    pj = ExitStack()
    ln2T = pj.enter_context(nc.sbuf_tensor("a1_ln2t", [128, NC * S], BF16))
    wq_sb = pj.enter_context(nc.sbuf_tensor("a1_wq", [128, NC * DM], BF16))
    wk_sb = pj.enter_context(nc.sbuf_tensor("a1_wk", [128, NC * DM], BF16))
    wv_sb = pj.enter_context(nc.sbuf_tensor("a1_wv", [128, NC * DM], BF16))
    xt = [pj.enter_context(nc.sbuf_tensor(f"a1_xt{i}", [128, DM], F32)) for i in range(2)]
    lnb = [pj.enter_context(nc.sbuf_tensor(f"a1_lnb{i}", [128, DM], BF16)) for i in range(2)]
    sq_tmp = [pj.enter_context(nc.sbuf_tensor(f"a1_sq{i}", [128, 24], F32)) for i in range(2)]
    stat = [pj.enter_context(nc.sbuf_tensor(f"a1_stat{i}", [128, 4], F32)) for i in range(2)]
    qtt = pj.enter_context(nc.sbuf_tensor("a1_qtt", [128, 512], BF16))
    vtt = pj.enter_context(nc.sbuf_tensor("a1_vtt", [128, 8 * 130], BF16))
    psT = [pj.enter_context(nc.psum_tensor(f"a1_psT{i}", [128, 1024], BF16)) for i in range(2)]
    psQ = pj.enter_context(nc.psum_tensor("a1_psQ", [128, 512], F32))
    psV = pj.enter_context(nc.psum_tensor("a1_psV", [128, 1024], F32))

    for (wh, wsb) in ((W["Wq"], wq_sb), (W["Wk"], wk_sb), (W["Wv"], wv_sb)):
        wsrc = AP(wh.tensor, 0, [[DM, 128], [128 * DM, NC], [1, DM]])
        wdst = AP(wsb.ap().tensor, 0, [[NC * DM, 128], [DM, NC], [1, DM]])
        gw = P.dma("gpsimd", lambda E, s=wsrc, d=wdst: E.dma_start(out=d, in_=s))

    endm1 = [P.marks(), P.marks()]
    def _zbody(t):
            par = t % 2
            pm2 = endm1[par]
            P.wait("sync", "e_vector", pm2.get("e_vector", 0))
            P.wait("sync", "e_scalar", pm2.get("e_scalar", 0))
            xb, lb2, sqp, stp, pT = xt[par], lnb[par], sq_tmp[par], stat[par], psT[par]
            vx = P.dma("sync", lambda E, d=xb, s=xin, t=t:
                       E.dma_start(out=d[:], in_=s[t * 128:(t + 1) * 128, :]), ctr=f"d_x{par}")
            P.wait("vector", f"d_x{par}", vx)
            P.wait("scalar", f"d_x{par}", vx)
            P.wait("vector", "e_tensor", pm2.get("e_tensor", 0))
            P.wait("scalar", "e_vector", pm2.get("e_vector", 0))
            rs = ln_stats(P, xb.ap(), DM, stp.ap(), C.eps5, sqp.ap())
            vln = ln_apply(P, xb.ap(), lb2.ap(), stp.ap(), rs)
            P.wait("tensor", "e_vector", vln)
            for cch in range(NC):
                tt = P.step("tensor", lambda E, C=C, c=cch, s=lb2, p=pT:
                            E.transpose(p[:, c * 128:(c + 1) * 128],
                                        s[:, c * 128:(c + 1) * 128], C.ident[:]))
            P.wait("vector", "e_tensor", tt)
            ldst = AP(ln2T.ap().tensor, t * 128, [[NC * S, 128], [S, NC], [1, 128]])
            lsrc = AP(pT.ap().tensor, 0, [[1024, 128], [128, NC], [1, 128]])
            vT = P.step("vector", lambda E, d=ldst, s=lsrc: E.tensor_copy(d, s))
            endm1[par] = P.marks()
    for _z0 in range(0, (NT), 2):
        _rs = []
        for _zi in range(2):
            P.record(); _zbody(_z0 + _zi); _rs.append(P.stop())
        P.emit_zip(*_rs)
    P.barrier()
    P.wait("tensor", "d_wl1", P.last.get("d_wl1", 0))

    prev_evac = 0
    for (wsb, hbm) in ((wq_sb, QT_h), (wk_sb, KT_h)):
        for ft in range(NC):
            for tc in range(8):
                P.wait("tensor", "e_vector", prev_evac)
                for cch in range(NC):
                    tq = P.step("tensor", lambda E, c=cch, f=ft, tc=tc, w=wsb, l=ln2T, p=psQ:
                                E.matmul(p[:], w[:, c * DM + f * 128:c * DM + (f + 1) * 128],
                                         l[:, c * S + tc * 512:c * S + (tc + 1) * 512],
                                         start=(c == 0), stop=(c == NC - 1),
                                         skip_group_check=True))
                P.wait("vector", "e_tensor", tq)
                P.wait("vector", "d_ps", P.last.get("d_ps", 0))
                prev_evac = P.step("vector", lambda E, p=psQ, d=qtt: E.tensor_copy(d[:], p[:]))
                P.wait("sync", "e_vector", prev_evac)
                P.dma("sync", lambda E, d=hbm, s=qtt, f=ft, tc=tc:
                      E.dma_start(out=d[f * 128:(f + 1) * 128, tc * 512:(tc + 1) * 512], in_=s[:]),
                      ctr="d_ps")
    vone = P.step("vector", lambda E, d=AP(vtt.ap().tensor, 64, [[8 * 130, 128], [130, 8], [65, 2]]):
                  E.memset(d, 1.0))
    prev_vv = 0
    for t in range(NT):
        P.wait("tensor", "e_vector", [prev_vv, vone])
        for cch in range(NC):
            for j in range(2):
                tv2 = P.step("tensor", lambda E, c=cch, j=j, t=t, w=wv_sb, l=ln2T, p=psV:
                             E.matmul(p[:, j * 512:(j + 1) * 512],
                                      l[:, c * S + t * 128:c * S + (t + 1) * 128],
                                      w[:, c * DM + j * 512:c * DM + (j + 1) * 512],
                                      start=(c == 0), stop=(c == NC - 1),
                                      skip_group_check=True))
        P.wait("vector", "e_tensor", tv2)
        P.wait("vector", "d_ps", P.last.get("d_ps", 0))
        vsrcp = AP(psV.ap().tensor, 0, [[1024, 128], [128, 8], [64, 2], [1, 64]])
        vdstp = AP(vtt.ap().tensor, 0, [[8 * 130, 128], [130, 8], [65, 2], [1, 64]])
        prev_vv = P.step("vector", lambda E, sp=vsrcp, dp=vdstp: E.tensor_copy(dp, sp))
        P.wait("sync", "e_vector", prev_vv)
        vsrc = AP(vtt.ap().tensor, 0, [[8 * 130, 128], [130, 8], [1, 130]])
        vdst = AP(V_h.tensor, t * 128 * 130, [[130, 128], [S * 130, 8], [1, 130]])
        P.dma("sync", lambda E, d=vdst, s=vsrc: E.dma_start(out=d, in_=s), ctr="d_ps")
    pj.close()
    P.barrier()

    ph = ExitStack()
    KTp = ph.enter_context(nc.sbuf_tensor("a1_ktp", [128, S], BF16))
    QTp = ph.enter_context(nc.sbuf_tensor("a1_qtp", [128, S], BF16))
    Vp = ph.enter_context(nc.sbuf_tensor("a1_vp", [128, NT * 130], BF16))
    sqb = ph.enter_context(nc.sbuf_tensor("a1_sqb", [128, S], BF16))
    Pt = [[ph.enter_context(nc.sbuf_tensor(f"a1_pt{hh}_{i}", [128, 1024], BF16))
           for i in range(2)] for hh in range(2)]
    aT = [ph.enter_context(nc.sbuf_tensor(f"a1_aT{hh}", [65, S], BF16)) for hh in range(2)]
    rcol8 = ph.enter_context(nc.sbuf_tensor("a1_rcol8", [128, 8], F32))
    negc2 = ph.enter_context(nc.sbuf_tensor("a1_negc", [128, 2], F32))
    grow = ph.enter_context(nc.sbuf_tensor("a1_grow", [1, 8], F32))
    u_row = ph.enter_context(nc.sbuf_tensor("a1_urow", [1, 65], F32))
    ddst = [ph.enter_context(nc.sbuf_tensor(f"a1_ddst{i}", [128, S], BF16)) for i in range(2)]
    dT_sb = ph.enter_context(nc.sbuf_tensor("a1_dT", [128, S], BF16))
    dout = ph.enter_context(nc.sbuf_tensor("a1_dout", [128, 64], F32))
    ufill = ph.enter_context(nc.sbuf_tensor("a1_ufill", [128, 2048], F32))
    stg = ph.enter_context(nc.sbuf_tensor("a1_stg", [128, 2048], F32))

    for pr in range(8):
        m = P.marks()
        for e in ("e_tensor", "e_vector", "e_scalar"):
            P.wait("sync", e, m.get(e, 0))
        dprc = f"d_pr{pr % 2}"
        P.dma("sync", lambda E, d=KTp, s=KT_h, pr=pr:
              E.dma_start(out=d[:], in_=s[pr * 128:(pr + 1) * 128, :]), ctr=dprc)
        P.dma("sync", lambda E, d=QTp, s=QT_h, pr=pr:
              E.dma_start(out=d[:], in_=s[pr * 128:(pr + 1) * 128, :]), ctr=dprc)
        vpd = AP(Vp.ap().tensor, 0, [[NT * 130, 128], [130, NT], [1, 130]])
        vps = AP(V_h.tensor, pr * S * 130, [[130, 128], [130 * 128, NT], [1, 130]])
        wv2 = P.dma("sync", lambda E, d=vpd, s=vps: E.dma_start(out=d, in_=s), ctr=dprc)
        P.wait("tensor", dprc, wv2)
        P.wait("scalar", dprc, wv2)
        P.wait("vector", dprc, wv2)
        st_s = ExitStack()
        psN = st_s.enter_context(nc.psum_tensor(f"a1_psN{pr}", [128, 512], F32))
        for hh in range(2):
            hb = hh * 64
            for (tens, col) in ((QTp, 0), (KTp, 1)):
                P.wait("vector", "e_tensor", P.last.get("e_tensor", 0))
                vq = P.step("vector", lambda E, a=tens, d=sqb: E.tensor_mul(d[:], a[:], a[:]))
                P.wait("tensor", "e_vector", vq)
                gmax = None
                for tc in range(8):
                    tn = P.step("tensor", lambda E, C=C, s=sqb, p=psN, hb=hb, tc=tc:
                                E.matmul(p[0:1, 0:512], C.ones_col[hb:hb + 64, 0:1],
                                         s[hb:hb + 64, tc * 512:(tc + 1) * 512],
                                         start=True, stop=True, skip_group_check=True))
                    P.wait("vector", "e_tensor", tn)
                    vm = P.step("vector", lambda E, p=psN, d=rcol8, col=col:
                                E.reduce_max(d[0:1, col:col + 1], p[0:1, 0:512], axis=AX))
                    P.wait("vector", "e_vector", vm)
                    if tc == 0:
                        gmax = P.step("vector", lambda E, d=grow, r=rcol8, col=col:
                                      E.tensor_copy(d[0:1, col:col + 1], r[0:1, col:col + 1]))
                    else:
                        gmax = P.step("vector", lambda E, d=grow, r=rcol8, col=col:
                                      E.tensor_max(d[0:1, col:col + 1], d[0:1, col:col + 1],
                                                   r[0:1, col:col + 1]))
                    P.wait("vector", "e_vector", gmax)
                    P.wait("tensor", "e_vector", gmax)
            P.wait("scalar", "e_vector", gmax)
            sl = P.step("scalar", lambda E, d=grow: E.activation(d[0:1, 2:4], d[0:1, 0:2], AF.Ln))
            P.wait("vector", "e_scalar", sl)
            vadd = P.step("vector", lambda E, d=grow: E.tensor_add(d[0:1, 4:5], d[0:1, 2:3], d[0:1, 3:4]))
            P.wait("scalar", "e_vector", vadd)
            se = P.step("scalar", lambda E, d=grow, b=C.lnsc32: E.activation(
                d[0:1, 5:6], d[0:1, 4:5], AF.Exp, scale=0.5, bias=b))
            P.wait("vector", "e_scalar", se)
            vneg = P.step("vector", lambda E, d=grow: E.tensor_scalar_mul(d[0:1, 6:7], d[0:1, 5:6], -1.0))
            P.wait("tensor", "e_vector", vneg)
            tb2 = P.step("tensor", lambda E, C=C, s=grow, p=psN, hh=hh: E.matmul(
                p[:, 0:1], C.ones_row[:], s[0:1, 6:7], start=True, stop=True, skip_group_check=True))
            P.wait("vector", "e_tensor", tb2)
            vnc = P.step("vector", lambda E, p=psN, d=negc2, hh=hh:
                         E.tensor_copy(d[:, hh:hh + 1], p[:, 0:1]))
            P.wait("scalar", "e_vector", vnc)
            P.wait("tensor", "e_vector", vnc)
        st_s.close()
        qs = ExitStack()
        psS = [qs.enter_context(nc.psum_tensor(f"a1_psS{pr}_{hh}", [128, 1024], F32)) for hh in range(2)]
        psPV = [qs.enter_context(nc.psum_tensor(f"a1_psPV{pr}_{hh}", [128, 1024], F32)) for hh in range(2)]
        for qq in range(4):
            sE = {}
            tS = {}
            for kc in range(NT):
                for hh in range(2):
                    hb = hh * 64
                    if kc >= 1:
                        P.wait("tensor", "e_scalar", sE[(kc - 1, hh)])
                    for j in range(2):
                        tS[(kc, hh)] = P.step("tensor",
                                    lambda E, k=KTp, q=QTp, p=psS[hh], kc=kc, qq=qq, j=j, hb=hb:
                                    E.matmul(p[:, j * 512:(j + 1) * 512],
                                             k[hb:hb + 64, kc * 128:(kc + 1) * 128],
                                             q[hb:hb + 64, qq * 1024 + j * 512:qq * 1024 + (j + 1) * 512],
                                             start=True, stop=True, skip_group_check=True))
                for hh in range(2):
                    P.wait("scalar", "e_tensor", tS[(kc, hh)])
                    sE[(kc, hh)] = P.step("scalar",
                                lambda E, p=psS[hh], d=Pt[hh][kc % 2], nb=negc2, hh=hh, sc=scale:
                                E.activation(d[:], p[:], AF.Exp, bias=nb[:, hh:hh + 1], scale=sc))
                if kc >= 1:
                    for hh in range(2):
                        P.wait("tensor", "e_scalar", sE[(kc - 1, hh)])
                        for j in range(2):
                            tP = P.step("tensor",
                                        lambda E, v2=Vp, d=Pt[hh][(kc - 1) % 2], p=psPV[hh],
                                        kc=kc - 1, j=j, hh=hh:
                                        E.matmul(p[0:65, j * 512:(j + 1) * 512],
                                                 v2[:, kc * 130 + hh * 65:kc * 130 + hh * 65 + 65],
                                                 d[:, j * 512:(j + 1) * 512],
                                                 start=(kc == 0), stop=False,
                                                 skip_group_check=True))
            for hh in range(2):
                P.wait("tensor", "e_scalar", sE[(NT - 1, hh)])
                for j in range(2):
                    tP = P.step("tensor",
                                lambda E, v2=Vp, d=Pt[hh][(NT - 1) % 2], p=psPV[hh],
                                kc=NT - 1, j=j, hh=hh:
                                E.matmul(p[0:65, j * 512:(j + 1) * 512],
                                         v2[:, kc * 130 + hh * 65:kc * 130 + hh * 65 + 65],
                                         d[:, j * 512:(j + 1) * 512],
                                         start=False, stop=True, skip_group_check=True))
            P.wait("vector", "e_tensor", tP)
            for hh in range(2):
                vA = P.step("vector", lambda E, p=psPV[hh], d=aT[hh], qq=qq:
                            E.tensor_copy(d[:, qq * 1024:(qq + 1) * 1024], p[0:65, :]))
            P.wait("tensor", "e_vector", vA)
        qs.close()
        es2 = ExitStack()
        psM = [es2.enter_context(nc.psum_tensor(f"a1_psM{pr}_{i}", [128, 1024], BF16)) for i in range(2)]
        psDel = es2.enter_context(nc.psum_tensor(f"a1_psDel{pr}", [128, 512], F32))
        for hh in range(2):
            h = pr * 2 + hh
            P.wait("tensor", "e_vector", P.last.get("e_vector", 0))
            for kc in range(NT):
                tu = P.step("tensor", lambda E, tv=tv_col, v2=Vp, p=psDel, kc=kc, hh=hh:
                            E.matmul(p[0:1, 64:129], tv[:],
                                     v2[:, kc * 130 + hh * 65:kc * 130 + hh * 65 + 65],
                                     start=(kc == 0), stop=(kc == NT - 1), skip_group_check=True))
            P.wait("vector", "e_tensor", tu)
            vu = P.step("vector", lambda E, p=psDel, d=u_row: E.tensor_copy(d[0:1, :], p[0:1, 64:129]))
            P.wait("tensor", "e_vector", vu)
            tu2 = P.step("tensor", lambda E, C=C, s=u_row, p=psDel: E.matmul(
                p[:, 0:64], C.ones_row[:], s[0:1, 0:64], start=True, stop=True, skip_group_check=True))
            P.wait("vector", "e_tensor", tu2)
            vu2 = P.step("vector", lambda E, p=psDel, d=u128: E.tensor_copy(d[:], p[:, 0:64]))
            P.wait("vector", "e_vector", vu2)
            if h >= 8:
                lastu = vu2
                for i in range(32):
                    lastu = P.step("vector", lambda E, d=ufill, s=u128, i=i:
                                   E.tensor_copy(d[:, i * 64:(i + 1) * 64], s[:]))
                    P.wait("vector", "e_vector", lastu)
                P.wait("sync", "e_vector", lastu)
                P.wait("sync", "d_st", P.last.get("d_st", 0))
                uf2_src = AP(ufill.ap().tensor, 0, [[2048, 128], [64, 32], [1, 64]])
                uf_dst = AP(stage_h.tensor, 0, [[64, 128], [8192, 32], [1, 64]])
                ufd = P.dma("sync", lambda E, d=uf_dst, s=uf2_src: E.dma_start(out=d, in_=s), ctr="d_st")
                P.wait("sync", "d_st", ufd)
                classes = [(A, q16h) for A in range(0, h - 8 + 1) for q16h in range(2)]
                dmark = {}
                def dgather(ci):
                    A, q16h = classes[ci]
                    s0_base = (A - h) * 256 + 4095 - 128 * q16h
                    base = (s0_base - 127) * 16
                    gsrc = AP(D_h.tensor, base, [[16, 128], [1, 4096]])
                    P.wait("sync", "e_tensor", dmark.get(ci - 2, 0))
                    return P.dma("sync", lambda E, d=ddst[ci % 2], s=gsrc:
                                 E.dma_start(out=d[:], in_=s), ctr=f"d_dg{ci % 2}")
                gval = {0: dgather(0)}
                for ci in range(len(classes)):
                    A, q16h = classes[ci]
                    if ci + 1 < len(classes):
                        gval[ci + 1] = dgather(ci + 1)
                    dd = ddst[ci % 2]
                    P.wait("tensor", f"d_dg{ci % 2}", gval[ci])
                    prev_dv = None
                    for bq in range(8):
                        pmx = psM[bq % 2]
                        if bq >= 2:
                            P.wait("tensor", "e_vector", prev_dv)
                        for j4 in range(4):
                            tdt = P.step("tensor", lambda E, s=dd, p=pmx, bq=bq, j4=j4, aj=antid_p:
                                         E.transpose(p[:, j4 * 128:(j4 + 1) * 128],
                                                     s[:, (bq * 4 + j4) * 128:(bq * 4 + j4 + 1) * 128],
                                                     aj[:]))
                        P.wait("vector", "e_tensor", tdt)
                        prev_dv = P.step("vector", lambda E, p=pmx, d=dT_sb, bq=bq:
                                         E.tensor_copy(d[:, bq * 512:(bq + 1) * 512], p[:, 0:512]))
                    P.wait("tensor", "e_vector", prev_dv)
                    for kc in range(NT):
                        tdm = P.step("tensor", lambda E, dt=dT_sb, v2=Vp, p=psDel, kc=kc, hh=hh:
                                     E.matmul(p[:, 0:64], dt[:, kc * 128:(kc + 1) * 128],
                                              v2[:, kc * 130 + hh * 65:kc * 130 + hh * 65 + 64],
                                              start=(kc == 0), stop=(kc == NT - 1),
                                              skip_group_check=True))
                    P.wait("vector", "e_tensor", tdm)
                    P.wait("vector", "d_st", P.last.get("d_st", 0))
                    vd = P.step("vector", lambda E, p=psDel, u=u128, d=dout:
                                E.tensor_add(d[:], u[:], p[:, 0:64]))
                    P.wait("sync", "e_vector", vd)
                    sc_dst = AP(stage_h.tensor, (128 * q16h * 16 + A) * 64,
                                [[1024, 128], [1, 64]])
                    P.dma("sync", lambda E, d=sc_dst, s=dout: E.dma_start(out=d, in_=s[:]), ctr="d_st")
                    P.wait("tensor", "e_vector", vd)
                    dmark[ci] = P.last.get("e_tensor", 0)
                P.wait("sync", "e_vector", P.last.get("e_vector", 0))
                P.wait("sync", "d_st", P.last.get("d_st", 0))
                sg_src = AP(stage_h.tensor, 0, [[64, 128], [8192, 32], [1, 64]])
                sg_dst = AP(stg.ap().tensor, 0, [[2048, 128], [64, 32], [1, 64]])
                sgd = P.dma("sync", lambda E, d=sg_dst, s=sg_src: E.dma_start(out=d, in_=s), ctr="d_st")
                P.wait("vector", "d_st", sgd)
            vg = {}
            P.wait("tensor", "e_vector", P.last.get("e_vector", 0))
            for g in range(4):
                pq = psM[g % 2]
                if g >= 2:
                    P.wait("tensor", "e_vector", vg[g - 2])
                for j8 in range(8):
                    last_tT = P.step("tensor", lambda E, C=C, s=aT[hh], p=pq, g=g, j8=j8:
                                     E.transpose(p[:, j8 * 66:j8 * 66 + 65],
                                                 s[0:65, (g * 8 + j8) * 128:(g * 8 + j8 + 1) * 128],
                                                 C.ident[0:65, 0:65]))
                P.wait("vector", "e_tensor", last_tT)
                den_src = AP(pq.ap().tensor, 64, [[1024, 128], [66, 8]])
                v1 = P.step("vector", lambda E, s=den_src, d=rcol8:
                            E.tensor_single_scalar(d[:], s, 1e-38, ALU.add))
                P.wait("vector", "e_vector", v1)
                v2b = P.step("vector", lambda E, d=rcol8: E.reciprocal(d[:], d[:]))
                P.wait("vector", "e_vector", v2b)
                for j8 in range(8):
                    qt = g * 8 + j8
                    badd = (stg.ap()[:, qt * 64:(qt + 1) * 64] if h >= 8 else u128.ap())
                    vg[g] = P.step("vector", lambda E, p=pq, r=rcol8, b=badd, d=attno, qt=qt, h=h, j8=j8:
                                E.scalar_tensor_tensor(d[:, qt * DM + h * 64:qt * DM + (h + 1) * 64],
                                                       p[:, j8 * 66:j8 * 66 + 64], r[:, j8:j8 + 1], b,
                                                       ALU.mult, ALU.add))
        es2.close()
    ph.close()
    P.barrier()

    fz = ExitStack()
    grep = fz.enter_context(nc.sbuf_tensor("a1_grep", [128, DM], F32))
    brep = fz.enter_context(nc.sbuf_tensor("a1_brep", [128, DM], F32))
    growf = fz.enter_context(nc.sbuf_tensor("a1_growf", [1, DM], F32))
    browf = fz.enter_context(nc.sbuf_tensor("a1_browf", [1, DM], F32))
    xb2 = [fz.enter_context(nc.sbuf_tensor(f"a1_xb2{i}", [128, DM], F32)) for i in range(4)]
    av32 = [fz.enter_context(nc.sbuf_tensor(f"a1_av32{i}", [128, DM], F32)) for i in range(4)]
    ht = [fz.enter_context(nc.sbuf_tensor(f"a1_ht{i}", [128, DM], F32)) for i in range(4)]
    yt = [fz.enter_context(nc.sbuf_tensor(f"a1_yt{i}", [128, DM], F32)) for i in range(4)]
    sq2 = [fz.enter_context(nc.sbuf_tensor(f"a1_sq2{i}", [128, 24], F32)) for i in range(4)]
    st2 = [fz.enter_context(nc.sbuf_tensor(f"a1_st2{i}", [128, 4], F32)) for i in range(4)]
    psBC = fz.enter_context(nc.psum_tensor("a1_psBC", [128, 512], F32))

    P.dma("sync", lambda E, d=growf, s=row_ap(W["gamma"], DM): E.dma_start(out=d[:], in_=s), ctr="d_pre")
    l3 = P.dma("sync", lambda E, d=browf, s=row_ap(W["beta"], DM): E.dma_start(out=d[:], in_=s), ctr="d_pre")
    P.wait("tensor", "d_pre", l3)
    v1b = bcast_row(P, C, growf.ap(), DM, grep, psBC)
    vbc = bcast_row(P, C, browf.ap(), DM, brep, psBC, after_v=v1b)

    endmf = [P.marks() for _ in range(4)]
    def _zbody(t):
            par = t % 4
            pm = endmf[par]
            xbp, avp, htp, ytp, sqp, stp = (xb2[par], av32[par], ht[par], yt[par],
                                            sq2[par], st2[par])
            P.wait("sync", "e_vector", pm.get("e_vector", 0))
            P.wait("sync", "e_scalar", pm.get("e_scalar", 0))
            P.wait("scalar", "e_vector", pm.get("e_vector", 0))
            vx = P.dma("sync", lambda E, d=xbp, s=xin, t=t:
                       E.dma_start(out=d[:], in_=s[t * 128:(t + 1) * 128, :]), ctr=f"d_x{par}")
            P.wait("vector", f"d_x{par}", vx)
            P.wait("scalar", f"d_x{par}", vx)
            P.wait("vector", "e_vector", vbc)
            P.wait("vector", f"d_o{par}", pm.get(f"d_o{par}", 0))
            va = P.step("vector", lambda E, s=attno, d=avp, t=t:
                        E.tensor_copy(d[:], s[:, t * DM:(t + 1) * DM]))
            P.wait("vector", "e_vector", va)
            P.wait("scalar", "e_vector", va)
            rs2 = ln_stats(P, avp.ap(), DM, stp.ap(), C.eps3, sqp.ap())
            P.wait("vector", "e_scalar", rs2)
            vy = P.step("vector", lambda E, hsrc=avp, g=grep, st=stp, d=ytp:
                        E.scalar_tensor_tensor(d[:], hsrc[:], st[:, 2:3], g[:], ALU.subtract, ALU.mult))
            P.wait("vector", "e_vector", vy)
            vy2 = P.step("vector", lambda E, d=ytp, st=stp: E.tensor_scalar_mul(d[:], d[:], st[:, 3:4]))
            P.wait("vector", "e_vector", vy2)
            vy3 = P.step("vector", lambda E, d=ytp, b=brep: E.tensor_add(d[:], d[:], b[:]))
            P.wait("vector", "e_vector", vy3)
            P.wait("scalar", "e_vector", vy3)
            rs3 = ln_stats(P, ytp.ap(), DM, stp.ap(), C.eps5, sqp.ap())
            vo1 = ln_apply(P, ytp.ap(), htp.ap(), stp.ap(), rs3)
            P.wait("vector", "e_vector", vo1)
            vo2 = P.step("vector", lambda E, x=xbp, hsrc=htp, d=htp: E.tensor_add(d[:], x[:], hsrc[:]))
            P.wait("sync", "e_vector", vo2)
            P.dma("sync", lambda E, d=xout, s=htp, t=t:
                  E.dma_start(out=d[t * 128:(t + 1) * 128, :], in_=s[:]), ctr=f"d_o{par}")
            endmf[par] = P.marks()
    for _z0 in range(0, (NT), 2):
        _rs = []
        for _zi in range(2):
            P.record(); _zbody(_z0 + _zi); _rs.append(P.stop())
        P.emit_zip(*_rs)
    fz.close()
    blk.close()


def build_l1_attn():
    nc = bass.Bass("TRN2", target_bir_lowering=False)
    es = ExitStack()
    P = Pr(nc, es)
    xin = nc.dram_tensor("xin", [S, DM], F32, kind="ExternalInput").ap()
    xout = nc.dram_tensor("xout", [S, DM], F32, kind="ExternalOutput").ap()
    W = {
        "Wq": nc.dram_tensor("w_wq", [DM, DM], F32, kind="ExternalInput").ap(),
        "Wk": nc.dram_tensor("w_wk", [DM, DM], F32, kind="ExternalInput").ap(),
        "Wv": nc.dram_tensor("w_wv", [DM, DM], F32, kind="ExternalInput").ap(),
        "rel": nc.dram_tensor("w_rel", [2047 * 16], F32, kind="ExternalInput").ap(),
        "gamma": nc.dram_tensor("w_gamma", [DM], F32, kind="ExternalInput").ap(),
        "beta": nc.dram_tensor("w_beta", [DM], F32, kind="ExternalInput").ap(),
    }
    C = setup_common(P, need_rope=False, need_antid=True)
    emit_l1_attn(P, C, xin, xout, W)
    for n in [k for k in P.last if k.startswith("d_o")]:
        P.wait("sync", n, P.last[n])
    P.finish(nc)
    es.close()
    return nc


def l1_attn_inmap(inputs, xin, hc):
    return {
        "xin": np.ascontiguousarray(xin),
        "w_wq": inputs["L1_attn_Wq"], "w_wk": inputs["L1_attn_Wk"],
        "w_wv": inputs["L1_attn_Wv"],
        "w_rel": np.ascontiguousarray(inputs["L1_attn_rel_table"]).reshape(-1),
        "w_gamma": inputs["L1_attn_gamma"], "w_beta": inputs["L1_attn_beta"],
        "c_ident": hc["ident"], "c_ones": hc["ones_row_f32"], "c_antid": hc["antid"],
    }


DEBUG_X = False

def build_fused():
    nc = bass.Bass("TRN2", target_bir_lowering=False)
    es = ExitStack()
    P = Pr(nc, es)
    kx = "ExternalOutput" if DEBUG_X else "Internal"
    xin = nc.dram_tensor("xin", [S, DM], F32, kind="ExternalInput").ap()
    xout = nc.dram_tensor("xout", [S, DM], F32, kind="ExternalOutput").ap()
    x1 = nc.dram_tensor("x1s", [S, DM], F32, kind=kx).ap()
    x2 = nc.dram_tensor("x2s", [S, DM], F32, kind=kx).ap()
    x3 = nc.dram_tensor("x3s", [S, DM], F32, kind=kx).ap()
    W1 = {
        "Wqkv": nc.dram_tensor("w_qkv", [DM, 3 * DM], F32, kind="ExternalInput").ap(),
        "bqkv": nc.dram_tensor("w_bqkv", [3 * DM], F32, kind="ExternalInput").ap(),
        "Wo": nc.dram_tensor("w_wo", [DM, DM], F32, kind="ExternalInput").ap(),
        "bo": nc.dram_tensor("w_bo", [DM], F32, kind="ExternalInput").ap(),
        "gamma": nc.dram_tensor("w_gamma0", [DM], F32, kind="ExternalInput").ap(),
        "beta": nc.dram_tensor("w_beta0", [DM], F32, kind="ExternalInput").ap(),
    }
    W2 = {
        "g1xr": nc.dram_tensor("w_g1xr", [4, 512, 512], F32, kind="ExternalInput").ap(),
        "g2xr": nc.dram_tensor("w_g2xr", [4, 512, 512], F32, kind="ExternalInput").ap(),
        "g1xi": nc.dram_tensor("w_g1xi", [4, 512, 512], F32, kind="ExternalInput").ap(),
        "g2xi": nc.dram_tensor("w_g2xi", [4, 512, 512], F32, kind="ExternalInput").ap(),
        "bias_real": nc.dram_tensor("w_br", [512], F32, kind="ExternalInput").ap(),
        "bias_imag": nc.dram_tensor("w_bi", [512], F32, kind="ExternalInput").ap(),
    }
    W3 = {
        "Wq": nc.dram_tensor("w_wq1", [DM, DM], F32, kind="ExternalInput").ap(),
        "Wk": nc.dram_tensor("w_wk1", [DM, DM], F32, kind="ExternalInput").ap(),
        "Wv": nc.dram_tensor("w_wv1", [DM, DM], F32, kind="ExternalInput").ap(),
        "rel": nc.dram_tensor("w_rel", [2047 * 16], F32, kind="ExternalInput").ap(),
        "gamma": nc.dram_tensor("w_gamma1", [DM], F32, kind="ExternalInput").ap(),
        "beta": nc.dram_tensor("w_beta1", [DM], F32, kind="ExternalInput").ap(),
    }
    W4 = {
        "Wg": nc.dram_tensor("w_wg", [8, DM, DM], F32, kind="ExternalInput").ap(),
        "Wu": nc.dram_tensor("w_wu", [8, DM, DM], F32, kind="ExternalInput").ap(),
        "Wd": nc.dram_tensor("w_wd", [8, DM, DM], F32, kind="ExternalInput").ap(),
        "Wr": nc.dram_tensor("w_wr", [DM, 8], F32, kind="ExternalInput").ap(),
        "temp": nc.dram_tensor("w_temp", [1], F32, kind="ExternalInput").ap(),
    }
    C = setup_common(P, need_rope=True, need_antid=True)
    emit_l0_attn(P, C, xin, x1, W1)
    P.barrier()
    emit_l0_ffn(P, C, x1, x2, W2)
    P.barrier()
    emit_l1_attn(P, C, x2, x3, W3)
    P.barrier()
    emit_l1_ffn(P, C, x3, xout, W4)
    for n in [k for k in P.last if k.startswith("d_o")]:
        P.wait("sync", n, P.last[n])
    P.finish(nc)
    es.close()
    return nc


def fused_inmap(inputs, b, hc):
    v = l0_ffn_views(inputs)
    return {
        "xin": np.ascontiguousarray(inputs["L0_attn_x"][b]),
        "w_qkv": inputs["L0_attn_Wqkv"], "w_bqkv": inputs["L0_attn_bqkv"],
        "w_wo": inputs["L0_attn_Wo"], "w_bo": inputs["L0_attn_bo"],
        "w_gamma0": inputs["L0_attn_gamma"], "w_beta0": inputs["L0_attn_beta"],
        "w_g1xr": v["g1xr"], "w_g2xr": v["g2xr"],
        "w_g1xi": v["g1xi"], "w_g2xi": v["g2xi"],
        "w_br": inputs["L0_ffn_bias_real"], "w_bi": inputs["L0_ffn_bias_imag"],
        "w_wq1": inputs["L1_attn_Wq"], "w_wk1": inputs["L1_attn_Wk"],
        "w_wv1": inputs["L1_attn_Wv"],
        "w_rel": np.ascontiguousarray(inputs["L1_attn_rel_table"]).reshape(-1),
        "w_gamma1": inputs["L1_attn_gamma"], "w_beta1": inputs["L1_attn_beta"],
        "w_wg": inputs["L1_ffn_Wg"], "w_wu": inputs["L1_ffn_Wu"],
        "w_wd": inputs["L1_ffn_Wd"], "w_wr": inputs["L1_ffn_Wr"],
        "w_temp": inputs["L1_ffn_temperature"],
        "c_ident": hc["ident"], "c_ones": hc["ones_row_f32"],
        "c_cosrep": hc["cosrep"], "c_sinrep": hc["sinrep"], "c_antid": hc["antid"],
    }


# version 18 (final)
# speedup vs baseline: 1.0085x; 1.0085x over previous
import math
from contextlib import ExitStack

import numpy as np
import ml_dtypes

import concourse.bass as bass
import concourse.mybir as mybir
from concourse.bass_utils import run_bass_kernel_spmd

F32 = mybir.dt.float32
BF16 = mybir.dt.bfloat16
AF = mybir.ActivationFunctionType
ALU = mybir.AluOpType
AX = mybir.AxisListType.X
AP = bass.AP

S = 4096
DM = 1024
NH = 16
HD = 64
NT = S // 128
NC = DM // 128

ENGS = ("sync", "tensor", "vector", "scalar", "gpsimd")


def row_ap(dram_ap, n):
    return AP(dram_ap.tensor, dram_ap.offset, [[n, 1], [1, n]])


class Sym:
    __slots__ = ("val",)
    def __init__(self):
        self.val = None


def _resolve(v):
    if v is None:
        return 0
    if isinstance(v, Sym):
        return v.val
    if isinstance(v, (list, tuple)):
        return max((_resolve(x) for x in v), default=0)
    return v


class Pr:
    def __init__(self, nc, es):
        self.nc = nc
        self.es = es
        self.q = {e: [] for e in ENGS}
        self.sems = {}
        self.last = {}
        self._rec = None

    def sem(self, name):
        if name not in self.sems:
            self.sems[name] = self.es.enter_context(self.nc.semaphore(name))
        return self.sems[name]

    def _push(self, eng, entry):
        (self._rec if self._rec is not None else self.q)[eng].append(entry)

    def wait(self, eng, name, val):
        if val is None or val == 0:
            return
        self.sem(name)
        self._push(eng, ("w", name, val))

    def step(self, eng, fn):
        name = "e_" + eng
        self.sem(name)
        s = Sym()
        self.last[name] = s
        self._push(eng, ("s", fn, name, 1, s))
        return s

    def do(self, eng, fn):
        self._push(eng, ("p", fn))

    def dma(self, eng, fn, ctr=None):
        name = ctr or ("d_" + eng)
        self.sem(name)
        s = Sym()
        self.last[name] = s
        self._push(eng, ("s", fn, name, 16, s))
        return s

    def marks(self):
        return dict(self.last)

    def barrier(self):
        vals = dict(self.last)
        for e in ENGS:
            for n, s in vals.items():
                if n != "e_" + e:
                    self.wait(e, n, s)

    def record(self):
        assert self._rec is None
        self._rec = {e: [] for e in ENGS}

    def stop(self):
        r = self._rec
        self._rec = None
        return r

    def emit_zip(self, *recs):
        for e in ENGS:
            ls = [r[e] for r in recs]
            idx = [0] * len(ls)
            while any(idx[i] < len(ls[i]) for i in range(len(ls))):
                for i in range(len(ls)):
                    if idx[i] < len(ls[i]):
                        self.q[e].append(ls[i][idx[i]])
                        idx[i] += 1

    def bake(self):
        cnt = {}
        for e in ENGS:
            for ent in self.q[e]:
                if ent[0] == "s":
                    _, fn, name, inc, s = ent
                    cnt[name] = cnt.get(name, 0) + inc
                    s.val = cnt[name]
        self.cnt = cnt

    def run_body(self, eng_name):
        sems = self.sems
        def body(E):
            for ent in self.q[eng_name]:
                k = ent[0]
                if k == "w":
                    E.wait_ge(sems[ent[1]], _resolve(ent[2]))
                elif k == "s":
                    _, fn, name, inc, s = ent
                    fn(E).then_inc(sems[name], inc)
                else:
                    ent[1](E)
        return body

    def finish(self, nc):
        self.bake()
        with nc.Block() as block:
            block.sync(self.run_body("sync"))
            block.tensor(self.run_body("tensor"))
            block.vector(self.run_body("vector"))
            block.scalar(self.run_body("scalar"))
            block.gpsimd(self.run_body("gpsimd"))


def ln_stats(P, src, ncols, stat, eps_col, sq_tmp):
    nch = (ncols + 511) // 512
    v0 = None
    for i in range(nch):
        w = min(512, ncols - i * 512)
        v0 = P.step("vector", lambda E, s=src, q=sq_tmp, i=i, w=w: E.bn_stats(
            q[:, i * 6:(i + 1) * 6], s[:, i * 512:i * 512 + w]))
    P.wait("vector", "e_vector", v0)
    v1 = P.step("vector", lambda E, q=sq_tmp, d=stat, nch=nch: E.bn_aggr(
        d[:, 2:4], q[:, 0:6 * nch]))
    P.wait("scalar", "e_vector", v1)
    s5 = P.step("scalar", lambda E, d=stat, eps=eps_col: E.activation(
        d[:, 0:1], d[:, 3:4], AF.Ln, bias=eps))
    P.wait("scalar", "e_scalar", s5)
    s6 = P.step("scalar", lambda E, d=stat: E.activation(
        d[:, 3:4], d[:, 0:1], AF.Exp, scale=-0.5))
    return s6


def ln_apply(P, src, dst, stat, ready_sc):
    P.wait("vector", "e_scalar", ready_sc)
    return P.step("vector", lambda E, s=src, d=dst, t=stat: E.tensor_scalar(
        d, s, t[:, 2:3], t[:, 3:4], ALU.subtract, ALU.mult))


def bcast_row(P, C, src_row, width, dst_sb, psum, after_v=None):
    last_v = after_v
    for i in range((width + 511) // 512):
        w = min(512, width - i * 512)
        P.wait("tensor", "e_vector", last_v)
        t = P.step("tensor", lambda E, C=C, s=src_row, p=psum, i=i, w=w:
                   E.matmul(p[:, 0:w], C.ones_row[:], s[:, i * 512:i * 512 + w],
                            start=True, stop=True, skip_group_check=True))
        P.wait("vector", "e_tensor", t)
        last_v = P.step("vector", lambda E, p=psum, d=dst_sb, i=i, w=w:
                        E.tensor_copy(d[:, i * 512:i * 512 + w], p[:, 0:w]))
    return last_v


def host_consts():
    c = {}
    c["ident"] = np.eye(128, dtype=np.float32).astype(ml_dtypes.bfloat16)
    c["antid"] = np.eye(128, dtype=np.float32)[::-1].copy().astype(ml_dtypes.bfloat16)
    c["ones_row_f32"] = np.ones((1, 128), np.float32)
    inv_freq = (1.0 / 10000.0 ** (np.arange(0, HD, 2, dtype=np.float32) / HD)).astype(np.float32)
    pos = np.arange(S, dtype=np.float32)
    fr = pos[:, None] * inv_freq[None, :]
    c["cosrep"] = np.tile(np.cos(fr).astype(np.float32), (1, NH))
    c["sinrep"] = np.tile(np.sin(fr).astype(np.float32), (1, NH))
    return c


class Ctx:
    pass


def setup_common(P, need_rope, need_antid=False):
    nc, es = P.nc, P.es
    C = Ctx()
    C.ident_h = nc.dram_tensor("c_ident", [128, 128], BF16, kind="ExternalInput").ap()
    if need_antid:
        C.antid_h = nc.dram_tensor("c_antid", [128, 128], BF16, kind="ExternalInput").ap()
    C.ones_h = nc.dram_tensor("c_ones", [1, 128], F32, kind="ExternalInput").ap()
    C.ident = es.enter_context(nc.sbuf_tensor("ident", [128, 128], BF16))
    C.ones_row = es.enter_context(nc.sbuf_tensor("ones_row", [1, 128], F32))
    C.ones_col = es.enter_context(nc.sbuf_tensor("ones_col", [128, 1], BF16))
    P.dma("sync", lambda E, C=C: E.dma_start(out=C.ident[:], in_=C.ident_h), ctr="d_pre")
    P.dma("sync", lambda E, C=C: E.dma_start(out=C.ones_row[:], in_=C.ones_h), ctr="d_pre")
    C.v_setup = P.step("vector", lambda E, C=C: E.memset(C.ones_col[:], 1.0))
    C.consts = {}
    def const_col(value, parts=128):
        key = (value, parts)
        if key not in C.consts:
            t = es.enter_context(nc.sbuf_tensor(f"cst{len(C.consts)}", [parts, 1], F32))
            C.consts[key] = (t, P.step("vector", lambda E, t=t, v=value: E.memset(t[:], v)))
        return C.consts[key][0].ap()
    C.const_col = const_col
    C.eps5 = const_col(1e-5)
    C.eps3 = const_col(1e-3)
    import math as _m
    C.lnsc8 = const_col(float(_m.log(1.0 / 8.0)), 1)
    C.lnsc32 = const_col(float(_m.log(1.0 / 32.0)), 1)
    if need_rope:
        C.cos_h = nc.dram_tensor("c_cosrep", [S, 512], F32, kind="ExternalInput").ap()
        C.sin_h = nc.dram_tensor("c_sinrep", [S, 512], F32, kind="ExternalInput").ap()
    return C


PHASE_LIMIT = 99
P3_PAIRS = 8
P3_HH = 2
P3_QQ = 4
P3_KC = NT
P4_NT = NT
P4_WO = True
P4_BC = True

def emit_l0_attn(P, C, xin, xout, W):
    nc = P.nc
    scale = 1.0 / math.sqrt(HD)

    QT_h = nc.dram_tensor("l0_qt", [DM, S], BF16).ap()
    KT_h = nc.dram_tensor("l0_kt", [DM, S], BF16).ap()
    V_h = nc.dram_tensor("l0_v", [8, S, 130], BF16).ap()

    blk = ExitStack()
    attno = blk.enter_context(nc.sbuf_tensor("attno", [128, NT * DM], BF16))
    nmQ = blk.enter_context(nc.sbuf_tensor("nmq", [128, NH], F32))
    nmK = blk.enter_context(nc.sbuf_tensor("nmk", [128, NH], F32))
    negc = blk.enter_context(nc.sbuf_tensor("negc", [128, 1], F32))

    vz = P.step("vector", lambda E, a=nmQ: E.memset(a[:], 0.0))
    vz2 = P.step("vector", lambda E, b=nmK: E.memset(b[:], 0.0))

    ph = ExitStack()
    Wqkv = ph.enter_context(nc.sbuf_tensor("wqkv", [128, NC * 3072], BF16))
    bqkv = ph.enter_context(nc.sbuf_tensor("bqkv", [1, 3072], F32))
    xt = [ph.enter_context(nc.sbuf_tensor(f"xt{i}", [128, DM], F32)) for i in range(2)]
    lnb = ph.enter_context(nc.sbuf_tensor("lnb", [128, DM], BF16))
    lnTt = ph.enter_context(nc.sbuf_tensor("lnTt", [128, DM], BF16))
    sq_tmp = ph.enter_context(nc.sbuf_tensor("sq_tmp", [128, 24], F32))
    stat = ph.enter_context(nc.sbuf_tensor("stat", [128, 4], F32))
    cost = [ph.enter_context(nc.sbuf_tensor(f"cost{i}", [128, 512], F32)) for i in range(2)]
    sint = [ph.enter_context(nc.sbuf_tensor(f"sint{i}", [128, 512], F32)) for i in range(2)]
    rtmp = [ph.enter_context(nc.sbuf_tensor(f"rtmp{i}", [128, 512], F32)) for i in range(2)]
    Qro = ph.enter_context(nc.sbuf_tensor("qro", [128, DM], BF16))
    Kro = ph.enter_context(nc.sbuf_tensor("kro", [128, DM], BF16))
    Vt = ph.enter_context(nc.sbuf_tensor("vt", [128, 8 * 130], BF16))
    QTt = ph.enter_context(nc.sbuf_tensor("qtt", [128, DM], BF16))
    KTt = ph.enter_context(nc.sbuf_tensor("ktt", [128, DM], BF16))
    sqf = ph.enter_context(nc.sbuf_tensor("sqf", [128, DM], F32))
    nupd = ph.enter_context(nc.sbuf_tensor("nupd", [128, NH], F32))
    psT = ph.enter_context(nc.psum_tensor("psT", [128, 1024], BF16))
    psA = ph.enter_context(nc.psum_tensor("psA", [128, 1024], F32))
    psB = ph.enter_context(nc.psum_tensor("psB", [128, 2048], F32))

    wq_src = AP(W["Wqkv"].tensor, 0, [[3072, 128], [128 * 3072, NC], [1, 3072]])
    wq_dst = AP(Wqkv.ap().tensor, 0, [[NC * 3072, 128], [3072, NC], [1, 3072]])
    gw = P.dma("gpsimd", lambda E, s=wq_src, d=wq_dst: E.dma_start(out=d, in_=s), ctr="d_wqkv")
    lb = P.dma("sync", lambda E, d=bqkv, s=row_ap(W["bqkv"], 3072):
               E.dma_start(out=d[:], in_=s), ctr="d_pre")

    vone = P.step("vector", lambda E, d=AP(Vt.ap().tensor, 64, [[8 * 130, 128], [130, 8], [65, 2]]):
                  E.memset(d, 1.0))
    P.wait("tensor", "d_wqkv", gw)
    P.wait("tensor", "d_pre", lb)
    endm = [P.marks(), P.marks()]
    prevm = P.marks()

    for t in range(NT):
        pm = endm[t % 2]
        P.wait("sync", "e_vector", pm.get("e_vector", 0))
        P.wait("sync", "e_scalar", pm.get("e_scalar", 0))
        xb, cb, sb = xt[t % 2], cost[t % 2], sint[t % 2]
        dxc = f"d_x{t % 2}"
        vx = P.dma("sync", lambda E, d=xb, s=xin, t=t:
                   E.dma_start(out=d[:], in_=s[t * 128:(t + 1) * 128, :]), ctr=dxc)
        P.dma("sync", lambda E, d=cb, s=C.cos_h, t=t:
              E.dma_start(out=d[:], in_=s[t * 128:(t + 1) * 128, :]), ctr=dxc)
        vs = P.dma("sync", lambda E, d=sb, s=C.sin_h, t=t:
                   E.dma_start(out=d[:], in_=s[t * 128:(t + 1) * 128, :]), ctr=dxc)
        P.wait("vector", dxc, vx)
        P.wait("scalar", dxc, vx)
        P.wait("vector", "e_tensor", prevm.get("e_tensor", 0))
        P.wait("scalar", "e_vector", prevm.get("e_vector", 0))
        rs = ln_stats(P, xb.ap(), DM, stat.ap(), C.eps5, sq_tmp.ap())
        vln = ln_apply(P, xb.ap(), lnb.ap(), stat.ap(), rs)
        P.wait("tensor", "e_vector", vln)
        for cch in range(NC):
            tt = P.step("tensor", lambda E, C=C, c=cch, s=lnb, p=psT:
                        E.transpose(p[:, c * 128:(c + 1) * 128],
                                    s[:, c * 128:(c + 1) * 128], C.ident[:]))
        P.wait("vector", "e_tensor", tt)
        vT = P.step("vector", lambda E, p=psT, d=lnTt: E.tensor_copy(d[:], p[:]))
        P.wait("tensor", "e_vector", vT)
        for cch in range(NC):
            for j in range(2):
                P.do("tensor", lambda E, c=cch, j=j, l=lnTt, w=Wqkv, p=psA:
                     E.matmul(p[:, j * 512:(j + 1) * 512], l[:, c * 128:(c + 1) * 128],
                              w[:, c * 3072 + j * 512:c * 3072 + (j + 1) * 512],
                              start=(c == 0), stop=False, skip_group_check=True))
            for j in range(4):
                P.do("tensor", lambda E, c=cch, j=j, l=lnTt, w=Wqkv, p=psB:
                     E.matmul(p[:, j * 512:(j + 1) * 512], l[:, c * 128:(c + 1) * 128],
                              w[:, c * 3072 + 1024 + j * 512:c * 3072 + 1024 + (j + 1) * 512],
                              start=(c == 0), stop=False, skip_group_check=True))
        for j in range(2):
            P.do("tensor", lambda E, C=C, j=j, b=bqkv, p=psA:
                 E.matmul(p[:, j * 512:(j + 1) * 512], C.ones_row[:],
                          b[:, j * 512:(j + 1) * 512], start=False, stop=True,
                          skip_group_check=True))
        for j in range(4):
            tq = P.step("tensor", lambda E, C=C, j=j, b=bqkv, p=psB:
                        E.matmul(p[:, j * 512:(j + 1) * 512], C.ones_row[:],
                                 b[:, 1024 + j * 512:1024 + (j + 1) * 512],
                                 start=False, stop=True, skip_group_check=True))
        P.wait("vector", dxc, vs)
        P.wait("vector", f"d_sp{t % 2}", prevm.get(f"d_sp{t % 2}", 0))
        P.wait("vector", "e_tensor", tq)
        vr = None
        for (src, dst) in ((psA, Qro), (psB, Kro)):
            w = src.ap().shape[1]
            e0 = AP(src.ap().tensor, 0, [[w, 128], [2, 512]])
            e1 = AP(src.ap().tensor, 1, [[w, 128], [2, 512]])
            d0 = AP(dst.ap().tensor, 0, [[DM, 128], [2, 512]])
            d1 = AP(dst.ap().tensor, 1, [[DM, 128], [2, 512]])
            P.do("vector", lambda E, e=e0, c=cb, r=rtmp[0]: E.tensor_mul(r[:], e, c[:]))
            va = P.step("vector", lambda E, e=e1, s2=sb, r=rtmp[1]: E.tensor_mul(r[:], e, s2[:]))
            P.wait("vector", "e_vector", va)
            P.do("vector", lambda E, a=rtmp[0], b=rtmp[1], d=d0: E.tensor_sub(d, a[:], b[:]))
            P.do("vector", lambda E, e=e0, s2=sb, r=rtmp[0]: E.tensor_mul(r[:], e, s2[:]))
            vb = P.step("vector", lambda E, e=e1, c=cb, r=rtmp[1]: E.tensor_mul(r[:], e, c[:]))
            P.wait("vector", "e_vector", vb)
            vr = P.step("vector", lambda E, a=rtmp[0], b=rtmp[1], d=d1: E.tensor_add(d, a[:], b[:]))
        vsrcp = AP(psB.ap().tensor, 1024, [[2048, 128], [128, 8], [64, 2], [1, 64]])
        vdstp = AP(Vt.ap().tensor, 0, [[8 * 130, 128], [130, 8], [65, 2], [1, 64]])
        vv = P.step("vector", lambda E, sp=vsrcp, dp=vdstp: E.tensor_copy(dp, sp))
        for (ro, nm) in ((Qro, nmQ), (Kro, nmK)):
            P.wait("vector", "e_vector", vr)
            vq = P.step("vector", lambda E, r=ro, q=sqf: E.tensor_mul(q[:], r[:], r[:]))
            P.wait("vector", "e_vector", vq)
            red_src = AP(sqf.ap().tensor, 0, [[DM, 128], [HD, NH], [1, HD]])
            red_dst = AP(nupd.ap().tensor, 0, [[NH, 128], [1, NH], [1, 1]])
            vq2 = P.step("vector", lambda E, q=red_src, d=red_dst: E.reduce_sum(d, q, axis=AX))
            P.wait("vector", "e_vector", vq2)
            vq3 = P.step("vector", lambda E, a=nm, b=nupd: E.tensor_max(a[:], a[:], b[:]))
        prev_ve = vT
        for (ro, dd, hbm) in ((Qro, QTt, QT_h), (Kro, KTt, KT_h)):
            P.wait("tensor", "e_vector", vr if ro is Qro else vr)
            P.wait("tensor", "e_vector", prev_ve)
            for cch in range(NC):
                tt2 = P.step("tensor", lambda E, C=C, c=cch, s=ro, p=psT:
                             E.transpose(p[:, c * 128:(c + 1) * 128],
                                         s[:, c * 128:(c + 1) * 128], C.ident[:]))
            P.wait("vector", "e_tensor", tt2)
            P.wait("vector", f"d_sp{t % 2}", pm.get(f"d_sp{t % 2}", 0))
            prev_ve = P.step("vector", lambda E, p=psT, d=dd: E.tensor_copy(d[:], p[:]))
            P.wait("sync", "e_vector", prev_ve)
            src = AP(dd.ap().tensor, 0, [[DM, 128], [128, NC], [1, 128]])
            dst = AP(hbm.tensor, t * 128, [[S, 128], [128 * S, NC], [1, 128]])
            P.dma("sync", lambda E, d=dst, s=src: E.dma_start(out=d, in_=s), ctr=f"d_sp{t % 2}")
        P.wait("sync", "e_vector", vv)
        vsrc = AP(Vt.ap().tensor, 0, [[8 * 130, 128], [130, 8], [1, 130]])
        vdst = AP(V_h.tensor, t * 128 * 130, [[130, 128], [S * 130, 8], [1, 130]])
        P.dma("sync", lambda E, d=vdst, s=vsrc: E.dma_start(out=d, in_=s), ctr=f"d_sp{t % 2}")
        endm[t % 2] = P.marks()
        prevm = P.marks()
    ph.close()
    P.barrier()
    if PHASE_LIMIT <= 1:
        blk.close()
        return

    fz = ExitStack()
    nrow = fz.enter_context(nc.sbuf_tensor("nrow", [128, 2], F32))
    nbf = fz.enter_context(nc.sbuf_tensor("nbf", [128, 2], BF16))
    ones2 = fz.enter_context(nc.sbuf_tensor("ones2", [128, 1], F32))
    ntp = fz.enter_context(nc.psum_tensor("ntp", [128, 512], F32))
    ntpb = fz.enter_context(nc.psum_tensor("ntpb", [2, 128], BF16))
    v = P.step("vector", lambda E, o=ones2: E.memset(o[:], 1.0))
    v = P.step("vector", lambda E, a=nmQ, d=nrow: E.reduce_max(d[:, 0:1], a[:], axis=AX))
    P.wait("vector", "e_vector", v)
    v = P.step("vector", lambda E, a=nmK, d=nrow: E.reduce_max(d[:, 1:2], a[:], axis=AX))
    P.wait("vector", "e_vector", v)
    v = P.step("vector", lambda E, s=nrow, d=nbf: E.tensor_copy(d[:], s[:]))
    P.wait("tensor", "e_vector", v)
    t0 = P.step("tensor", lambda E, C=C, s=nbf, p=ntpb: E.transpose(p[:], s[:], C.ident[:]))
    P.wait("vector", "e_tensor", t0)
    v = P.step("vector", lambda E, p=ntpb, d=nrow: E.reduce_max(d[0:2, 0:1], p[:], axis=AX))
    P.wait("scalar", "e_vector", v)
    s1 = P.step("scalar", lambda E, d=nrow: E.activation(d[0:2, 1:2], d[0:2, 0:1], AF.Ln))
    P.wait("tensor", "e_scalar", s1)
    t1 = P.step("tensor", lambda E, s=nrow, o=ones2, p=ntp: E.matmul(
        p[0:1, 0:1], s[0:2, 1:2], o[0:2, 0:1], start=True, stop=True, skip_group_check=True))
    P.wait("scalar", "e_tensor", t1)
    s2 = P.step("scalar", lambda E, p=ntp, d=nrow, b=C.lnsc8: E.activation(
        d[0:1, 1:2], p[0:1, 0:1], AF.Exp, scale=0.5, bias=b))
    P.wait("vector", "e_scalar", s2)
    v = P.step("vector", lambda E, d=nrow: E.tensor_scalar_mul(d[0:1, 0:1], d[0:1, 1:2], -1.0))
    P.wait("tensor", "e_vector", v)
    t2 = P.step("tensor", lambda E, C=C, s=nrow, p=ntp: E.matmul(
        p[:, 0:1], C.ones_row[:], s[0:1, 0:1], start=True, stop=True, skip_group_check=True))
    P.wait("vector", "e_tensor", t2)
    v = P.step("vector", lambda E, p=ntp, d=negc: E.tensor_copy(d[:], p[:, 0:1]))
    fz.close()
    P.barrier()
    if PHASE_LIMIT <= 2:
        blk.close()
        return

    ph = ExitStack()
    KTp = ph.enter_context(nc.sbuf_tensor("ktp", [128, S], BF16))
    QTp = ph.enter_context(nc.sbuf_tensor("qtp", [128, S], BF16))
    Vp = ph.enter_context(nc.sbuf_tensor("vp", [128, NT * 130], BF16))
    Pt = [[ph.enter_context(nc.sbuf_tensor(f"pt{hh}_{i}", [128, 1024], BF16))
           for i in range(2)] for hh in range(2)]
    aT = [ph.enter_context(nc.sbuf_tensor(f"aT{hh}", [65, S], BF16)) for hh in range(2)]
    rcol8 = ph.enter_context(nc.sbuf_tensor("rcol8", [128, 8], F32))

    for pr in range(P3_PAIRS):
        m = P.marks()
        for e in ("e_tensor", "e_vector", "e_scalar"):
            P.wait("sync", e, m.get(e, 0))
        dprc = f"d_pr{pr % 2}"
        P.dma("sync", lambda E, d=KTp, s=KT_h, pr=pr:
              E.dma_start(out=d[:], in_=s[pr * 128:(pr + 1) * 128, :]), ctr=dprc)
        P.dma("sync", lambda E, d=QTp, s=QT_h, pr=pr:
              E.dma_start(out=d[:], in_=s[pr * 128:(pr + 1) * 128, :]), ctr=dprc)
        vpd = AP(Vp.ap().tensor, 0, [[NT * 130, 128], [130, NT], [1, 130]])
        vps = AP(V_h.tensor, pr * S * 130, [[130, 128], [130 * 128, NT], [1, 130]])
        wv = P.dma("sync", lambda E, d=vpd, s=vps: E.dma_start(out=d, in_=s), ctr=dprc)
        P.wait("tensor", dprc, wv)
        P.wait("scalar", dprc, wv)
        qs = ExitStack()
        psS = [qs.enter_context(nc.psum_tensor(f"psS{pr}_{hh}", [128, 1024], F32)) for hh in range(2)]
        psPV = [qs.enter_context(nc.psum_tensor(f"psPV{pr}_{hh}", [128, 1024], F32)) for hh in range(2)]
        for qq in range(P3_QQ):
            sE = {}
            tS = {}
            for kc in range(P3_KC):
                for hh in range(2):
                    hb = hh * 64
                    if kc >= 1:
                        P.wait("tensor", "e_scalar", sE[(kc - 1, hh)])
                    for j in range(2):
                        tS[(kc, hh)] = P.step("tensor",
                                    lambda E, k=KTp, q=QTp, p=psS[hh], kc=kc, qq=qq, j=j, hb=hb:
                                    E.matmul(p[:, j * 512:(j + 1) * 512],
                                             k[hb:hb + 64, kc * 128:(kc + 1) * 128],
                                             q[hb:hb + 64, qq * 1024 + j * 512:qq * 1024 + (j + 1) * 512],
                                             start=True, stop=True, skip_group_check=True))
                for hh in range(2):
                    P.wait("scalar", "e_tensor", tS[(kc, hh)])
                    sE[(kc, hh)] = P.step("scalar",
                                lambda E, p=psS[hh], d=Pt[hh][kc % 2], nb=negc, sc=scale:
                                E.activation(d[:], p[:], AF.Exp, bias=nb[:, 0:1], scale=sc))
                if kc >= 1:
                    for hh in range(2):
                        P.wait("tensor", "e_scalar", sE[(kc - 1, hh)])
                        for j in range(2):
                            tP = P.step("tensor",
                                        lambda E, v2=Vp, d=Pt[hh][(kc - 1) % 2], p=psPV[hh],
                                        kc=kc - 1, j=j, hh=hh:
                                        E.matmul(p[0:65, j * 512:(j + 1) * 512],
                                                 v2[:, kc * 130 + hh * 65:kc * 130 + hh * 65 + 65],
                                                 d[:, j * 512:(j + 1) * 512],
                                                 start=(kc == 0), stop=False,
                                                 skip_group_check=True))
            for hh in range(2):
                P.wait("tensor", "e_scalar", sE[(P3_KC - 1, hh)])
                for j in range(2):
                    tP = P.step("tensor",
                                lambda E, v2=Vp, d=Pt[hh][(P3_KC - 1) % 2], p=psPV[hh],
                                kc=P3_KC - 1, j=j, hh=hh:
                                E.matmul(p[0:65, j * 512:(j + 1) * 512],
                                         v2[:, kc * 130 + hh * 65:kc * 130 + hh * 65 + 65],
                                         d[:, j * 512:(j + 1) * 512],
                                         start=False, stop=True, skip_group_check=True))
            P.wait("vector", "e_tensor", tP)
            for hh in range(2):
                vA = P.step("vector", lambda E, p=psPV[hh], d=aT[hh], qq=qq:
                            E.tensor_copy(d[:, qq * 1024:(qq + 1) * 1024], p[0:65, :]))
            P.wait("tensor", "e_vector", vA)
        qs.close()
        es2 = ExitStack()
        psM = [es2.enter_context(nc.psum_tensor(f"psM{pr}_{i}", [128, 1024], BF16)) for i in range(2)]
        for hh in range(P3_HH):
            h = pr * 2 + hh
            vg = {}
            P.wait("tensor", "e_vector", P.last.get("e_vector", 0))
            for g in range(4):
                pq = psM[g % 2]
                if g >= 2:
                    P.wait("tensor", "e_vector", vg[g - 2])
                for j8 in range(8):
                    last_tT = P.step("tensor", lambda E, C=C, s=aT[hh], p=pq, g=g, j8=j8:
                                     E.transpose(p[:, j8 * 66:j8 * 66 + 65],
                                                 s[0:65, (g * 8 + j8) * 128:(g * 8 + j8 + 1) * 128],
                                                 C.ident[0:65, 0:65]))
                P.wait("vector", "e_tensor", last_tT)
                den_src = AP(pq.ap().tensor, 64, [[1024, 128], [66, 8]])
                v1 = P.step("vector", lambda E, s=den_src, d=rcol8:
                            E.tensor_single_scalar(d[:], s, 1e-38, ALU.add))
                P.wait("vector", "e_vector", v1)
                v2 = P.step("vector", lambda E, d=rcol8: E.reciprocal(d[:], d[:]))
                P.wait("vector", "e_vector", v2)
                for j8 in range(8):
                    qt = g * 8 + j8
                    vg[g] = P.step("vector", lambda E, p=pq, r=rcol8, d=attno, qt=qt, h=h, j8=j8:
                                E.tensor_scalar_mul(d[:, qt * DM + h * 64:qt * DM + (h + 1) * 64],
                                                    p[:, j8 * 66:j8 * 66 + 64], r[:, j8:j8 + 1]))
        es2.close()
    ph.close()
    P.barrier()

    ph = ExitStack()
    Wo_sb = ph.enter_context(nc.sbuf_tensor("wo", [128, NC * DM], BF16))
    bo_sb = ph.enter_context(nc.sbuf_tensor("bo", [1, DM], F32))
    grep = ph.enter_context(nc.sbuf_tensor("grep", [128, DM], F32))
    brep = ph.enter_context(nc.sbuf_tensor("brep", [128, DM], F32))
    grow = ph.enter_context(nc.sbuf_tensor("grow", [1, DM], F32))
    brow = ph.enter_context(nc.sbuf_tensor("brow", [1, DM], F32))
    xb2 = [ph.enter_context(nc.sbuf_tensor(f"xb2{i}", [128, DM], F32)) for i in range(4)]
    aTt = [ph.enter_context(nc.sbuf_tensor(f"aTt{i}", [128, DM], BF16)) for i in range(4)]
    ht = [ph.enter_context(nc.sbuf_tensor(f"ht{i}", [128, DM], F32)) for i in range(4)]
    yt = [ph.enter_context(nc.sbuf_tensor(f"yt{i}", [128, DM], F32)) for i in range(4)]
    sq_tmp = [ph.enter_context(nc.sbuf_tensor(f"sq_tmp2{i}", [128, 24], F32)) for i in range(4)]
    stat = [ph.enter_context(nc.sbuf_tensor(f"stat2{i}", [128, 4], F32)) for i in range(4)]
    psT2 = [ph.enter_context(nc.psum_tensor(f"psT2{i}", [128, 1024], BF16)) for i in range(3)]
    psO = [ph.enter_context(nc.psum_tensor(f"psO{i}", [128, 512], F32)) for i in range(3)]
    psBC = ph.enter_context(nc.psum_tensor("psBC", [128, 512], F32))

    wo_src = AP(W["Wo"].tensor, 0, [[DM, 128], [128 * DM, NC], [1, DM]])
    wo_dst = AP(Wo_sb.ap().tensor, 0, [[NC * DM, 128], [DM, NC], [1, DM]])
    gw2 = P.dma("gpsimd", lambda E, s=wo_src, d=wo_dst: E.dma_start(out=d, in_=s), ctr="d_wo")
    P.dma("sync", lambda E, d=bo_sb, s=row_ap(W["bo"], DM): E.dma_start(out=d[:], in_=s), ctr="d_pre")
    P.dma("sync", lambda E, d=grow, s=row_ap(W["gamma"], DM): E.dma_start(out=d[:], in_=s), ctr="d_pre")
    l3 = P.dma("sync", lambda E, d=brow, s=row_ap(W["beta"], DM): E.dma_start(out=d[:], in_=s), ctr="d_pre")
    P.wait("tensor", "d_pre", l3)
    v1b = bcast_row(P, C, grow.ap(), DM, grep, psBC)
    vbc = bcast_row(P, C, brow.ap(), DM, brep, psBC, after_v=v1b)
    P.wait("tensor", "d_wo", gw2)
    P.wait("tensor", "e_vector", vbc)

    endm4 = [P.marks() for _ in range(4)]
    endp3 = [P.marks() for _ in range(3)]
    def _zbody(t):
            par = t % 4
            pm = endm4[par]
            pm3 = endp3[t % 3]
            xb, aT2, htp, ytp, sqp, stp = xb2[par], aTt[par], ht[par], yt[par], sq_tmp[par], stat[par]
            pT, pO = psT2[t % 3], psO[t % 3]
            P.wait("sync", "e_vector", pm.get("e_vector", 0))
            P.wait("sync", "e_scalar", pm.get("e_scalar", 0))
            vx = P.dma("sync", lambda E, d=xb, s=xin, t=t:
                       E.dma_start(out=d[:], in_=s[t * 128:(t + 1) * 128, :]), ctr=f"d_x{par}")
            P.wait("tensor", "e_vector", pm3.get("e_vector", 0))
            for cch in range(NC):
                tt = P.step("tensor", lambda E, C=C, s=attno, p=pT, t=t, c=cch:
                            E.transpose(p[:, c * 128:(c + 1) * 128],
                                        s[:, t * DM + c * 128:t * DM + (c + 1) * 128], C.ident[:]))
            P.wait("vector", "e_tensor", tt)
            P.wait("vector", f"d_x{par}", vx)
            vT = P.step("vector", lambda E, p=pT, d=aT2: E.tensor_copy(d[:], p[:]))
            P.wait("tensor", "e_vector", vT)
            for jj in range(2):
                for cch in range(NC):
                    P.do("tensor", lambda E, c=cch, j=jj, a=aT2, w=Wo_sb, p=pO:
                         E.matmul(p[:], a[:, c * 128:(c + 1) * 128],
                                  w[:, c * DM + j * 512:c * DM + (j + 1) * 512],
                                  start=(c == 0), stop=False, skip_group_check=True))
                tb = P.step("tensor", lambda E, C=C, j=jj, b=bo_sb, p=pO:
                            E.matmul(p[:], C.ones_row[:],
                                     b[:, j * 512:(j + 1) * 512], start=False, stop=True,
                                     skip_group_check=True))
                P.wait("vector", "e_tensor", tb)
                vev = P.step("vector", lambda E, p=pO, d=ytp, j=jj:
                             E.tensor_copy(d[:, j * 512:(j + 1) * 512], p[:]))
                P.wait("tensor", "e_vector", vev)
            P.wait("vector", f"d_o{par}", pm.get(f"d_o{par}", 0))
            P.wait("scalar", f"d_x{par}", vx)
            P.wait("scalar", "e_vector", pm.get("e_vector", 0))
            rs = ln_stats(P, xb.ap(), DM, stp.ap(), C.eps5, sqp.ap())
            vln = ln_apply(P, xb.ap(), htp.ap(), stp.ap(), rs)
            P.wait("vector", "e_vector", [vln, vev])
            vh = P.step("vector", lambda E, a=ytp, b=htp, d=htp: E.tensor_add(d[:], a[:], b[:]))
            P.wait("vector", "e_vector", vh)
            P.wait("scalar", "e_vector", vh)
            rs2 = ln_stats(P, htp.ap(), DM, stp.ap(), C.eps5, sqp.ap())
            P.wait("vector", "e_scalar", rs2)
            vy = P.step("vector", lambda E, hsrc=htp, g=grep, st=stp, d=ytp:
                        E.scalar_tensor_tensor(d[:], hsrc[:], st[:, 2:3], g[:], ALU.subtract, ALU.mult))
            P.wait("vector", "e_vector", vy)
            vy2 = P.step("vector", lambda E, d=ytp, st=stp: E.tensor_scalar_mul(d[:], d[:], st[:, 3:4]))
            P.wait("vector", "e_vector", vy2)
            vy3 = P.step("vector", lambda E, d=ytp, b=brep: E.tensor_add(d[:], d[:], b[:]))
            P.wait("vector", "e_vector", vy3)
            P.wait("scalar", "e_vector", vy3)
            rs3 = ln_stats(P, ytp.ap(), DM, stp.ap(), C.eps5, sqp.ap())
            vo1 = ln_apply(P, ytp.ap(), htp.ap(), stp.ap(), rs3)
            P.wait("vector", "e_vector", vo1)
            vo2 = P.step("vector", lambda E, x=xb, hsrc=htp, d=htp: E.tensor_add(d[:], x[:], hsrc[:]))
            P.wait("sync", "e_vector", vo2)
            P.dma("sync", lambda E, d=xout, s=htp, t=t:
                  E.dma_start(out=d[t * 128:(t + 1) * 128, :], in_=s[:]), ctr=f"d_o{par}")
            endm4[par] = endp3[t % 3] = P.marks()
    for _z0 in range(0, (P4_NT), 2):
        _rs = []
        for _zi in range(min(2, P4_NT - _z0)):
            P.record(); _zbody(_z0 + _zi); _rs.append(P.stop())
        P.emit_zip(*_rs)
    ph.close()
    blk.close()


def build_l0_attn():
    nc = bass.Bass("TRN2", target_bir_lowering=False)
    es = ExitStack()
    P = Pr(nc, es)
    xin = nc.dram_tensor("xin", [S, DM], F32, kind="ExternalInput").ap()
    xout = nc.dram_tensor("xout", [S, DM], F32, kind="ExternalOutput").ap()
    W = {
        "Wqkv": nc.dram_tensor("w_qkv", [DM, 3 * DM], F32, kind="ExternalInput").ap(),
        "bqkv": nc.dram_tensor("w_bqkv", [3 * DM], F32, kind="ExternalInput").ap(),
        "Wo": nc.dram_tensor("w_wo", [DM, DM], F32, kind="ExternalInput").ap(),
        "bo": nc.dram_tensor("w_bo", [DM], F32, kind="ExternalInput").ap(),
        "gamma": nc.dram_tensor("w_gamma", [DM], F32, kind="ExternalInput").ap(),
        "beta": nc.dram_tensor("w_beta", [DM], F32, kind="ExternalInput").ap(),
    }
    C = setup_common(P, need_rope=True)
    emit_l0_attn(P, C, xin, xout, W)
    for n in [k for k in P.last if k.startswith("d_o")]:
        P.wait("sync", n, P.last[n])
    P.finish(nc)
    es.close()
    return nc


def l0_attn_inmap(inputs, b, hc):
    return {
        "xin": np.ascontiguousarray(inputs["L0_attn_x"][b]),
        "w_qkv": inputs["L0_attn_Wqkv"], "w_bqkv": inputs["L0_attn_bqkv"],
        "w_wo": inputs["L0_attn_Wo"], "w_bo": inputs["L0_attn_bo"],
        "w_gamma": inputs["L0_attn_gamma"], "w_beta": inputs["L0_attn_beta"],
        "c_ident": hc["ident"], "c_ones": hc["ones_row_f32"],
        "c_cosrep": hc["cosrep"], "c_sinrep": hc["sinrep"],
    }


def kernel(**inputs):
    hc = host_consts()
    cores = list(range(8))
    nc = build_fused()
    res = run_bass_kernel_spmd(nc, [fused_inmap(inputs, b, hc) for b in cores], core_ids=cores)
    return np.stack([r["xout"] for r in res.results]).astype(np.float32)

def emit_l0_ffn(P, C, xin, xout, W):
    nc = P.nc
    blk = ExitStack()
    WTr = blk.enter_context(nc.sbuf_tensor("wtr", [128, 4 * 512], BF16))
    WTi = blk.enter_context(nc.sbuf_tensor("wti", [128, 4 * 512], BF16))
    WTin = blk.enter_context(nc.sbuf_tensor("wtin", [128, 4 * 512], BF16))
    brr = blk.enter_context(nc.sbuf_tensor("brr", [1, 512], F32))
    bri = blk.enter_context(nc.sbuf_tensor("bri", [1, 512], F32))
    g1t = blk.enter_context(nc.sbuf_tensor("g1t", [128, 512], F32))
    g2t = blk.enter_context(nc.sbuf_tensor("g2t", [128, 512], F32))
    accw = blk.enter_context(nc.sbuf_tensor("accw", [128, 512], F32))
    prodw = blk.enter_context(nc.sbuf_tensor("prodw", [128, 512], F32))

    P.dma("sync", lambda E, d=brr, s=row_ap(W["bias_real"], 512): E.dma_start(out=d[:], in_=s), ctr="d_pre")
    P.dma("sync", lambda E, d=bri, s=row_ap(W["bias_imag"], 512): E.dma_start(out=d[:], in_=s), ctr="d_pre")
    for (g1h, g2h, dst, extra_neg) in ((W["g1xr"], W["g2xr"], WTr, None),
                                       (W["g1xi"], W["g2xi"], WTi, WTin)):
        for k in range(4):
            last_v = None
            for c in range(4):
                ld = P.dma("sync", lambda E, d=g1t, s=g1h, c=c, k=k:
                           E.dma_start(out=d[:], in_=s[c, k * 128:(k + 1) * 128, :]), ctr="d_pre")
                ld = P.dma("sync", lambda E, d=g2t, s=g2h, c=c, k=k:
                           E.dma_start(out=d[:], in_=s[c, k * 128:(k + 1) * 128, :]), ctr="d_pre")
                P.wait("vector", "d_pre", ld)
                if c == 0:
                    last_v = P.step("vector", lambda E, a=g1t, b=g2t, d=accw:
                                    E.tensor_mul(d[:], a[:], b[:]))
                else:
                    vp_ = P.step("vector", lambda E, a=g1t, b=g2t, d=prodw:
                                 E.tensor_mul(d[:], a[:], b[:]))
                    P.wait("vector", "e_vector", vp_)
                    last_v = P.step("vector", lambda E, d=accw, p=prodw:
                                    E.tensor_add(d[:], d[:], p[:]))
                P.wait("vector", "e_vector", last_v)
                P.wait("sync", "e_vector", last_v)
            vc = P.step("vector", lambda E, d=dst, a=accw, k=k:
                        E.tensor_copy(d[:, k * 512:(k + 1) * 512], a[:]))
            if extra_neg is not None:
                P.wait("vector", "e_vector", vc)
                vc = P.step("vector", lambda E, d=extra_neg, a=accw, k=k:
                            E.tensor_scalar_mul(d[:, k * 512:(k + 1) * 512], a[:], -1.0))
            P.wait("vector", "e_vector", vc)

    ph = ExitStack()
    xt = [ph.enter_context(nc.sbuf_tensor(f"fxt{i}", [128, DM], F32)) for i in range(4)]
    lnb = [ph.enter_context(nc.sbuf_tensor(f"flnb{i}", [128, DM], BF16)) for i in range(4)]
    lnTt = [ph.enter_context(nc.sbuf_tensor(f"flnTt{i}", [128, DM], BF16)) for i in range(4)]
    sq_tmp = [ph.enter_context(nc.sbuf_tensor(f"fsq{i}", [128, 24], F32)) for i in range(4)]
    stat = [ph.enter_context(nc.sbuf_tensor(f"fstat{i}", [128, 4], F32)) for i in range(4)]
    yt = [ph.enter_context(nc.sbuf_tensor(f"fyt{i}", [128, DM], F32)) for i in range(4)]
    ht = [ph.enter_context(nc.sbuf_tensor(f"fht{i}", [128, DM], F32)) for i in range(4)]
    psT = [ph.enter_context(nc.psum_tensor(f"fpsT{i}", [128, 1024], BF16)) for i in range(2)]
    psR = [ph.enter_context(nc.psum_tensor(f"fpsR{i}", [128, 512], F32)) for i in range(2)]
    psI = [ph.enter_context(nc.psum_tensor(f"fpsI{i}", [128, 512], F32)) for i in range(2)]

    endm = [P.marks() for _ in range(4)]
    def _zbody(t):
            par = t % 4
            pm = endm[par]
            xb, lb2, lT, sqp, stp, ytp, htp = (xt[par], lnb[par], lnTt[par], sq_tmp[par],
                                               stat[par], yt[par], ht[par])
            pT, pR, pI = psT[t % 2], psR[t % 2], psI[t % 2]
            P.wait("sync", "e_vector", pm.get("e_vector", 0))
            P.wait("sync", "e_scalar", pm.get("e_scalar", 0))
            vx = P.dma("sync", lambda E, d=xb, s=xin, t=t:
                       E.dma_start(out=d[:], in_=s[t * 128:(t + 1) * 128, :]), ctr=f"d_x{par}")
            P.wait("vector", f"d_x{par}", vx)
            P.wait("scalar", f"d_x{par}", vx)
            P.wait("vector", "e_tensor", pm.get("e_tensor", 0))
            P.wait("vector", f"d_o{par}", pm.get(f"d_o{par}", 0))
            P.wait("scalar", "e_vector", pm.get("e_vector", 0))
            rs = ln_stats(P, xb.ap(), DM, stp.ap(), C.eps5, sqp.ap())
            vln = ln_apply(P, xb.ap(), lb2.ap(), stp.ap(), rs)
            P.wait("tensor", "e_vector", vln)
            for cch in range(NC):
                tt = P.step("tensor", lambda E, C=C, c=cch, s=lb2, p=pT:
                            E.transpose(p[:, c * 128:(c + 1) * 128],
                                        s[:, c * 128:(c + 1) * 128], C.ident[:]))
            P.wait("vector", "e_tensor", tt)
            vT = P.step("vector", lambda E, p=pT, d=lT: E.tensor_copy(d[:], p[:]))
            P.wait("tensor", "e_vector", vT)
            for cch in range(NC):
                wsel_r = WTr if cch < 4 else WTin
                wsel_i = WTi if cch < 4 else WTr
                kk = cch % 4
                P.do("tensor", lambda E, c=cch, k=kk, l=lT, w=wsel_r, p=pR:
                     E.matmul(p[:], l[:, c * 128:(c + 1) * 128],
                              w[:, k * 512:(k + 1) * 512],
                              start=(c == 0), stop=False, skip_group_check=True))
                P.do("tensor", lambda E, c=cch, k=kk, l=lT, w=wsel_i, p=pI:
                     E.matmul(p[:], l[:, c * 128:(c + 1) * 128],
                              w[:, k * 512:(k + 1) * 512],
                              start=(c == 0), stop=False, skip_group_check=True))
            P.do("tensor", lambda E, C=C, b=brr, p=pR:
                 E.matmul(p[:], C.ones_row[:], b[:], start=False, stop=True,
                          skip_group_check=True))
            tb = P.step("tensor", lambda E, C=C, b=bri, p=pI:
                        E.matmul(p[:], C.ones_row[:], b[:], start=False, stop=True,
                                 skip_group_check=True))
            P.wait("vector", "e_tensor", tb)
            vy1 = P.step("vector", lambda E, p=pR, d=ytp: E.tensor_copy(d[:, 0:512], p[:]))
            vy2 = P.step("vector", lambda E, p=pI, d=ytp: E.tensor_copy(d[:, 512:1024], p[:]))
            P.wait("vector", "e_vector", vy2)
            P.wait("scalar", "e_vector", vy2)
            rs2 = ln_stats(P, ytp.ap(), DM, stp.ap(), C.eps5, sqp.ap())
            vo1 = ln_apply(P, ytp.ap(), htp.ap(), stp.ap(), rs2)
            P.wait("vector", "e_vector", vo1)
            vo2 = P.step("vector", lambda E, x=xb, h=htp, d=htp: E.tensor_add(d[:], x[:], h[:]))
            P.wait("sync", "e_vector", vo2)
            P.dma("sync", lambda E, d=xout, s=htp, t=t:
                  E.dma_start(out=d[t * 128:(t + 1) * 128, :], in_=s[:]), ctr=f"d_o{par}")
            endm[par] = P.marks()
    for _z0 in range(0, (NT), 2):
        _rs = []
        for _zi in range(2):
            P.record(); _zbody(_z0 + _zi); _rs.append(P.stop())
        P.emit_zip(*_rs)
    ph.close()
    blk.close()


def build_l0_ffn():
    nc = bass.Bass("TRN2", target_bir_lowering=False)
    es = ExitStack()
    P = Pr(nc, es)
    xin = nc.dram_tensor("xin", [S, DM], F32, kind="ExternalInput").ap()
    xout = nc.dram_tensor("xout", [S, DM], F32, kind="ExternalOutput").ap()
    W = {
        "g1xr": nc.dram_tensor("w_g1xr", [4, 512, 512], F32, kind="ExternalInput").ap(),
        "g2xr": nc.dram_tensor("w_g2xr", [4, 512, 512], F32, kind="ExternalInput").ap(),
        "g1xi": nc.dram_tensor("w_g1xi", [4, 512, 512], F32, kind="ExternalInput").ap(),
        "g2xi": nc.dram_tensor("w_g2xi", [4, 512, 512], F32, kind="ExternalInput").ap(),
        "bias_real": nc.dram_tensor("w_br", [512], F32, kind="ExternalInput").ap(),
        "bias_imag": nc.dram_tensor("w_bi", [512], F32, kind="ExternalInput").ap(),
    }
    C = setup_common(P, need_rope=False)
    emit_l0_ffn(P, C, xin, xout, W)
    for n in [k for k in P.last if k.startswith("d_o")]:
        P.wait("sync", n, P.last[n])
    P.finish(nc)
    es.close()
    return nc


def l0_ffn_views(inputs):
    out = {}
    for (nm, g1, g2) in (("r", inputs["L0_ffn_G1_R"], inputs["L0_ffn_G2_R"]),
                         ("i", inputs["L0_ffn_G1_I"], inputs["L0_ffn_G2_I"])):
        a = np.empty((4, 512, 512), np.float32)
        b = np.empty((4, 512, 512), np.float32)
        for c in range(4):
            A = g1[0, :, c, :]
            B = g2[c, :, 0, :]
            a[c] = np.broadcast_to(A.T[:, None, :, None], (32, 16, 32, 16)).reshape(512, 512)
            b[c] = np.broadcast_to(B.T[None, :, None, :], (32, 16, 32, 16)).reshape(512, 512)
        out["g1x" + nm] = a
        out["g2x" + nm] = b
    return out


def l0_ffn_inmap(inputs, xin, hc):
    v = l0_ffn_views(inputs)
    return {
        "xin": np.ascontiguousarray(xin),
        "w_g1xr": v["g1xr"], "w_g2xr": v["g2xr"],
        "w_g1xi": v["g1xi"], "w_g2xi": v["g2xi"],
        "w_br": inputs["L0_ffn_bias_real"], "w_bi": inputs["L0_ffn_bias_imag"],
        "c_ident": hc["ident"], "c_ones": hc["ones_row_f32"],
    }


GOLDEN_C = float(1.0 / math.e)
GOLDEN_LO = float(0.5 - math.log(4.0 / 3.0))
GOLDEN_UP = 0.5

def emit_l1_ffn(P, C, xin, xout, W):
    nc = P.nc
    ln3T_h = nc.dram_tensor("l1f_lnt", [DM, S], BF16).ap()

    blk = ExitStack()
    wr_sb = blk.enter_context(nc.sbuf_tensor("wr", [128, NC * 8], BF16))
    w_sb = blk.enter_context(nc.sbuf_tensor("wrt", [128, NT * 8], F32))
    t11 = blk.enter_context(nc.sbuf_tensor("t11", [1, 2], F32))
    ninvt = blk.enter_context(nc.sbuf_tensor("ninvt", [128, 1], F32))

    wr_src = AP(W["Wr"].tensor, 0, [[8, 128], [128 * 8, NC], [1, 8]])
    wr_dst = AP(wr_sb.ap().tensor, 0, [[NC * 8, 128], [8, NC], [1, 8]])
    gw = P.dma("gpsimd", lambda E, s=wr_src, d=wr_dst: E.dma_start(out=d, in_=s), ctr="d_wr")
    lt = P.dma("sync", lambda E, d=t11, s=row_ap(W["temp"], 1): E.dma_start(out=d[0:1, 0:1], in_=s), ctr="d_pre")
    P.wait("vector", "d_pre", lt)
    v = P.step("vector", lambda E, d=t11: E.reciprocal(d[0:1, 1:2], d[0:1, 0:1]))
    P.wait("vector", "e_vector", v)
    v = P.step("vector", lambda E, d=t11: E.tensor_scalar_mul(d[0:1, 0:1], d[0:1, 1:2], -1.0))
    P.wait("tensor", "e_vector", v)

    ph = ExitStack()
    xt = [ph.enter_context(nc.sbuf_tensor(f"gxt{i}", [128, DM], F32)) for i in range(4)]
    lnb = [ph.enter_context(nc.sbuf_tensor(f"glnb{i}", [128, DM], BF16)) for i in range(4)]
    lnTt = [ph.enter_context(nc.sbuf_tensor(f"glnTt{i}", [128, DM], BF16)) for i in range(4)]
    sq_tmp = [ph.enter_context(nc.sbuf_tensor(f"gsq{i}", [128, 24], F32)) for i in range(4)]
    stat = [ph.enter_context(nc.sbuf_tensor(f"gstat{i}", [128, 4], F32)) for i in range(4)]
    rtb = [ph.enter_context(nc.sbuf_tensor(f"rtb{i}", [128, 64], F32)) for i in range(4)]
    rc = [ph.enter_context(nc.sbuf_tensor(f"rc{i}", [128, 8], F32)) for i in range(4)]
    psT = [ph.enter_context(nc.psum_tensor(f"gpsT{i}", [128, 1024], BF16)) for i in range(3)]
    psZ = [ph.enter_context(nc.psum_tensor(f"gpsZ{i}", [128, 8], F32)) for i in range(3)]
    psBC2 = ph.enter_context(nc.psum_tensor("gpsBC", [128, 512], F32))

    t0 = P.step("tensor", lambda E, C=C, s=t11, p=psBC2: E.matmul(
        p[:, 0:1], C.ones_row[:], s[0:1, 0:1], start=True, stop=True, skip_group_check=True))
    P.wait("vector", "e_tensor", t0)
    v = P.step("vector", lambda E, p=psBC2, d=ninvt: E.tensor_copy(d[:], p[:, 0:1]))
    P.wait("tensor", "d_wr", gw)

    prevm = P.marks()
    endg = [P.marks() for _ in range(4)]
    def _zbody(t):
            par = t % 4
            pm2 = endg[par]
            rsl = (lambda par: (lambda i: rtb[par].ap()[:, i * 8:(i + 1) * 8]))(par)
            rcp = rc[par]
            lb2, lT, sqp, stp, pT, pZ = (lnb[par], lnTt[par], sq_tmp[par], stat[par],
                                         psT[t % 3], psZ[t % 3])
            P.wait("sync", "e_vector", pm2.get("e_vector", 0))
            P.wait("sync", "e_scalar", pm2.get("e_scalar", 0))
            xb = xt[par]
            vx = P.dma("sync", lambda E, d=xb, s=xin, t=t:
                       E.dma_start(out=d[:], in_=s[t * 128:(t + 1) * 128, :]), ctr=f"d_x{par}")
            P.wait("vector", f"d_x{par}", vx)
            P.wait("scalar", f"d_x{par}", vx)
            P.wait("vector", "e_tensor", pm2.get("e_tensor", 0))
            P.wait("scalar", "e_vector", pm2.get("e_vector", 0))
            rs = ln_stats(P, xb.ap(), DM, stp.ap(), C.eps5, sqp.ap())
            vln = ln_apply(P, xb.ap(), lb2.ap(), stp.ap(), rs)
            P.wait("tensor", "e_vector", vln)
            for cch in range(NC):
                tt = P.step("tensor", lambda E, C=C, c=cch, s=lb2, p=pT:
                            E.transpose(p[:, c * 128:(c + 1) * 128],
                                        s[:, c * 128:(c + 1) * 128], C.ident[:]))
            P.wait("vector", "e_tensor", tt)
            P.wait("vector", f"d_sp{par}", pm2.get(f"d_sp{par}", 0))
            vT = P.step("vector", lambda E, p=pT, d=lT: E.tensor_copy(d[:], p[:]))
            P.wait("sync", "e_vector", vT)
            src = AP(lT.ap().tensor, 0, [[DM, 128], [128, NC], [1, 128]])
            dst = AP(ln3T_h.tensor, t * 128, [[S, 128], [128 * S, NC], [1, 128]])
            P.dma("sync", lambda E, d=dst, s=src: E.dma_start(out=d, in_=s), ctr=f"d_sp{par}")
            P.wait("tensor", "e_vector", vT)
            for cch in range(NC):
                tz = P.step("tensor", lambda E, c=cch, l=lT, w=wr_sb, p=pZ:
                            E.matmul(p[:], l[:, c * 128:(c + 1) * 128],
                                     w[:, c * 8:(c + 1) * 8],
                                     start=(c == 0), stop=(c == NC - 1), skip_group_check=True))
            P.wait("scalar", "e_tensor", tz)
            s1 = P.step("scalar", lambda E, p=pZ, d=rsl(0), sc=ninvt:
                        E.activation(d, p[:], AF.Exp, scale=sc[:, 0:1]))
            P.wait("vector", "e_scalar", s1)
            v = P.step("vector", lambda E, a=rsl(0), d=rsl(1): E.tensor_scalar_add(d, a, 1.0))
            P.wait("vector", "e_vector", v)
            v = P.step("vector", lambda E, d=rsl(1): E.reciprocal(d, d))
            P.wait("vector", "e_vector", v)
            v = P.step("vector", lambda E, a=rsl(1), d=rsl(2): E.tensor_single_scalar(d, a, GOLDEN_LO, ALU.is_ge))
            P.wait("vector", "e_vector", v)
            v = P.step("vector", lambda E, a=rsl(1), d=rsl(6): E.tensor_single_scalar(d, a, GOLDEN_UP, ALU.is_le))
            P.wait("vector", "e_vector", v)
            v = P.step("vector", lambda E, a=rsl(2), b=rsl(6), d=rsl(2): E.tensor_mul(d, a, b))
            P.wait("vector", "e_vector", v)
            v = P.step("vector", lambda E, a=rsl(1), d=rsl(3): E.tensor_single_scalar(d, a, GOLDEN_C, ALU.subtract))
            P.wait("vector", "e_vector", v)
            v = P.step("vector", lambda E, a=rsl(3), d=rsl(6): E.tensor_scalar_mul(d, a, -1.0))
            P.wait("vector", "e_vector", v)
            v = P.step("vector", lambda E, a=rsl(3), b=rsl(6), d=rsl(3): E.tensor_max(d, a, b))
            P.wait("scalar", "e_vector", v)
            s2 = P.step("scalar", lambda E, a=rsl(3), d=rsl(4): E.activation(d, a, AF.Exp, scale=-10.0))
            s3 = P.step("scalar", lambda E, a=rsl(3), d=rsl(5): E.activation(d, a, AF.Exp, scale=float(-1.0 / 0.3)))
            P.wait("vector", "e_scalar", s2)
            v = P.step("vector", lambda E, a=rsl(4), b=rsl(2), d=rsl(4): E.tensor_mul(d, a, b))
            P.wait("vector", "e_vector", v)
            v = P.step("vector", lambda E, a=rsl(4), d=rcp: E.reduce_sum(d[:, 0:1], a, axis=AX))
            P.wait("vector", "e_vector", v)
            v = P.step("vector", lambda E, d=rcp: E.tensor_single_scalar(d[:, 1:2], d[:, 0:1], 1e-8, ALU.is_lt))
            P.wait("vector", "e_scalar", s3)
            P.wait("vector", "e_vector", v)
            v = P.step("vector", lambda E, a=rsl(5), d=rcp: E.reduce_max(d[:, 2:3], a, axis=AX))
            P.wait("vector", "e_vector", v)
            v = P.step("vector", lambda E, a=rsl(5), d=rsl(6), m=rcp: E.tensor_single_scalar(d, a, m[:, 2:3], ALU.is_equal))
            P.wait("vector", "e_vector", v)
            v = P.step("vector", lambda E, a=rsl(6), b=rsl(5), d=rsl(7): E.scalar_tensor_tensor(d, a, -1e30, b, ALU.mult, ALU.add))
            P.wait("vector", "e_vector", v)
            v = P.step("vector", lambda E, a=rsl(7), d=rcp: E.reduce_max(d[:, 3:4], a, axis=AX))
            P.wait("vector", "e_vector", v)
            v = P.step("vector", lambda E, a=rsl(5), d=rsl(6), m=rcp: E.tensor_single_scalar(d, a, m[:, 3:4], ALU.is_ge))
            P.wait("vector", "e_vector", v)
            v = P.step("vector", lambda E, a=rsl(5), b=rsl(6), d=rsl(7): E.tensor_mul(d, a, b))
            P.wait("vector", "e_vector", v)
            v = P.step("vector", lambda E, a=rsl(7), d=rcp: E.reduce_sum(d[:, 4:5], a, axis=AX))
            P.wait("vector", "e_vector", v)
            v = P.step("vector", lambda E, d=rcp: E.tensor_scalar_max(d[:, 4:5], d[:, 4:5], 1e-8))
            P.wait("vector", "e_vector", v)
            v = P.step("vector", lambda E, d=rcp: E.reciprocal(d[:, 4:5], d[:, 4:5]))
            P.wait("vector", "e_vector", v)
            v = P.step("vector", lambda E, a=rsl(7), d=rsl(7), m=rcp: E.tensor_scalar_mul(d, a, m[:, 4:5]))
            P.wait("vector", "e_vector", v)
            v = P.step("vector", lambda E, a=rsl(7), b=rsl(4), d=rsl(6): E.tensor_sub(d, a, b))
            P.wait("vector", "e_vector", v)
            v = P.step("vector", lambda E, a=rsl(6), b=rsl(4), d=rsl(4), m=rcp: E.scalar_tensor_tensor(d, a, m[:, 1:2], b, ALU.mult, ALU.add))
            P.wait("vector", "e_vector", v)
            v = P.step("vector", lambda E, a=rsl(4), d=rcp: E.reduce_sum(d[:, 5:6], a, axis=AX))
            P.wait("vector", "e_vector", v)
            v = P.step("vector", lambda E, d=rcp: E.tensor_scalar_max(d[:, 5:6], d[:, 5:6], 1e-8))
            P.wait("vector", "e_vector", v)
            v = P.step("vector", lambda E, d=rcp: E.reciprocal(d[:, 5:6], d[:, 5:6]))
            P.wait("vector", "e_vector", v)
            v = P.step("vector", lambda E, a=rsl(4), d=w_sb, t=t, m=rcp:
                       E.tensor_scalar_mul(d[:, t * 8:(t + 1) * 8], a, m[:, 5:6]))
            endg[par] = P.marks()
    for _z0 in range(0, (NT), 2):
        _rs = []
        for _zi in range(min(2, NT - _z0)):
            P.record(); _zbody(_z0 + _zi); _rs.append(P.stop())
        P.emit_zip(*_rs)
    ph.close()
    P.barrier()

    ph = ExitStack()
    lnq = ph.enter_context(nc.sbuf_tensor("lnq", [128, NC * 1024], BF16))
    hq = ph.enter_context(nc.sbuf_tensor("hq", [128, 8 * 1024], BF16))
    gsb = ph.enter_context(nc.sbuf_tensor("gsb", [128, 1024], BF16))
    acc = ph.enter_context(nc.sbuf_tensor("acc", [128, 8 * 1024], F32))
    wg_sb = [ph.enter_context(nc.sbuf_tensor(f"wg{i}", [128, NC * 1024], BF16)) for i in range(2)]
    wu_sb = [ph.enter_context(nc.sbuf_tensor(f"wu{i}", [128, NC * 1024], BF16)) for i in range(2)]
    wd_sb = [ph.enter_context(nc.sbuf_tensor(f"wd{i}", [128, NC * 1024], BF16)) for i in range(2)]
    xt2 = [ph.enter_context(nc.sbuf_tensor(f"hxt{i}", [128, DM], F32)) for i in range(4)]
    sq2 = [ph.enter_context(nc.sbuf_tensor(f"hsq{i}", [128, 24], F32)) for i in range(4)]
    st2 = [ph.enter_context(nc.sbuf_tensor(f"hstat{i}", [128, 4], F32)) for i in range(4)]
    ht2 = [ph.enter_context(nc.sbuf_tensor(f"hht{i}", [128, DM], F32)) for i in range(4)]
    psG = ph.enter_context(nc.psum_tensor("psG", [128, 1024], F32))
    psU = ph.enter_context(nc.psum_tensor("psU", [128, 1024], F32))
    psD = [ph.enter_context(nc.psum_tensor(f"psD{i}", [128, 1024], F32)) for i in range(2)]

    items = [(q, e) for q in range(4) for e in range(8)]

    def prefetch(idx):
        q, e = items[idx]
        bi = idx % 2
        gw = None
        for (wh, wsbs) in ((W["Wg"], wg_sb), (W["Wu"], wu_sb), (W["Wd"], wd_sb)):
            wsrc = AP(wh.tensor, e * DM * DM, [[DM, 128], [128 * DM, NC], [1, DM]])
            wdst = AP(wsbs[bi].ap().tensor, 0, [[NC * 1024, 128], [1024, NC], [1, DM]])
            gw = P.dma("gpsimd", lambda E, s=wsrc, d=wdst: E.dma_start(out=d, in_=s),
                       ctr=f"d_w{idx % 2}")
        return gw

    dval = {}
    dval[0] = prefetch(0)
    end_mark = {}
    for idx, (q, e) in enumerate(items):
        bi = idx % 2
        wg, wu, wd = wg_sb[bi], wu_sb[bi], wd_sb[bi]
        if e == 0:
            mq = P.marks()
            P.wait("sync", "e_tensor", mq.get("e_tensor", 0))
            lq_src = AP(ln3T_h.tensor, q * 1024, [[S, 128], [128 * S, NC], [1, 1024]])
            lq_dst = AP(lnq.ap().tensor, 0, [[NC * 1024, 128], [1024, NC], [1, 1024]])
            lq = P.dma("sync", lambda E, d=lq_dst, s=lq_src: E.dma_start(out=d, in_=s), ctr="d_lnq")
            P.wait("tensor", "d_lnq", lq)
        if idx + 1 < len(items):
            P.wait("gpsimd", "e_tensor", end_mark.get(idx - 1, 0))
            dval[idx + 1] = prefetch(idx + 1)
        P.wait("tensor", f"d_w{idx % 2}", dval[idx])
        for ic in range(NC):
            for (ps, wsb2) in ((psG, wg), (psU, wu)):
                for cch in range(NC):
                    for j in range(2):
                        tm = P.step("tensor", lambda E, c=cch, j=j, i=ic, l=lnq, w=wsb2, p=ps:
                                    E.matmul(p[:, j * 512:(j + 1) * 512],
                                             w[:, c * 1024 + i * 128:c * 1024 + (i + 1) * 128],
                                             l[:, c * 1024 + j * 512:c * 1024 + (j + 1) * 512],
                                             start=(c == 0), stop=(c == NC - 1),
                                             skip_group_check=True))
                if ps is psG:
                    tG = tm
            P.wait("scalar", "e_tensor", tG)
            sg = P.step("scalar", lambda E, p=psG, d=gsb: E.activation(d[:], p[:], AF.Silu))
            P.wait("vector", "e_scalar", sg)
            P.wait("vector", "e_tensor", tm)
            vh = P.step("vector", lambda E, g=gsb, p=psU, d=hq, i=ic:
                        E.tensor_mul(d[:, i * 1024:(i + 1) * 1024], g[:], p[:]))
            P.wait("tensor", "e_scalar", sg)
            P.wait("tensor", "e_vector", vh)
        vaccs = {}
        for tt2 in range(8):
            pD = psD[tt2 % 2]
            if tt2 >= 2:
                P.wait("tensor", "e_vector", vaccs[tt2 - 2])
            for ic in range(NC):
                for j in range(2):
                    td = P.step("tensor", lambda E, i=ic, j=j, tt=tt2, h=hq, w=wd, p=pD:
                                E.matmul(p[:, j * 512:(j + 1) * 512],
                                         h[:, i * 1024 + tt * 128:i * 1024 + (tt + 1) * 128],
                                         w[:, i * 1024 + j * 512:i * 1024 + (j + 1) * 512],
                                         start=(i == 0), stop=(i == NC - 1),
                                         skip_group_check=True))
            P.wait("vector", "e_tensor", td)
            wcol = w_sb.ap()[:, (q * 8 + tt2) * 8 + e:(q * 8 + tt2) * 8 + e + 1]
            if e == 0:
                vaccs[tt2] = P.step("vector", lambda E, p=pD, d=acc, tt=tt2, wc=wcol:
                            E.tensor_scalar_mul(d[:, tt * 1024:(tt + 1) * 1024], p[:], wc))
            else:
                vaccs[tt2] = P.step("vector", lambda E, p=pD, d=acc, tt=tt2, wc=wcol:
                            E.scalar_tensor_tensor(d[:, tt * 1024:(tt + 1) * 1024],
                                                   p[:], wc, d[:, tt * 1024:(tt + 1) * 1024],
                                                   ALU.mult, ALU.add))
        P.wait("tensor", "e_vector", vaccs[7])
        end_mark[idx] = P.last.get("e_tensor", 0)
        if e == 7:
            epar = [P.marks(), P.marks()]
            def _zbody(tt2):
                    t = q * 8 + tt2
                    par2 = tt2 % 2
                    mp = epar[par2]
                    P.wait("sync", "e_vector", mp.get("e_vector", 0))
                    P.wait("sync", "e_scalar", mp.get("e_scalar", 0))
                    xb = xt2[par2]
                    sqp2, stp2, htp2 = sq2[par2], st2[par2], ht2[par2]
                    vx = P.dma("sync", lambda E, d=xb, s=xin, t=t:
                               E.dma_start(out=d[:], in_=s[t * 128:(t + 1) * 128, :]), ctr=f"d_x{par2}")
                    P.wait("vector", f"d_x{par2}", vx)
                    P.wait("vector", f"d_o{par2}", mp.get(f"d_o{par2}", 0))
                    P.wait("scalar", f"d_x{par2}", vx)
                    P.wait("scalar", "e_vector", mp.get("e_vector", 0))
                    atile = acc.ap()[:, tt2 * 1024:(tt2 + 1) * 1024]
                    rs = ln_stats(P, atile, DM, stp2.ap(), C.eps5, sqp2.ap())
                    vo1 = ln_apply(P, atile, htp2.ap(), stp2.ap(), rs)
                    P.wait("vector", "e_vector", vo1)
                    vo2 = P.step("vector", lambda E, x=xb, h=htp2, d=htp2: E.tensor_add(d[:], x[:], h[:]))
                    P.wait("sync", "e_vector", vo2)
                    P.dma("sync", lambda E, d=xout, s=htp2, t=t:
                          E.dma_start(out=d[t * 128:(t + 1) * 128, :], in_=s[:]), ctr=f"d_o{par2}")
                    epar[par2] = P.marks()
            for _z0 in range(0, (8), 2):
                _rs = []
                for _zi in range(2):
                    P.record(); _zbody(_z0 + _zi); _rs.append(P.stop())
                P.emit_zip(*_rs)
    ph.close()
    blk.close()


def build_l1_ffn():
    nc = bass.Bass("TRN2", target_bir_lowering=False)
    es = ExitStack()
    P = Pr(nc, es)
    xin = nc.dram_tensor("xin", [S, DM], F32, kind="ExternalInput").ap()
    xout = nc.dram_tensor("xout", [S, DM], F32, kind="ExternalOutput").ap()
    W = {
        "Wg": nc.dram_tensor("w_wg", [8, DM, DM], F32, kind="ExternalInput").ap(),
        "Wu": nc.dram_tensor("w_wu", [8, DM, DM], F32, kind="ExternalInput").ap(),
        "Wd": nc.dram_tensor("w_wd", [8, DM, DM], F32, kind="ExternalInput").ap(),
        "Wr": nc.dram_tensor("w_wr", [DM, 8], F32, kind="ExternalInput").ap(),
        "temp": nc.dram_tensor("w_temp", [1], F32, kind="ExternalInput").ap(),
    }
    C = setup_common(P, need_rope=False)
    emit_l1_ffn(P, C, xin, xout, W)
    for n in [k for k in P.last if k.startswith("d_o")]:
        P.wait("sync", n, P.last[n])
    P.finish(nc)
    es.close()
    return nc


def l1_ffn_inmap(inputs, xin, hc):
    return {
        "xin": np.ascontiguousarray(xin),
        "w_wg": inputs["L1_ffn_Wg"], "w_wu": inputs["L1_ffn_Wu"],
        "w_wd": inputs["L1_ffn_Wd"], "w_wr": inputs["L1_ffn_Wr"],
        "w_temp": inputs["L1_ffn_temperature"],
        "c_ident": hc["ident"], "c_ones": hc["ones_row_f32"],
    }


def emit_l1_attn(P, C, xin, xout, W):
    nc = P.nc
    scale = 1.0 / 32.0

    QT_h = nc.dram_tensor("l1_qt", [DM, S], BF16).ap()
    KT_h = nc.dram_tensor("l1_kt", [DM, S], BF16).ap()
    V_h = nc.dram_tensor("l1_v", [8, S, 130], BF16).ap()
    D_h = nc.dram_tensor("l1_dflat", [2304 * 16], BF16).ap()
    stage_h = nc.dram_tensor("l1_stage", [S, 64], F32).ap()

    blk = ExitStack()
    attno = blk.enter_context(nc.sbuf_tensor("a1_attno", [128, NT * DM], BF16))
    tv_col = blk.enter_context(nc.sbuf_tensor("a1_tv", [128, 1], BF16))
    antid_p = blk.enter_context(nc.sbuf_tensor("a1_antidp", [128, 128], BF16))
    u128 = blk.enter_context(nc.sbuf_tensor("a1_u128", [128, 64], F32))

    dz = ExitStack()
    zb = dz.enter_context(nc.sbuf_tensor("a1_zb", [128, 288], BF16))
    tload = dz.enter_context(nc.sbuf_tensor("a1_tload", [128, 16], F32))
    t2r = dz.enter_context(nc.sbuf_tensor("a1_t2r", [1, 128], F32))
    t2rep = dz.enter_context(nc.sbuf_tensor("a1_t2rep", [128, 16], F32))
    dt_bf = dz.enter_context(nc.sbuf_tensor("a1_dtbf", [128, 16], BF16))
    psX = dz.enter_context(nc.psum_tensor("a1_psX", [128, 512], F32))

    la = P.dma("sync", lambda E, d=antid_p, s=C.antid_h: E.dma_start(out=d[:], in_=s), ctr="d_pre")
    vz = P.step("vector", lambda E, d=zb: E.memset(d[:], 0.0))
    P.wait("sync", "e_vector", vz)
    zdst = AP(D_h.tensor, 0, [[288, 128], [1, 288]])
    P.dma("sync", lambda E, d=zdst, s=zb: E.dma_start(out=d, in_=s.ap()), ctr="d_db")
    l2 = P.dma("sync", lambda E, d=t2r, s=AP(W["rel"].tensor, 2046 * 16, [[16, 1], [1, 16]]):
               E.dma_start(out=d.ap()[0:1, 0:16], in_=s), ctr="d_pre")
    P.wait("vector", "d_pre", l2)
    lastv = None
    for i in range(1, 8):
        lastv = P.step("vector", lambda E, d=t2r, i=i:
                       E.tensor_copy(d[0:1, i * 16:(i + 1) * 16], d[0:1, 0:16]))
        P.wait("vector", "e_vector", lastv)
    P.wait("tensor", "e_vector", lastv)
    P.wait("tensor", "d_pre", la)
    tbc = P.step("tensor", lambda E, C=C, s=t2r, p=psX: E.matmul(
        p[:, 0:16], C.ones_row[:], s[0:1, 0:16], start=True, stop=True, skip_group_check=True))
    P.wait("vector", "e_tensor", tbc)
    v = P.step("vector", lambda E, p=psX, d=t2rep: E.tensor_copy(d[:], p[:, 0:16]))
    P.wait("vector", "e_vector", v)
    for tile in range(16):
        nrows = 128 if tile < 15 else 127
        P.wait("sync", "e_vector", P.last.get("e_vector", 0))
        tsrc = AP(W["rel"].tensor, tile * 128 * 16, [[16, nrows], [1, 16]])
        lt2 = P.dma("sync", lambda E, d=tload, s=tsrc, n=nrows:
                    E.dma_start(out=d.ap()[0:n, :], in_=s), ctr="d_pre")
        P.wait("vector", "d_pre", lt2)
        vsub = P.step("vector", lambda E, a=tload, b=t2rep, d=dt_bf: E.tensor_sub(d[:], a[:], b[:]))
        P.wait("sync", "e_vector", vsub)
        ddst2 = AP(D_h.tensor, tile * 128 * 16, [[16, nrows], [1, 16]])
        P.dma("sync", lambda E, d=ddst2, s=dt_bf, n=nrows:
              E.dma_start(out=d, in_=s.ap()[0:n, :]), ctr="d_db")
    P.wait("tensor", "e_vector", P.last.get("e_vector", 0))
    tvm = P.step("tensor", lambda E, s=t2r, o=C.ones_row, p=psX: E.matmul(
        p[:, 300:301], s[0:1, 0:128], o[0:1, 0:1], start=True, stop=True, skip_group_check=True))
    P.wait("vector", "e_tensor", tvm)
    v = P.step("vector", lambda E, p=psX, d=tv_col: E.tensor_copy(d[:], p[:, 300:301]))
    P.wait("vector", "e_vector", v)
    dz.close()

    pj = ExitStack()
    ln2T = pj.enter_context(nc.sbuf_tensor("a1_ln2t", [128, NC * S], BF16))
    wq_sb = pj.enter_context(nc.sbuf_tensor("a1_wq", [128, NC * DM], BF16))
    wk_sb = pj.enter_context(nc.sbuf_tensor("a1_wk", [128, NC * DM], BF16))
    wv_sb = pj.enter_context(nc.sbuf_tensor("a1_wv", [128, NC * DM], BF16))
    xt = [pj.enter_context(nc.sbuf_tensor(f"a1_xt{i}", [128, DM], F32)) for i in range(4)]
    lnb = [pj.enter_context(nc.sbuf_tensor(f"a1_lnb{i}", [128, DM], BF16)) for i in range(4)]
    sq_tmp = [pj.enter_context(nc.sbuf_tensor(f"a1_sq{i}", [128, 24], F32)) for i in range(4)]
    stat = [pj.enter_context(nc.sbuf_tensor(f"a1_stat{i}", [128, 4], F32)) for i in range(4)]
    qtt = pj.enter_context(nc.sbuf_tensor("a1_qtt", [128, 512], BF16))
    vtt = pj.enter_context(nc.sbuf_tensor("a1_vtt", [128, 8 * 130], BF16))
    psT = [pj.enter_context(nc.psum_tensor(f"a1_psT{i}", [128, 1024], BF16)) for i in range(2)]
    psQ = pj.enter_context(nc.psum_tensor("a1_psQ", [128, 512], F32))
    psV = pj.enter_context(nc.psum_tensor("a1_psV", [128, 1024], F32))

    for (wh, wsb) in ((W["Wq"], wq_sb), (W["Wk"], wk_sb), (W["Wv"], wv_sb)):
        wsrc = AP(wh.tensor, 0, [[DM, 128], [128 * DM, NC], [1, DM]])
        wdst = AP(wsb.ap().tensor, 0, [[NC * DM, 128], [DM, NC], [1, DM]])
        gw = P.dma("gpsimd", lambda E, s=wsrc, d=wdst: E.dma_start(out=d, in_=s))

    endm1 = [P.marks() for _ in range(4)]
    def _zbody(t):
            par = t % 4
            pm2 = endm1[par]
            P.wait("sync", "e_vector", pm2.get("e_vector", 0))
            P.wait("sync", "e_scalar", pm2.get("e_scalar", 0))
            xb, lb2, sqp, stp, pT = xt[par], lnb[par], sq_tmp[par], stat[par], psT[t % 2]
            vx = P.dma("sync", lambda E, d=xb, s=xin, t=t:
                       E.dma_start(out=d[:], in_=s[t * 128:(t + 1) * 128, :]), ctr=f"d_x{par}")
            P.wait("vector", f"d_x{par}", vx)
            P.wait("scalar", f"d_x{par}", vx)
            P.wait("vector", "e_tensor", pm2.get("e_tensor", 0))
            P.wait("scalar", "e_vector", pm2.get("e_vector", 0))
            rs = ln_stats(P, xb.ap(), DM, stp.ap(), C.eps5, sqp.ap())
            vln = ln_apply(P, xb.ap(), lb2.ap(), stp.ap(), rs)
            P.wait("tensor", "e_vector", vln)
            for cch in range(NC):
                tt = P.step("tensor", lambda E, C=C, c=cch, s=lb2, p=pT:
                            E.transpose(p[:, c * 128:(c + 1) * 128],
                                        s[:, c * 128:(c + 1) * 128], C.ident[:]))
            P.wait("vector", "e_tensor", tt)
            ldst = AP(ln2T.ap().tensor, t * 128, [[NC * S, 128], [S, NC], [1, 128]])
            lsrc = AP(pT.ap().tensor, 0, [[1024, 128], [128, NC], [1, 128]])
            vT = P.step("vector", lambda E, d=ldst, s=lsrc: E.tensor_copy(d, s))
            endm1[par] = P.marks()
    for _z0 in range(0, (NT), 2):
        _rs = []
        for _zi in range(2):
            P.record(); _zbody(_z0 + _zi); _rs.append(P.stop())
        P.emit_zip(*_rs)
    P.barrier()
    P.wait("tensor", "d_wl1", P.last.get("d_wl1", 0))

    prev_evac = 0
    for (wsb, hbm) in ((wq_sb, QT_h), (wk_sb, KT_h)):
        for ft in range(NC):
            for tc in range(8):
                P.wait("tensor", "e_vector", prev_evac)
                for cch in range(NC):
                    tq = P.step("tensor", lambda E, c=cch, f=ft, tc=tc, w=wsb, l=ln2T, p=psQ:
                                E.matmul(p[:], w[:, c * DM + f * 128:c * DM + (f + 1) * 128],
                                         l[:, c * S + tc * 512:c * S + (tc + 1) * 512],
                                         start=(c == 0), stop=(c == NC - 1),
                                         skip_group_check=True))
                P.wait("vector", "e_tensor", tq)
                P.wait("vector", "d_ps", P.last.get("d_ps", 0))
                prev_evac = P.step("vector", lambda E, p=psQ, d=qtt: E.tensor_copy(d[:], p[:]))
                P.wait("sync", "e_vector", prev_evac)
                P.dma("sync", lambda E, d=hbm, s=qtt, f=ft, tc=tc:
                      E.dma_start(out=d[f * 128:(f + 1) * 128, tc * 512:(tc + 1) * 512], in_=s[:]),
                      ctr="d_ps")
    vone = P.step("vector", lambda E, d=AP(vtt.ap().tensor, 64, [[8 * 130, 128], [130, 8], [65, 2]]):
                  E.memset(d, 1.0))
    prev_vv = 0
    for t in range(NT):
        P.wait("tensor", "e_vector", [prev_vv, vone])
        for cch in range(NC):
            for j in range(2):
                tv2 = P.step("tensor", lambda E, c=cch, j=j, t=t, w=wv_sb, l=ln2T, p=psV:
                             E.matmul(p[:, j * 512:(j + 1) * 512],
                                      l[:, c * S + t * 128:c * S + (t + 1) * 128],
                                      w[:, c * DM + j * 512:c * DM + (j + 1) * 512],
                                      start=(c == 0), stop=(c == NC - 1),
                                      skip_group_check=True))
        P.wait("vector", "e_tensor", tv2)
        P.wait("vector", "d_ps", P.last.get("d_ps", 0))
        vsrcp = AP(psV.ap().tensor, 0, [[1024, 128], [128, 8], [64, 2], [1, 64]])
        vdstp = AP(vtt.ap().tensor, 0, [[8 * 130, 128], [130, 8], [65, 2], [1, 64]])
        prev_vv = P.step("vector", lambda E, sp=vsrcp, dp=vdstp: E.tensor_copy(dp, sp))
        P.wait("sync", "e_vector", prev_vv)
        vsrc = AP(vtt.ap().tensor, 0, [[8 * 130, 128], [130, 8], [1, 130]])
        vdst = AP(V_h.tensor, t * 128 * 130, [[130, 128], [S * 130, 8], [1, 130]])
        P.dma("sync", lambda E, d=vdst, s=vsrc: E.dma_start(out=d, in_=s), ctr="d_ps")
    pj.close()
    P.barrier()

    ph = ExitStack()
    KTp = ph.enter_context(nc.sbuf_tensor("a1_ktp", [128, S], BF16))
    QTp = ph.enter_context(nc.sbuf_tensor("a1_qtp", [128, S], BF16))
    Vp = ph.enter_context(nc.sbuf_tensor("a1_vp", [128, NT * 130], BF16))
    sqb = ph.enter_context(nc.sbuf_tensor("a1_sqb", [128, S], BF16))
    Pt = [[ph.enter_context(nc.sbuf_tensor(f"a1_pt{hh}_{i}", [128, 1024], BF16))
           for i in range(2)] for hh in range(2)]
    aT = [ph.enter_context(nc.sbuf_tensor(f"a1_aT{hh}", [65, S], BF16)) for hh in range(2)]
    rcol8 = ph.enter_context(nc.sbuf_tensor("a1_rcol8", [128, 8], F32))
    negc2 = ph.enter_context(nc.sbuf_tensor("a1_negc", [128, 2], F32))
    grow = ph.enter_context(nc.sbuf_tensor("a1_grow", [1, 8], F32))
    u_row = ph.enter_context(nc.sbuf_tensor("a1_urow", [1, 65], F32))
    ddst = [ph.enter_context(nc.sbuf_tensor(f"a1_ddst{i}", [128, S], BF16)) for i in range(2)]
    dT_sb = ph.enter_context(nc.sbuf_tensor("a1_dT", [128, S], BF16))
    dout = ph.enter_context(nc.sbuf_tensor("a1_dout", [128, 64], F32))
    ufill = ph.enter_context(nc.sbuf_tensor("a1_ufill", [128, 2048], F32))
    stg = ph.enter_context(nc.sbuf_tensor("a1_stg", [128, 2048], F32))

    for pr in range(8):
        m = P.marks()
        for e in ("e_tensor", "e_vector", "e_scalar"):
            P.wait("sync", e, m.get(e, 0))
        dprc = f"d_pr{pr % 2}"
        P.dma("sync", lambda E, d=KTp, s=KT_h, pr=pr:
              E.dma_start(out=d[:], in_=s[pr * 128:(pr + 1) * 128, :]), ctr=dprc)
        P.dma("sync", lambda E, d=QTp, s=QT_h, pr=pr:
              E.dma_start(out=d[:], in_=s[pr * 128:(pr + 1) * 128, :]), ctr=dprc)
        vpd = AP(Vp.ap().tensor, 0, [[NT * 130, 128], [130, NT], [1, 130]])
        vps = AP(V_h.tensor, pr * S * 130, [[130, 128], [130 * 128, NT], [1, 130]])
        wv2 = P.dma("sync", lambda E, d=vpd, s=vps: E.dma_start(out=d, in_=s), ctr=dprc)
        P.wait("tensor", dprc, wv2)
        P.wait("scalar", dprc, wv2)
        P.wait("vector", dprc, wv2)
        st_s = ExitStack()
        psN = st_s.enter_context(nc.psum_tensor(f"a1_psN{pr}", [128, 512], F32))
        for hh in range(2):
            hb = hh * 64
            for (tens, col) in ((QTp, 0), (KTp, 1)):
                P.wait("vector", "e_tensor", P.last.get("e_tensor", 0))
                vq = P.step("vector", lambda E, a=tens, d=sqb: E.tensor_mul(d[:], a[:], a[:]))
                P.wait("tensor", "e_vector", vq)
                gmax = None
                for tc in range(8):
                    tn = P.step("tensor", lambda E, C=C, s=sqb, p=psN, hb=hb, tc=tc:
                                E.matmul(p[0:1, 0:512], C.ones_col[hb:hb + 64, 0:1],
                                         s[hb:hb + 64, tc * 512:(tc + 1) * 512],
                                         start=True, stop=True, skip_group_check=True))
                    P.wait("vector", "e_tensor", tn)
                    vm = P.step("vector", lambda E, p=psN, d=rcol8, col=col:
                                E.reduce_max(d[0:1, col:col + 1], p[0:1, 0:512], axis=AX))
                    P.wait("vector", "e_vector", vm)
                    if tc == 0:
                        gmax = P.step("vector", lambda E, d=grow, r=rcol8, col=col:
                                      E.tensor_copy(d[0:1, col:col + 1], r[0:1, col:col + 1]))
                    else:
                        gmax = P.step("vector", lambda E, d=grow, r=rcol8, col=col:
                                      E.tensor_max(d[0:1, col:col + 1], d[0:1, col:col + 1],
                                                   r[0:1, col:col + 1]))
                    P.wait("vector", "e_vector", gmax)
                    P.wait("tensor", "e_vector", gmax)
            P.wait("scalar", "e_vector", gmax)
            sl = P.step("scalar", lambda E, d=grow: E.activation(d[0:1, 2:4], d[0:1, 0:2], AF.Ln))
            P.wait("vector", "e_scalar", sl)
            vadd = P.step("vector", lambda E, d=grow: E.tensor_add(d[0:1, 4:5], d[0:1, 2:3], d[0:1, 3:4]))
            P.wait("scalar", "e_vector", vadd)
            se = P.step("scalar", lambda E, d=grow, b=C.lnsc32: E.activation(
                d[0:1, 5:6], d[0:1, 4:5], AF.Exp, scale=0.5, bias=b))
            P.wait("vector", "e_scalar", se)
            vneg = P.step("vector", lambda E, d=grow: E.tensor_scalar_mul(d[0:1, 6:7], d[0:1, 5:6], -1.0))
            P.wait("tensor", "e_vector", vneg)
            tb2 = P.step("tensor", lambda E, C=C, s=grow, p=psN, hh=hh: E.matmul(
                p[:, 0:1], C.ones_row[:], s[0:1, 6:7], start=True, stop=True, skip_group_check=True))
            P.wait("vector", "e_tensor", tb2)
            vnc = P.step("vector", lambda E, p=psN, d=negc2, hh=hh:
                         E.tensor_copy(d[:, hh:hh + 1], p[:, 0:1]))
            P.wait("scalar", "e_vector", vnc)
            P.wait("tensor", "e_vector", vnc)
        st_s.close()
        qs = ExitStack()
        psS = [qs.enter_context(nc.psum_tensor(f"a1_psS{pr}_{hh}", [128, 1024], F32)) for hh in range(2)]
        psPV = [qs.enter_context(nc.psum_tensor(f"a1_psPV{pr}_{hh}", [128, 1024], F32)) for hh in range(2)]
        for qq in range(4):
            sE = {}
            tS = {}
            for kc in range(NT):
                for hh in range(2):
                    hb = hh * 64
                    if kc >= 1:
                        P.wait("tensor", "e_scalar", sE[(kc - 1, hh)])
                    for j in range(2):
                        tS[(kc, hh)] = P.step("tensor",
                                    lambda E, k=KTp, q=QTp, p=psS[hh], kc=kc, qq=qq, j=j, hb=hb:
                                    E.matmul(p[:, j * 512:(j + 1) * 512],
                                             k[hb:hb + 64, kc * 128:(kc + 1) * 128],
                                             q[hb:hb + 64, qq * 1024 + j * 512:qq * 1024 + (j + 1) * 512],
                                             start=True, stop=True, skip_group_check=True))
                for hh in range(2):
                    P.wait("scalar", "e_tensor", tS[(kc, hh)])
                    sE[(kc, hh)] = P.step("scalar",
                                lambda E, p=psS[hh], d=Pt[hh][kc % 2], nb=negc2, hh=hh, sc=scale:
                                E.activation(d[:], p[:], AF.Exp, bias=nb[:, hh:hh + 1], scale=sc))
                if kc >= 1:
                    for hh in range(2):
                        P.wait("tensor", "e_scalar", sE[(kc - 1, hh)])
                        for j in range(2):
                            tP = P.step("tensor",
                                        lambda E, v2=Vp, d=Pt[hh][(kc - 1) % 2], p=psPV[hh],
                                        kc=kc - 1, j=j, hh=hh:
                                        E.matmul(p[0:65, j * 512:(j + 1) * 512],
                                                 v2[:, kc * 130 + hh * 65:kc * 130 + hh * 65 + 65],
                                                 d[:, j * 512:(j + 1) * 512],
                                                 start=(kc == 0), stop=False,
                                                 skip_group_check=True))
            for hh in range(2):
                P.wait("tensor", "e_scalar", sE[(NT - 1, hh)])
                for j in range(2):
                    tP = P.step("tensor",
                                lambda E, v2=Vp, d=Pt[hh][(NT - 1) % 2], p=psPV[hh],
                                kc=NT - 1, j=j, hh=hh:
                                E.matmul(p[0:65, j * 512:(j + 1) * 512],
                                         v2[:, kc * 130 + hh * 65:kc * 130 + hh * 65 + 65],
                                         d[:, j * 512:(j + 1) * 512],
                                         start=False, stop=True, skip_group_check=True))
            P.wait("vector", "e_tensor", tP)
            for hh in range(2):
                vA = P.step("vector", lambda E, p=psPV[hh], d=aT[hh], qq=qq:
                            E.tensor_copy(d[:, qq * 1024:(qq + 1) * 1024], p[0:65, :]))
            P.wait("tensor", "e_vector", vA)
        qs.close()
        es2 = ExitStack()
        psM = [es2.enter_context(nc.psum_tensor(f"a1_psM{pr}_{i}", [128, 1024], BF16)) for i in range(2)]
        psDel = es2.enter_context(nc.psum_tensor(f"a1_psDel{pr}", [128, 512], F32))
        for hh in range(2):
            h = pr * 2 + hh
            P.wait("tensor", "e_vector", P.last.get("e_vector", 0))
            for kc in range(NT):
                tu = P.step("tensor", lambda E, tv=tv_col, v2=Vp, p=psDel, kc=kc, hh=hh:
                            E.matmul(p[0:1, 64:129], tv[:],
                                     v2[:, kc * 130 + hh * 65:kc * 130 + hh * 65 + 65],
                                     start=(kc == 0), stop=(kc == NT - 1), skip_group_check=True))
            P.wait("vector", "e_tensor", tu)
            vu = P.step("vector", lambda E, p=psDel, d=u_row: E.tensor_copy(d[0:1, :], p[0:1, 64:129]))
            P.wait("tensor", "e_vector", vu)
            tu2 = P.step("tensor", lambda E, C=C, s=u_row, p=psDel: E.matmul(
                p[:, 0:64], C.ones_row[:], s[0:1, 0:64], start=True, stop=True, skip_group_check=True))
            P.wait("vector", "e_tensor", tu2)
            vu2 = P.step("vector", lambda E, p=psDel, d=u128: E.tensor_copy(d[:], p[:, 0:64]))
            P.wait("vector", "e_vector", vu2)
            if h >= 8:
                lastu = vu2
                for i in range(32):
                    lastu = P.step("vector", lambda E, d=ufill, s=u128, i=i:
                                   E.tensor_copy(d[:, i * 64:(i + 1) * 64], s[:]))
                    P.wait("vector", "e_vector", lastu)
                P.wait("sync", "e_vector", lastu)
                P.wait("sync", "d_st", P.last.get("d_st", 0))
                uf2_src = AP(ufill.ap().tensor, 0, [[2048, 128], [64, 32], [1, 64]])
                uf_dst = AP(stage_h.tensor, 0, [[64, 128], [8192, 32], [1, 64]])
                ufd = P.dma("sync", lambda E, d=uf_dst, s=uf2_src: E.dma_start(out=d, in_=s), ctr="d_st")
                P.wait("sync", "d_st", ufd)
                classes = [(A, q16h) for A in range(0, h - 8 + 1) for q16h in range(2)]
                dmark = {}
                def dgather(ci):
                    A, q16h = classes[ci]
                    s0_base = (A - h) * 256 + 4095 - 128 * q16h
                    base = (s0_base - 127) * 16
                    gsrc = AP(D_h.tensor, base, [[16, 128], [1, 4096]])
                    P.wait("sync", "e_tensor", dmark.get(ci - 2, 0))
                    return P.dma("sync", lambda E, d=ddst[ci % 2], s=gsrc:
                                 E.dma_start(out=d[:], in_=s), ctr=f"d_dg{ci % 2}")
                gval = {0: dgather(0)}
                for ci in range(len(classes)):
                    A, q16h = classes[ci]
                    if ci + 1 < len(classes):
                        gval[ci + 1] = dgather(ci + 1)
                    dd = ddst[ci % 2]
                    P.wait("tensor", f"d_dg{ci % 2}", gval[ci])
                    prev_dv = None
                    for bq in range(8):
                        pmx = psM[bq % 2]
                        if bq >= 2:
                            P.wait("tensor", "e_vector", prev_dv)
                        for j4 in range(4):
                            tdt = P.step("tensor", lambda E, s=dd, p=pmx, bq=bq, j4=j4, aj=antid_p:
                                         E.transpose(p[:, j4 * 128:(j4 + 1) * 128],
                                                     s[:, (bq * 4 + j4) * 128:(bq * 4 + j4 + 1) * 128],
                                                     aj[:]))
                        P.wait("vector", "e_tensor", tdt)
                        prev_dv = P.step("vector", lambda E, p=pmx, d=dT_sb, bq=bq:
                                         E.tensor_copy(d[:, bq * 512:(bq + 1) * 512], p[:, 0:512]))
                    P.wait("tensor", "e_vector", prev_dv)
                    for kc in range(NT):
                        tdm = P.step("tensor", lambda E, dt=dT_sb, v2=Vp, p=psDel, kc=kc, hh=hh:
                                     E.matmul(p[:, 0:64], dt[:, kc * 128:(kc + 1) * 128],
                                              v2[:, kc * 130 + hh * 65:kc * 130 + hh * 65 + 64],
                                              start=(kc == 0), stop=(kc == NT - 1),
                                              skip_group_check=True))
                    P.wait("vector", "e_tensor", tdm)
                    P.wait("vector", "d_st", P.last.get("d_st", 0))
                    vd = P.step("vector", lambda E, p=psDel, u=u128, d=dout:
                                E.tensor_add(d[:], u[:], p[:, 0:64]))
                    P.wait("sync", "e_vector", vd)
                    sc_dst = AP(stage_h.tensor, (128 * q16h * 16 + A) * 64,
                                [[1024, 128], [1, 64]])
                    P.dma("sync", lambda E, d=sc_dst, s=dout: E.dma_start(out=d, in_=s[:]), ctr="d_st")
                    P.wait("tensor", "e_vector", vd)
                    dmark[ci] = P.last.get("e_tensor", 0)
                P.wait("sync", "e_vector", P.last.get("e_vector", 0))
                P.wait("sync", "d_st", P.last.get("d_st", 0))
                sg_src = AP(stage_h.tensor, 0, [[64, 128], [8192, 32], [1, 64]])
                sg_dst = AP(stg.ap().tensor, 0, [[2048, 128], [64, 32], [1, 64]])
                sgd = P.dma("sync", lambda E, d=sg_dst, s=sg_src: E.dma_start(out=d, in_=s), ctr="d_st")
                P.wait("vector", "d_st", sgd)
            vg = {}
            P.wait("tensor", "e_vector", P.last.get("e_vector", 0))
            for g in range(4):
                pq = psM[g % 2]
                if g >= 2:
                    P.wait("tensor", "e_vector", vg[g - 2])
                for j8 in range(8):
                    last_tT = P.step("tensor", lambda E, C=C, s=aT[hh], p=pq, g=g, j8=j8:
                                     E.transpose(p[:, j8 * 66:j8 * 66 + 65],
                                                 s[0:65, (g * 8 + j8) * 128:(g * 8 + j8 + 1) * 128],
                                                 C.ident[0:65, 0:65]))
                P.wait("vector", "e_tensor", last_tT)
                den_src = AP(pq.ap().tensor, 64, [[1024, 128], [66, 8]])
                v1 = P.step("vector", lambda E, s=den_src, d=rcol8:
                            E.tensor_single_scalar(d[:], s, 1e-38, ALU.add))
                P.wait("vector", "e_vector", v1)
                v2b = P.step("vector", lambda E, d=rcol8: E.reciprocal(d[:], d[:]))
                P.wait("vector", "e_vector", v2b)
                for j8 in range(8):
                    qt = g * 8 + j8
                    badd = (stg.ap()[:, qt * 64:(qt + 1) * 64] if h >= 8 else u128.ap())
                    vg[g] = P.step("vector", lambda E, p=pq, r=rcol8, b=badd, d=attno, qt=qt, h=h, j8=j8:
                                E.scalar_tensor_tensor(d[:, qt * DM + h * 64:qt * DM + (h + 1) * 64],
                                                       p[:, j8 * 66:j8 * 66 + 64], r[:, j8:j8 + 1], b,
                                                       ALU.mult, ALU.add))
        es2.close()
    ph.close()
    P.barrier()

    fz = ExitStack()
    grep = fz.enter_context(nc.sbuf_tensor("a1_grep", [128, DM], F32))
    brep = fz.enter_context(nc.sbuf_tensor("a1_brep", [128, DM], F32))
    growf = fz.enter_context(nc.sbuf_tensor("a1_growf", [1, DM], F32))
    browf = fz.enter_context(nc.sbuf_tensor("a1_browf", [1, DM], F32))
    xb2 = [fz.enter_context(nc.sbuf_tensor(f"a1_xb2{i}", [128, DM], F32)) for i in range(4)]
    av32 = [fz.enter_context(nc.sbuf_tensor(f"a1_av32{i}", [128, DM], F32)) for i in range(4)]
    ht = [fz.enter_context(nc.sbuf_tensor(f"a1_ht{i}", [128, DM], F32)) for i in range(4)]
    yt = [fz.enter_context(nc.sbuf_tensor(f"a1_yt{i}", [128, DM], F32)) for i in range(4)]
    sq2 = [fz.enter_context(nc.sbuf_tensor(f"a1_sq2{i}", [128, 24], F32)) for i in range(4)]
    st2 = [fz.enter_context(nc.sbuf_tensor(f"a1_st2{i}", [128, 4], F32)) for i in range(4)]
    psBC = fz.enter_context(nc.psum_tensor("a1_psBC", [128, 512], F32))

    P.dma("sync", lambda E, d=growf, s=row_ap(W["gamma"], DM): E.dma_start(out=d[:], in_=s), ctr="d_pre")
    l3 = P.dma("sync", lambda E, d=browf, s=row_ap(W["beta"], DM): E.dma_start(out=d[:], in_=s), ctr="d_pre")
    P.wait("tensor", "d_pre", l3)
    v1b = bcast_row(P, C, growf.ap(), DM, grep, psBC)
    vbc = bcast_row(P, C, browf.ap(), DM, brep, psBC, after_v=v1b)

    endmf = [P.marks() for _ in range(4)]
    def _zbody(t):
            par = t % 4
            pm = endmf[par]
            xbp, avp, htp, ytp, sqp, stp = (xb2[par], av32[par], ht[par], yt[par],
                                            sq2[par], st2[par])
            P.wait("sync", "e_vector", pm.get("e_vector", 0))
            P.wait("sync", "e_scalar", pm.get("e_scalar", 0))
            P.wait("scalar", "e_vector", pm.get("e_vector", 0))
            vx = P.dma("sync", lambda E, d=xbp, s=xin, t=t:
                       E.dma_start(out=d[:], in_=s[t * 128:(t + 1) * 128, :]), ctr=f"d_x{par}")
            P.wait("vector", f"d_x{par}", vx)
            P.wait("scalar", f"d_x{par}", vx)
            P.wait("vector", "e_vector", vbc)
            P.wait("vector", f"d_o{par}", pm.get(f"d_o{par}", 0))
            va = P.step("vector", lambda E, s=attno, d=avp, t=t:
                        E.tensor_copy(d[:], s[:, t * DM:(t + 1) * DM]))
            P.wait("vector", "e_vector", va)
            P.wait("scalar", "e_vector", va)
            rs2 = ln_stats(P, avp.ap(), DM, stp.ap(), C.eps3, sqp.ap())
            P.wait("vector", "e_scalar", rs2)
            vy = P.step("vector", lambda E, hsrc=avp, g=grep, st=stp, d=ytp:
                        E.scalar_tensor_tensor(d[:], hsrc[:], st[:, 2:3], g[:], ALU.subtract, ALU.mult))
            P.wait("vector", "e_vector", vy)
            vy2 = P.step("vector", lambda E, d=ytp, st=stp: E.tensor_scalar_mul(d[:], d[:], st[:, 3:4]))
            P.wait("vector", "e_vector", vy2)
            vy3 = P.step("vector", lambda E, d=ytp, b=brep: E.tensor_add(d[:], d[:], b[:]))
            P.wait("vector", "e_vector", vy3)
            P.wait("scalar", "e_vector", vy3)
            rs3 = ln_stats(P, ytp.ap(), DM, stp.ap(), C.eps5, sqp.ap())
            vo1 = ln_apply(P, ytp.ap(), htp.ap(), stp.ap(), rs3)
            P.wait("vector", "e_vector", vo1)
            vo2 = P.step("vector", lambda E, x=xbp, hsrc=htp, d=htp: E.tensor_add(d[:], x[:], hsrc[:]))
            P.wait("sync", "e_vector", vo2)
            P.dma("sync", lambda E, d=xout, s=htp, t=t:
                  E.dma_start(out=d[t * 128:(t + 1) * 128, :], in_=s[:]), ctr=f"d_o{par}")
            endmf[par] = P.marks()
    for _z0 in range(0, (NT), 2):
        _rs = []
        for _zi in range(2):
            P.record(); _zbody(_z0 + _zi); _rs.append(P.stop())
        P.emit_zip(*_rs)
    fz.close()
    blk.close()


def build_l1_attn():
    nc = bass.Bass("TRN2", target_bir_lowering=False)
    es = ExitStack()
    P = Pr(nc, es)
    xin = nc.dram_tensor("xin", [S, DM], F32, kind="ExternalInput").ap()
    xout = nc.dram_tensor("xout", [S, DM], F32, kind="ExternalOutput").ap()
    W = {
        "Wq": nc.dram_tensor("w_wq", [DM, DM], F32, kind="ExternalInput").ap(),
        "Wk": nc.dram_tensor("w_wk", [DM, DM], F32, kind="ExternalInput").ap(),
        "Wv": nc.dram_tensor("w_wv", [DM, DM], F32, kind="ExternalInput").ap(),
        "rel": nc.dram_tensor("w_rel", [2047 * 16], F32, kind="ExternalInput").ap(),
        "gamma": nc.dram_tensor("w_gamma", [DM], F32, kind="ExternalInput").ap(),
        "beta": nc.dram_tensor("w_beta", [DM], F32, kind="ExternalInput").ap(),
    }
    C = setup_common(P, need_rope=False, need_antid=True)
    emit_l1_attn(P, C, xin, xout, W)
    for n in [k for k in P.last if k.startswith("d_o")]:
        P.wait("sync", n, P.last[n])
    P.finish(nc)
    es.close()
    return nc


def l1_attn_inmap(inputs, xin, hc):
    return {
        "xin": np.ascontiguousarray(xin),
        "w_wq": inputs["L1_attn_Wq"], "w_wk": inputs["L1_attn_Wk"],
        "w_wv": inputs["L1_attn_Wv"],
        "w_rel": np.ascontiguousarray(inputs["L1_attn_rel_table"]).reshape(-1),
        "w_gamma": inputs["L1_attn_gamma"], "w_beta": inputs["L1_attn_beta"],
        "c_ident": hc["ident"], "c_ones": hc["ones_row_f32"], "c_antid": hc["antid"],
    }


DEBUG_X = False

def build_fused():
    nc = bass.Bass("TRN2", target_bir_lowering=False)
    es = ExitStack()
    P = Pr(nc, es)
    kx = "ExternalOutput" if DEBUG_X else "Internal"
    xin = nc.dram_tensor("xin", [S, DM], F32, kind="ExternalInput").ap()
    xout = nc.dram_tensor("xout", [S, DM], F32, kind="ExternalOutput").ap()
    x1 = nc.dram_tensor("x1s", [S, DM], F32, kind=kx).ap()
    x2 = nc.dram_tensor("x2s", [S, DM], F32, kind=kx).ap()
    x3 = nc.dram_tensor("x3s", [S, DM], F32, kind=kx).ap()
    W1 = {
        "Wqkv": nc.dram_tensor("w_qkv", [DM, 3 * DM], F32, kind="ExternalInput").ap(),
        "bqkv": nc.dram_tensor("w_bqkv", [3 * DM], F32, kind="ExternalInput").ap(),
        "Wo": nc.dram_tensor("w_wo", [DM, DM], F32, kind="ExternalInput").ap(),
        "bo": nc.dram_tensor("w_bo", [DM], F32, kind="ExternalInput").ap(),
        "gamma": nc.dram_tensor("w_gamma0", [DM], F32, kind="ExternalInput").ap(),
        "beta": nc.dram_tensor("w_beta0", [DM], F32, kind="ExternalInput").ap(),
    }
    W2 = {
        "g1xr": nc.dram_tensor("w_g1xr", [4, 512, 512], F32, kind="ExternalInput").ap(),
        "g2xr": nc.dram_tensor("w_g2xr", [4, 512, 512], F32, kind="ExternalInput").ap(),
        "g1xi": nc.dram_tensor("w_g1xi", [4, 512, 512], F32, kind="ExternalInput").ap(),
        "g2xi": nc.dram_tensor("w_g2xi", [4, 512, 512], F32, kind="ExternalInput").ap(),
        "bias_real": nc.dram_tensor("w_br", [512], F32, kind="ExternalInput").ap(),
        "bias_imag": nc.dram_tensor("w_bi", [512], F32, kind="ExternalInput").ap(),
    }
    W3 = {
        "Wq": nc.dram_tensor("w_wq1", [DM, DM], F32, kind="ExternalInput").ap(),
        "Wk": nc.dram_tensor("w_wk1", [DM, DM], F32, kind="ExternalInput").ap(),
        "Wv": nc.dram_tensor("w_wv1", [DM, DM], F32, kind="ExternalInput").ap(),
        "rel": nc.dram_tensor("w_rel", [2047 * 16], F32, kind="ExternalInput").ap(),
        "gamma": nc.dram_tensor("w_gamma1", [DM], F32, kind="ExternalInput").ap(),
        "beta": nc.dram_tensor("w_beta1", [DM], F32, kind="ExternalInput").ap(),
    }
    W4 = {
        "Wg": nc.dram_tensor("w_wg", [8, DM, DM], F32, kind="ExternalInput").ap(),
        "Wu": nc.dram_tensor("w_wu", [8, DM, DM], F32, kind="ExternalInput").ap(),
        "Wd": nc.dram_tensor("w_wd", [8, DM, DM], F32, kind="ExternalInput").ap(),
        "Wr": nc.dram_tensor("w_wr", [DM, 8], F32, kind="ExternalInput").ap(),
        "temp": nc.dram_tensor("w_temp", [1], F32, kind="ExternalInput").ap(),
    }
    C = setup_common(P, need_rope=True, need_antid=True)
    emit_l0_attn(P, C, xin, x1, W1)
    P.barrier()
    emit_l0_ffn(P, C, x1, x2, W2)
    P.barrier()
    emit_l1_attn(P, C, x2, x3, W3)
    P.barrier()
    emit_l1_ffn(P, C, x3, xout, W4)
    for n in [k for k in P.last if k.startswith("d_o")]:
        P.wait("sync", n, P.last[n])
    P.finish(nc)
    es.close()
    return nc


def fused_inmap(inputs, b, hc):
    v = l0_ffn_views(inputs)
    return {
        "xin": np.ascontiguousarray(inputs["L0_attn_x"][b]),
        "w_qkv": inputs["L0_attn_Wqkv"], "w_bqkv": inputs["L0_attn_bqkv"],
        "w_wo": inputs["L0_attn_Wo"], "w_bo": inputs["L0_attn_bo"],
        "w_gamma0": inputs["L0_attn_gamma"], "w_beta0": inputs["L0_attn_beta"],
        "w_g1xr": v["g1xr"], "w_g2xr": v["g2xr"],
        "w_g1xi": v["g1xi"], "w_g2xi": v["g2xi"],
        "w_br": inputs["L0_ffn_bias_real"], "w_bi": inputs["L0_ffn_bias_imag"],
        "w_wq1": inputs["L1_attn_Wq"], "w_wk1": inputs["L1_attn_Wk"],
        "w_wv1": inputs["L1_attn_Wv"],
        "w_rel": np.ascontiguousarray(inputs["L1_attn_rel_table"]).reshape(-1),
        "w_gamma1": inputs["L1_attn_gamma"], "w_beta1": inputs["L1_attn_beta"],
        "w_wg": inputs["L1_ffn_Wg"], "w_wu": inputs["L1_ffn_Wu"],
        "w_wd": inputs["L1_ffn_Wd"], "w_wr": inputs["L1_ffn_Wr"],
        "w_temp": inputs["L1_ffn_temperature"],
        "c_ident": hc["ident"], "c_ones": hc["ones_row_f32"],
        "c_cosrep": hc["cosrep"], "c_sinrep": hc["sinrep"], "c_antid": hc["antid"],
    }
